# version 03
# baseline (speedup 1.0000x reference)
import contextlib
import numpy as np
import concourse.bass as bass
import concourse.mybir as mybir
from concourse.bass_utils import run_bass_kernel_spmd

F32 = mybir.dt.float32
BF16 = mybir.dt.bfloat16
F8E5 = mybir.dt.float8e5
AF = mybir.ActivationFunctionType
ALU = mybir.AluOpType
AX = mybir.AxisListType

MAGIC = 1.5 * 2 ** 23
NCORES = 8
TOK = 2048
NT = TOK // 128
D = 1024
ND = D // 128

ENGINES = ("sync", "vector", "scalar", "tensor", "gpsimd")


class PB:

    def __init__(self, nc):
        self.nc = nc
        self.es = contextlib.ExitStack()
        self.ops = {e: [] for e in ENGINES}
        self.cnt = {}
        self._names = {}
        self.const_need = set()
        self._lastop = {e: None for e in ENGINES}

    def cf(self, v):
        v = float(v)
        if v not in (0.0, 1.0, -1.0):
            self.const_need.add(v)
        return v

    def sem(self, name):
        h = self.es.enter_context(self.nc.semaphore(name))
        self.cnt[h] = 0
        self._names[h] = name
        return h

    def sbuf(self, name, shape, dtype):
        return self.es.enter_context(self.nc.sbuf_tensor(name, shape, dtype))

    def psum(self, name, shape, dtype=F32):
        return self.es.enter_context(self.nc.psum_tensor(name, shape, dtype))

    def emit(self, eng, fn):
        self.ops[eng].append(fn)

    def op(self, eng, method, *args, sem=None, inc=1, **kw):
        if eng in ("vector", "scalar", "gpsimd") and self._lastop[eng] == "compute" \
                and method != "dma_start":
            self.emit(eng, lambda e: e.drain())
        if method != "dma_start":
            self._lastop[eng] = "compute"
        if sem is not None:
            self.cnt[sem] += inc

        def fn(e, method=method, args=args, kw=kw, sem=sem, inc=inc):
            try:
                ins = getattr(e, method)(*args, **kw)
            except Exception as ex:
                raise RuntimeError(f"emit failed: {method} kw={list(kw)}") from ex
            if sem is not None:
                ins.then_inc(sem, inc)
        self.emit(eng, fn)
        return self.cnt[sem] if sem is not None else None

    def dma(self, dst, src, sem, eng="sync"):
        self.wait(eng, sem, self.cnt[sem])
        return self.op(eng, "dma_start", out=dst, in_=src, sem=sem, inc=16)

    def draininc(self, eng, sem, inc=1):
        self.cnt[sem] += inc

        def fn(e, sem=sem, inc=inc):
            e.drain().then_inc(sem, inc)
        self.emit(eng, fn)
        return self.cnt[sem]

    def wait(self, eng, sem, val=None):
        v = self.cnt[sem] if val is None else val
        self.emit(eng, lambda e, sem=sem, v=v: e.wait_ge(sem, v))

    def build(self):
        nc = self.nc
        if self.const_need:
            vals = sorted(self.const_need)
            pool = nc.alloc_sbuf_tensor("constpool_sb", [128, len(vals)], F32, side="right")
            csem = self.es.enter_context(nc.semaphore("constpool_sem"))
            for i, v in enumerate(vals):
                nc.const_aps.register(F32, float(v), pool.ap()[:, i:i + 1])

            def fills(e, vals=vals, pool=pool, csem=csem):
                for i, v in enumerate(vals):
                    ins = e.memset(pool.ap()[:, i:i + 1], float(v))
                ins.then_inc(csem, 1)
            self.ops["vector"].insert(0, fills)
            n = 1
            for eng in ENGINES:
                if eng == "vector" or not self.ops[eng]:
                    continue
                self.ops[eng].insert(0, lambda e, csem=csem, n=n: e.wait_ge(csem, n))
        with self.es:
            with nc.Block() as block:
                for eng in ENGINES:
                    lst = self.ops[eng]
                    if not lst:
                        continue
                    dec = getattr(block, eng)

                    def body(e, lst=lst):
                        for fn in lst:
                            fn(e)
                    dec(body)
        return nc



def emit_ln_stats(pb, xres, stats, rstd, nbias, sems, eps=1e-5, ntiles=NT, width=D,
                  dummy=None):
    sV, sS = sems["sV"], sems["sS"]
    nt = ntiles
    for i in range(ntiles):
        pb.op("vector", "reduce_sum", out=stats[:, i:i + 1],
              in_=xres[:, width * i:width * (i + 1)], axis=AX.X,
              sem=sV if i == ntiles - 1 else None)
    for i in range(ntiles):
        pb.op("scalar", "activation", out=dummy[:, 0:width],
              in_=xres[:, width * i:width * (i + 1)], func=AF.Square,
              accum_out=stats[:, nt + i:nt + i + 1],
              sem=sS if i == ntiles - 1 else None)
    pb.wait("vector", sS)
    pb.op("vector", "tensor_scalar", out=stats[:, 2 * nt:3 * nt],
          in0=stats[:, 0:nt], scalar1=1.0 / width, scalar2=None, op0=ALU.mult)
    pb.op("vector", "tensor_tensor", out=stats[:, 3 * nt:4 * nt],
          in0=stats[:, 2 * nt:3 * nt], in1=stats[:, 2 * nt:3 * nt], op=ALU.mult)
    pb.op("vector", "scalar_tensor_tensor", out=stats[:, 3 * nt:4 * nt],
          in0=stats[:, nt:2 * nt], scalar=1.0 / width,
          in1=stats[:, 3 * nt:4 * nt], op0=ALU.mult, op1=ALU.subtract, sem=sV)
    pb.wait("scalar", sV)
    pb.op("scalar", "activation", out=rstd[:, 0:nt], in_=stats[:, 3 * nt:4 * nt],
          func=AF.Ln, bias=pb.cf(eps))
    pb.op("scalar", "activation", out=rstd[:, 0:nt], in_=rstd[:, 0:nt],
          func=AF.Exp, scale=pb.cf(-0.5), sem=sS)
    pb.wait("vector", sS)
    pb.op("vector", "scalar_tensor_tensor", out=nbias[:, 0:nt],
          in0=stats[:, 2 * nt:3 * nt], scalar=-1.0,
          in1=rstd[:, 0:nt], op0=ALU.mult, op1=ALU.mult, sem=sV)
    pb.wait("scalar", sV)


def emit_ln_apply(pb, xres_slice, out_ap, rstd_col, nbias_col, sem=None):
    pb.op("vector", "tensor_scalar", out=out_ap, in0=xres_slice,
          scalar1=rstd_col, scalar2=nbias_col, op0=ALU.mult, op1=ALU.add,
          sem=sem)


def emit_mm(pb, psum_ap, pairs, sem=None, inc=1):
    nk = len(pairs)
    for k, (lt, rh) in enumerate(pairs):
        pb.op("tensor", "matmul", psum_ap, lt, rh,
              start=(k == 0), stop=(k == nk - 1),
              sem=sem if k == nk - 1 else None, inc=inc)


def emit_load_cast(pb, dram_ap_rows, stage, dst_cols, dma_sem, sems):
    sV = sems["sV"]
    for j in range(len(dram_ap_rows)):
        shp = dram_ap_rows[j].shape
        W = 1
        for s in shp[1:]:
            W *= int(s)
        st_view = stage[:, 0:W]
        if len(shp) == 3:
            st_view = st_view.rearrange("p (a d) -> p a d", d=int(shp[-1]))
        pb.wait("sync", sV, pb.cnt[sV])
        pb.op("sync", "dma_start", out=st_view, in_=dram_ap_rows[j],
              sem=dma_sem, inc=16)
        pb.wait("vector", dma_sem)
        pb.op("vector", "tensor_copy", out=dst_cols[j], in_=st_view, sem=sV)
    pb.wait("sync", sV, pb.cnt[sV])



def emit_ffn0(pb, xres, weights, scratch, sems):
    sV, sS, sT = sems["sV"], sems["sS"], sems["sT"]
    xqT, hq = scratch["xqT_f8"], scratch["hq_f8"]
    ps_tr, ps_h, ps_y = scratch["ps_tr"], scratch["ps_h"], scratch["ps_y"]
    ident = scratch["ident"]
    w1T, w2T = weights["w1T_f8"], weights["w2T_f8"]
    b1q, b2bc, maskbf = weights["b1q"], weights["b2bc"], weights["maskbf"]

    emit_ln_stats(pb, xres, scratch["stats"], scratch["rstd"], scratch["nbias"], sems, dummy=scratch["xlnb"])

    xqT3 = xqT.rearrange("p (j t) -> p j t", t=TOK)
    xlnb = scratch["xlnb"]
    y2d = xlnb[:, 0:D]
    for i in range(NT):
        half = i % 2
        if i >= 2:
            pb.wait("vector", sT)
        emit_ln_apply(pb, xres[:, D * i:D * (i + 1)],
                      xlnb[:, D * half:D * (half + 1)],
                      scratch["rstd"][:, i:i + 1], scratch["nbias"][:, i:i + 1],
                      sem=sV)
        pb.wait("tensor", sV)
        for g in range(2):
            pb.wait("tensor", sV)
            for jj in range(4):
                j = 4 * g + jj
                pb.op("tensor", "transpose", ps_tr[:, 128 * jj:128 * (jj + 1)],
                      xlnb[:, D * half + 128 * j:D * half + 128 * (j + 1)],
                      ident[:], sem=sT if jj == 3 else None)
            pb.wait("vector", sT)
            pb.op("vector", "tensor_copy",
                  out=xqT3[:, 4 * g:4 * g + 4, 128 * i:128 * (i + 1)],
                  in_=ps_tr[:].rearrange("p (j t) -> p j t", t=128), sem=sV)
    pb.wait("tensor", sV)

    for tc in range(4):
        t0 = 512 * tc
        for m in range(32):
            pairs = [(w1T[:, 4096 * j + 128 * m:4096 * j + 128 * (m + 1)],
                      xqT[:, TOK * j + t0:TOK * j + t0 + 512])
                     for j in range(ND)]
            emit_mm(pb, ps_h[:], pairs, sem=sT)
            pb.wait("vector", sT)
            pb.op("vector", "tensor_scalar", out=scratch["hbf"][:, 0:512],
                  in0=ps_h[:], scalar1=b1q[:, m:m + 1], scalar2=0.0,
                  op0=ALU.add, op1=ALU.max, sem=sV)
            pb.op("vector", "tensor_copy", out=hq[:, 512 * m:512 * (m + 1)],
                  in_=scratch["hbf"][:, 0:512], sem=sV)
            pb.wait("tensor", sV, pb.cnt[sV] - 1)
        pb.wait("tensor", sV)
        for sub in range(4):
            for dh in range(2):
                pairs = [(hq[:, 512 * m + 128 * sub:512 * m + 128 * (sub + 1)],
                          w2T[:, 1024 * m + 512 * dh:1024 * m + 512 * (dh + 1)])
                         for m in range(32)]
                emit_mm(pb, ps_y[:, 512 * dh:512 * (dh + 1)], pairs, sem=sT)
            pb.wait("vector", sT)
            i = 4 * tc + sub
            pb.op("vector", "tensor_tensor", out=scratch["y2bf"][:],
                  in0=ps_y[:], in1=b2bc[:], op=ALU.add, sem=sV)
            pb.op("vector", "tensor_tensor", out=y2d,
                  in0=scratch["y2bf"][:], in1=maskbf[:, D * i:D * (i + 1)],
                  op=ALU.mult, sem=sV)
            pb.op("vector", "tensor_tensor", out=xres[:, D * i:D * (i + 1)],
                  in0=xres[:, D * i:D * (i + 1)], in1=y2d,
                  op=ALU.add, sem=sV)
            pb.wait("tensor", sV, pb.cnt[sV] - 2)


def ffn0_weight_prep(pb, dram, weights, stage, ps_b2, dma_sem, sems):
    sV, sT = sems["sV"], sems["sT"]
    w1rows = [dram["w1T"][128 * j:128 * (j + 1), 2048 * c:2048 * (c + 1)]
              for j in range(ND) for c in range(2)]
    w1dst = [weights["w1T_f8"][:, 4096 * j + 2048 * c:4096 * j + 2048 * (c + 1)]
             for j in range(ND) for c in range(2)]
    emit_load_cast(pb, w1rows, stage, w1dst, dma_sem, sems)
    w2rows = [dram["w2T"][256 * m:256 * (m + 1), :].rearrange("(a p) d -> p a d", p=128)
              for m in range(16)]
    w2dst = [weights["w2T_f8"][:, 2048 * m:2048 * (m + 1)].rearrange("p (a d) -> p a d", d=D)
             for m in range(16)]
    emit_load_cast(pb, w2rows, stage, w2dst, dma_sem, sems)
    pb.op("sync", "dma_start", out=weights["b1_raw"][:], in_=dram["b1c"][:],
          sem=dma_sem, inc=16)
    pb.op("sync", "dma_start", out=weights["b2_raw"][0:1, :], in_=dram["b2r"][:],
          sem=dma_sem, inc=16)
    pb.op("sync", "dma_start", out=weights["onesbf"][0:1, :], in_=dram["ones1"][:],
          sem=dma_sem, inc=16)
    pb.wait("vector", dma_sem)
    pb.op("vector", "tensor_copy", out=weights["b1_f8"][:], in_=weights["b1_raw"][:])
    pb.op("vector", "tensor_copy", out=weights["b1q"][:], in_=weights["b1_f8"][:])
    pb.op("vector", "tensor_copy", out=weights["b2_f8"][0:1, :],
          in_=weights["b2_raw"][0:1, :])
    pb.op("vector", "tensor_copy", out=weights["b2q_bf"][0:1, :],
          in_=weights["b2_f8"][0:1, :], sem=sV)
    pb.wait("tensor", sV)
    for dh in range(2):
        pb.op("tensor", "matmul", ps_b2[:, 512 * dh:512 * (dh + 1)],
              weights["onesbf"][0:1, :],
              weights["b2q_bf"][0:1, 512 * dh:512 * (dh + 1)],
              start=True, stop=True, sem=sT if dh == 1 else None)
    pb.wait("vector", sT)
    pb.op("vector", "tensor_copy", out=weights["b2bc"][:], in_=ps_b2[:], sem=sV)


def alloc_ffn0_weights(pb, tag=""):
    return {
        "w1T_f8": pb.sbuf("f0w1T" + tag, [128, ND * 4096], F8E5),
        "w2T_f8": pb.sbuf("f0w2T" + tag, [128, 32 * 1024], F8E5),
        "b1_raw": pb.sbuf("f0b1r" + tag, [128, 32], F32),
        "b1_f8": pb.sbuf("f0b18" + tag, [128, 32], F8E5),
        "b1q": pb.sbuf("f0b1q" + tag, [128, 32], F32),
        "b2_raw": pb.sbuf("f0b2r" + tag, [1, D], F32),
        "b2_f8": pb.sbuf("f0b28" + tag, [1, D], F8E5),
        "b2q_bf": pb.sbuf("f0b2b" + tag, [1, D], BF16),
        "onesbf": pb.sbuf("f0ones" + tag, [1, 128], BF16),
        "b2bc": pb.sbuf("f0b2bc" + tag, [128, D], F32),
    }


def alloc_ffn0_scratch(pb, tag=""):
    return {
        "xlnb": pb.sbuf("f0xlnb" + tag, [128, 2 * D], F32),
        "xqT_f8": pb.sbuf("f0xqT" + tag, [128, ND * TOK], F8E5),
        "hq_f8": pb.sbuf("f0hq" + tag, [128, 32 * 512], F8E5),
        "stats": pb.sbuf("f0stats" + tag, [128, 4 * NT], F32),
        "rstd": pb.sbuf("f0rstd" + tag, [128, NT], F32),
        "nbias": pb.sbuf("f0nbias" + tag, [128, NT], F32),
        "hbf": pb.sbuf("f0hbf" + tag, [128, 512], BF16),
        "y2bf": pb.sbuf("f0y2bf" + tag, [128, D], BF16),
    }



def emit_transpose_batch(pb, srcs, dsts, ident, ps_tr, sems, width=128):
    sV, sT = sems["sV"], sems["sT"]
    n = len(srcs)
    for g in range(0, n, 4):
        gn = min(4, n - g)
        pb.wait("tensor", sV)
        for jj in range(gn):
            pb.op("tensor", "transpose", ps_tr[:, 128 * jj:128 * (jj + 1)],
                  srcs[g + jj], ident[:], sem=sT if jj == gn - 1 else None)
        pb.wait("vector", sT)
        for jj in range(gn):
            pb.op("vector", "tensor_copy", out=dsts[g + jj],
                  in_=ps_tr[:, 128 * jj:128 * (jj + 1)],
                  sem=sV if jj == gn - 1 else None)



def emit_attnb0(pb, xres, weights, scratch, sems):
    sV, sS, sT = sems["sV"], sems["sS"], sems["sT"]
    ident, identb = scratch["ident"], scratch["identb"]
    ps_tr, ps_trb = scratch["ps_tr"], scratch["ps_trb"]
    ps_q, ps_att = scratch["ps_q"], scratch["ps_att"]
    ps_yg, ps_p = scratch["ps_yg"], scratch["ps_p"]
    wqkv, wproj = weights["wqkv_bf"], weights["wproj_bf"]
    lnwbc, masks = weights["lnwbc"], weights["maskneg"]
    xlnb, xlnT = scratch["xlnb"], scratch["xlnT"]
    qf, kf, vbf = scratch["qf"], scratch["kf"], scratch["vbf"]
    qnT, knT = scratch["qnT"], scratch["knT"]
    ybf, yT = scratch["ybf"], scratch["yT"]
    cols = scratch["cols"]
    attm, attp = scratch["attm"], scratch["attp"]
    attpT = scratch["attpT"]

    emit_ln_stats(pb, xres, scratch["stats"], scratch["rstd"], scratch["nbias"],
                  sems, dummy=xlnb)

    for g in range(8):
        for s in range(2):
            i = 2 * g + s
            emit_ln_apply(pb, xres[:, D * i:D * (i + 1)],
                          xlnb[:, D * s:D * (s + 1)],
                          scratch["rstd"][:, i:i + 1], scratch["nbias"][:, i:i + 1],
                          sem=sV)
            pb.wait("tensor", sV)
            srcs = [xlnb[:, D * s + 128 * j:D * s + 128 * (j + 1)] for j in range(8)]
            dsts = [xlnT[:, 256 * j + 128 * s:256 * j + 128 * (s + 1)] for j in range(8)]
            emit_transpose_batch(pb, srcs, dsts, ident, ps_tr, sems)
        pb.wait("tensor", sV)
        for s in range(2):
            for nck in range(6):
                pairs = [(xlnT[:, 256 * j + 128 * s:256 * j + 128 * (s + 1)],
                          wqkv[:, 3072 * j + 512 * nck:3072 * j + 512 * (nck + 1)])
                         for j in range(8)]
                emit_mm(pb, ps_q[:], pairs, sem=sT)
                pb.wait("vector", sT)
                off = 512 * (nck % 2)
                if nck < 2:
                    pb.op("vector", "tensor_copy",
                          out=qf[:, D * s + off:D * s + off + 512], in_=ps_q[:], sem=sV)
                elif nck < 4:
                    pb.op("vector", "tensor_copy",
                          out=kf[:, D * s + off:D * s + off + 512], in_=ps_q[:], sem=sV)
                else:
                    pb.op("vector", "tensor_copy",
                          out=vbf[:, D * s + off:D * s + off + 512], in_=ps_q[:], sem=sV)
                pb.wait("tensor", sV)
        for s in range(2):
            for (src, dstT, ci) in ((qf, qnT, 0), (kf, knT, 1)):
                pb.wait("scalar", sV)
                pb.op("scalar", "activation", out=xlnb[:, 0:D],
                      in_=src[:, D * s:D * (s + 1)], func=AF.Square,
                      accum_out=cols[:, ci:ci + 1], sem=sS)
                pb.wait("scalar", sS)
                pb.op("scalar", "activation", out=cols[:, 2 + ci:3 + ci],
                      in_=cols[:, ci:ci + 1], func=AF.Ln,
                      scale=pb.cf(1.0 / D), bias=pb.cf(1e-6))
                pb.op("scalar", "activation", out=cols[:, 2 + ci:3 + ci],
                      in_=cols[:, 2 + ci:3 + ci], func=AF.Exp, scale=pb.cf(-0.5),
                      sem=sS)
                pb.wait("vector", sS)
                pb.op("vector", "scalar_tensor_tensor", out=scratch["qnb"][:, 0:D],
                      in0=src[:, D * s:D * (s + 1)], scalar=cols[:, 2 + ci:3 + ci],
                      in1=lnwbc[:], op0=ALU.mult, op1=ALU.mult, sem=sV)
                pb.wait("tensor", sV)
                srcs = [scratch["qnb"][:, 128 * j:128 * (j + 1)] for j in range(8)]
                dsts = [dstT[:, 256 * j + 128 * s:256 * j + 128 * (s + 1)]
                        for j in range(8)]
                emit_transpose_batch(pb, srcs, dsts, identb, ps_trb, sems)
        pb.wait("tensor", sV)
        for s in range(2):
            for h in range(16):
                j, rb = h // 2, 64 * (h % 2)
                pb.op("tensor", "matmul", ps_att[:],
                      qnT[rb:rb + 64, 256 * j + 128 * s:256 * j + 128 * (s + 1)],
                      knT[rb:rb + 64, 256 * j:256 * (j + 1)],
                      start=True, stop=True, sem=sT)
                pb.wait("vector", sT)
                pb.op("vector", "scalar_tensor_tensor", out=attm[:],
                      in0=ps_att[:], scalar=0.125, in1=masks[:, 256 * s:256 * (s + 1)],
                      op0=ALU.mult, op1=ALU.add, sem=sV)
                pb.op("vector", "reduce_max", out=cols[:, 4:5], in_=attm[:],
                      axis=AX.X, negate=True, sem=sV)
                pb.wait("scalar", sV)
                pb.op("scalar", "activation", out=attm[:], in_=attm[:], func=AF.Exp,
                      bias=cols[:, 4:5], accum_out=cols[:, 5:6], sem=sS)
                pb.wait("vector", sS)
                pb.op("vector", "reciprocal", out=cols[:, 6:7], in_=cols[:, 5:6],
                      sem=sV)
                pb.op("vector", "tensor_scalar", out=attp[:], in0=attm[:],
                      scalar1=cols[:, 6:7], scalar2=None, op0=ALU.mult, sem=sV)
                pb.wait("tensor", sV)
                srcs = [attp[:, 0:128], attp[:, 128:256]]
                dsts = [attpT[:, 0:128], attpT[:, 128:256]]
                emit_transpose_batch(pb, srcs, dsts, identb, ps_trb, sems)
                pb.wait("tensor", sV)
                hh = h % 8
                pairs = [(attpT[:, 0:128], vbf[:, 0 * D + 64 * h:0 * D + 64 * h + 64]),
                         (attpT[:, 128:256], vbf[:, 1 * D + 64 * h:1 * D + 64 * h + 64])]
                emit_mm(pb, ps_yg[:, 64 * hh:64 * (hh + 1)], pairs,
                        sem=sT if hh == 7 else None)
                if hh == 7:
                    pb.wait("vector", sT)
                    half = h // 8
                    pb.op("vector", "tensor_copy",
                          out=ybf[:, D * s + 512 * half:D * s + 512 * (half + 1)],
                          in_=ps_yg[:], sem=sV)
                    pb.wait("tensor", sV)
        for s in range(2):
            i = 2 * g + s
            srcs = [ybf[:, D * s + 128 * j:D * s + 128 * (j + 1)] for j in range(8)]
            dsts = [yT[:, 256 * j + 128 * s:256 * j + 128 * (s + 1)] for j in range(8)]
            emit_transpose_batch(pb, srcs, dsts, identb, ps_trb, sems)
            pb.wait("tensor", sV)
            for dc in range(2):
                pairs = [(yT[:, 256 * j + 128 * s:256 * j + 128 * (s + 1)],
                          wproj[:, 1024 * j + 512 * dc:1024 * j + 512 * (dc + 1)])
                         for j in range(8)]
                emit_mm(pb, ps_p[:], pairs, sem=sT)
                pb.wait("vector", sT)
                pb.op("vector", "tensor_tensor",
                      out=xres[:, D * i + 512 * dc:D * i + 512 * (dc + 1)],
                      in0=xres[:, D * i + 512 * dc:D * i + 512 * (dc + 1)],
                      in1=ps_p[:], op=ALU.add, sem=sV)
                pb.wait("tensor", sV)


def attnb0_weight_prep(pb, dram, weights, stage, ps_b2, dma_sem, sems):
    sV, sT = sems["sV"], sems["sT"]
    rows = [dram["wqkv"][128 * j:128 * (j + 1), 1536 * c:1536 * (c + 1)]
            for j in range(8) for c in range(2)]
    dsts = [weights["wqkv_bf"][:, 3072 * j + 1536 * c:3072 * j + 1536 * (c + 1)]
            for j in range(8) for c in range(2)]
    emit_load_cast(pb, rows, stage, dsts, dma_sem, sems)
    rows = [dram["wproj"][128 * j:128 * (j + 1), :] for j in range(8)]
    dsts = [weights["wproj_bf"][:, 1024 * j:1024 * (j + 1)] for j in range(8)]
    emit_load_cast(pb, rows, stage, dsts, dma_sem, sems)
    pb.op("sync", "dma_start", out=weights["lnw_raw"][0:1, :], in_=dram["lnw"][:],
          sem=dma_sem, inc=16)
    pb.op("sync", "dma_start", out=weights["ones32"][0:1, :], in_=dram["ones32"][:],
          sem=dma_sem, inc=16)
    pb.op("sync", "dma_start", out=weights["maskneg"][:], in_=dram["maskneg"][:],
          sem=dma_sem, inc=16)
    pb.wait("tensor", dma_sem)
    for dh in range(2):
        pb.op("tensor", "matmul", ps_b2[:, 512 * dh:512 * (dh + 1)],
              weights["ones32"][0:1, :],
              weights["lnw_raw"][0:1, 512 * dh:512 * (dh + 1)],
              start=True, stop=True, sem=sT if dh == 1 else None)
    pb.wait("vector", sT)
    pb.op("vector", "tensor_copy", out=weights["lnwbc"][:], in_=ps_b2[:], sem=sV)


def alloc_attnb0(pb, tag=""):
    weights = {
        "wqkv_bf": pb.sbuf("abwqkv" + tag, [128, 8 * 3072], BF16),
        "wproj_bf": pb.sbuf("abwproj" + tag, [128, 8 * 1024], BF16),
        "lnw_raw": pb.sbuf("ablnw" + tag, [1, D], F32),
        "ones32": pb.sbuf("abones" + tag, [1, 128], F32),
        "lnwbc": pb.sbuf("ablnwbc" + tag, [128, D], F32),
        "maskneg": pb.sbuf("abmask" + tag, [128, 512], F32),
    }
    scratch = {
        "xlnb": pb.sbuf("abxlnb" + tag, [128, 2 * D], F32),
        "xlnT": pb.sbuf("abxlnT" + tag, [128, 8 * 256], BF16),
        "qf": pb.sbuf("abqf" + tag, [128, 2 * D], F32),
        "kf": pb.sbuf("abkf" + tag, [128, 2 * D], F32),
        "vbf": pb.sbuf("abvbf" + tag, [128, 2 * D], BF16),
        "qnb": pb.sbuf("abqnb" + tag, [128, D], BF16),
        "qnT": pb.sbuf("abqnT" + tag, [128, 8 * 256], BF16),
        "knT": pb.sbuf("abknT" + tag, [128, 8 * 256], BF16),
        "ybf": pb.sbuf("abybf" + tag, [128, 2 * D], BF16),
        "yT": pb.sbuf("abyT" + tag, [128, 8 * 256], BF16),
        "attm": pb.sbuf("abattm" + tag, [128, 256], F32),
        "attp": pb.sbuf("abattp" + tag, [128, 256], BF16),
        "attpT": pb.sbuf("abattpT" + tag, [128, 256], BF16),
        "cols": pb.sbuf("abcols" + tag, [128, 8], F32),
        "stats": pb.sbuf("abstats" + tag, [128, 4 * NT], F32),
        "rstd": pb.sbuf("abrstd" + tag, [128, NT], F32),
        "nbias": pb.sbuf("abnbias" + tag, [128, NT], F32),
    }
    return weights, scratch


def build_attnb0_program():
    nc = bass.Bass("TRN2", target_bir_lowering=False, debug=False)
    pb = PB(nc)
    dram = {
        "xres": nc.dram_tensor("xres", [TOK, D], F32, kind="ExternalInput").ap(),
        "wqkv": nc.dram_tensor("wqkv", [D, 3072], F32, kind="ExternalInput").ap(),
        "wproj": nc.dram_tensor("wproj", [D, D], F32, kind="ExternalInput").ap(),
        "lnw": nc.dram_tensor("lnw", [1, D], F32, kind="ExternalInput").ap(),
        "ones32": nc.dram_tensor("ones32", [1, 128], F32, kind="ExternalInput").ap(),
        "maskneg": nc.dram_tensor("maskneg", [128, 512], F32, kind="ExternalInput").ap(),
        "ident": nc.dram_tensor("ident", [128, 128], F32, kind="ExternalInput").ap(),
        "identb": nc.dram_tensor("identb", [128, 128], BF16, kind="ExternalInput").ap(),
        "xout": nc.dram_tensor("xout", [TOK, D], F32, kind="ExternalOutput").ap(),
    }
    weights, scratch = alloc_attnb0(pb)
    scratch["ident"] = pb.sbuf("identsb", [128, 128], F32)
    scratch["identb"] = pb.sbuf("identbsb", [128, 128], BF16)
    scratch["ps_tr"] = pb.psum("ps_tr", [128, 512], F32)
    scratch["ps_trb"] = pb.psum("ps_trb", [128, 512], BF16)
    scratch["ps_q"] = pb.psum("ps_q", [128, 512], F32)
    scratch["ps_att"] = pb.psum("ps_att", [128, 256], F32)
    scratch["ps_yg"] = pb.psum("ps_yg", [128, 512], F32)
    scratch["ps_p"] = pb.psum("ps_p", [128, 512], F32)
    ps_b2 = pb.psum("ps_b2", [128, 1024], F32)
    xres = pb.sbuf("xres_sb", [128, NT * D], F32)
    stage = pb.sbuf("stage", [128, 2048], F32)
    dma_sem = pb.sem("dma")
    sems = {"sV": pb.sem("sV"), "sS": pb.sem("sS"), "sT": pb.sem("sT"),
            "sG": pb.sem("sG")}
    st = pb.sem("st")

    xr3 = dram["xres"].rearrange("(i p) d -> p i d", p=128)
    pb.op("sync", "dma_start", out=xres.rearrange("p (i d) -> p i d", d=D),
          in_=xr3, sem=dma_sem, inc=16)
    pb.op("sync", "dma_start", out=scratch["ident"][:], in_=dram["ident"][:],
          sem=dma_sem, inc=16)
    pb.op("sync", "dma_start", out=scratch["identb"][:], in_=dram["identb"][:],
          sem=dma_sem, inc=16)
    attnb0_weight_prep(pb, dram, weights, stage, ps_b2, dma_sem, sems)
    pb.wait("vector", dma_sem)
    pb.wait("scalar", dma_sem)
    pb.wait("tensor", dma_sem)

    emit_attnb0(pb, xres, weights, scratch, sems)

    pb.wait("sync", sems["sV"])
    pb.op("sync", "dma_start",
          out=dram["xout"].rearrange("(i p) d -> p i d", p=128),
          in_=xres.rearrange("p (i d) -> p i d", d=D), sem=st, inc=16)
    pb.wait("sync", st)
    return pb.build()


def host_attnb0_masks():
    m = np.zeros((128, 512), np.float32)
    for s in range(2):
        for r in range(128):
            for c in range(256):
                t_q = 128 * s + r
                if c > t_q:
                    m[r, 256 * s + c] = -1e30
    return m




def emit_ffn1_weightprep(pb, dram, fw, scratch, sems, dma_sem):
    sV, sS, sT = sems["sV"], sems["sS"], sems["sT"]
    stage, stageb = scratch["stage"], scratch["hsb"]
    cols = fw["wcols"]
    ps1 = scratch["ps_one"]
    pb.wait("sync", dma_sem, pb.cnt[dma_sem])
    for wi, (wsrc, wdst, nrow, ncol) in enumerate(
            ((dram["f1w1T"], dram["w1q_dram"], 8, 4096),
             (dram["f1w2T"], dram["w2q_dram"], 32, 1024))):
        W = min(ncol, 2048)
        nck = max(1, ncol // W)
        chunks = [(r, c) for r in range(nrow) for c in range(nck)]
        pb.op("vector", "memset", fw["acc"][:, 0:1], 0.0, sem=sV)
        for (r, c) in chunks:
            pb.wait("sync", sS, pb.cnt[sS])
            pb.op("sync", "dma_start", out=stage[:, 0:W],
                  in_=wsrc[128 * r:128 * (r + 1), W * c:W * (c + 1)],
                  sem=dma_sem, inc=16)
            pb.wait("scalar", dma_sem)
            pb.wait("scalar", sV)
            pb.op("scalar", "activation", out=stageb[:, 0:W], in_=stage[:, 0:W],
                  func=AF.Abs, accum_out=cols[:, 0:1], sem=sS)
            pb.wait("vector", sS)
            pb.op("vector", "tensor_tensor", out=fw["acc"][:, 0:1],
                  in0=fw["acc"][:, 0:1], in1=cols[:, 0:1], op=ALU.add, sem=sV)
        pb.wait("tensor", sV)
        pb.op("tensor", "matmul", ps1[0:1, 0:1], fw["acc"][:, 0:1],
              fw["ones_col"][:, 0:1], start=True, stop=True, sem=sT)
        pb.wait("vector", sT)
        pb.op("vector", "tensor_scalar", out=cols[0:1, 1:2], in0=ps1[0:1, 0:1],
              scalar1=1.0 / (nrow * 128 * ncol), scalar2=1e-5,
              op0=ALU.mult, op1=ALU.max)
        pb.op("vector", "reciprocal", out=cols[0:1, 2:3], in_=cols[0:1, 1:2],
              sem=sV)
        pb.wait("tensor", sV)
        pb.op("tensor", "matmul", ps1[:, 2:4], fw["ones_row"][0:1, :],
              cols[0:1, 1:3], start=True, stop=True, sem=sT)
        pb.wait("vector", sT)
        pb.op("vector", "tensor_copy", out=fw["dq_col"][:, wi:wi + 1],
              in_=ps1[:, 2:3])
        pb.op("vector", "tensor_copy", out=fw["sw_col"][:, wi:wi + 1],
              in_=ps1[:, 3:4], sem=sV)
        for (r, c) in chunks:
            pb.wait("sync", sV, pb.cnt[sV])
            pb.op("sync", "dma_start", out=stage[:, 0:W],
                  in_=wsrc[128 * r:128 * (r + 1), W * c:W * (c + 1)],
                  sem=dma_sem, inc=16)
            pb.wait("vector", dma_sem)
            pb.op("vector", "tensor_scalar", out=stage[:, 0:W], in0=stage[:, 0:W],
                  scalar1=fw["sw_col"][:, wi:wi + 1], scalar2=MAGIC,
                  op0=ALU.mult, op1=ALU.add)
            pb.op("vector", "tensor_scalar", out=stage[:, 0:W], in0=stage[:, 0:W],
                  scalar1=-MAGIC, scalar2=None, op0=ALU.add)
            pb.op("vector", "tensor_scalar", out=stageb[:, 0:W], in0=stage[:, 0:W],
                  scalar1=-1.0, scalar2=1.0, op0=ALU.max, op1=ALU.min, sem=sV)
            pb.wait("sync", sV)
            pb.op("sync", "dma_start",
                  out=wdst[128 * r:128 * (r + 1), W * c:W * (c + 1)],
                  in_=stageb[:, 0:W], sem=dma_sem, inc=16)
        pb.wait("sync", dma_sem)


def emit_actquant_token(pb, xv, cols, rows, sems):
    sV = sems["sV"]
    pb.op("vector", "tensor_reduce", out=cols[0:rows, 0:1], in_=xv,
          axis=AX.X, op=ALU.max, apply_absolute_value=True)
    pb.op("vector", "tensor_scalar", out=cols[0:rows, 1:2],
          in0=cols[0:rows, 0:1], scalar1=1e-5, scalar2=None, op0=ALU.max)
    pb.op("vector", "reciprocal", out=cols[0:rows, 2:3], in_=cols[0:rows, 1:2])
    pb.op("vector", "tensor_scalar", out=cols[0:rows, 3:4],
          in0=cols[0:rows, 2:3], scalar1=127.0, scalar2=None, op0=ALU.mult)
    pb.op("vector", "tensor_scalar", out=cols[0:rows, 4:5],
          in0=cols[0:rows, 1:2], scalar1=1.0 / 127.0, scalar2=None, op0=ALU.mult)
    pb.op("vector", "tensor_scalar", out=xv, in0=xv,
          scalar1=cols[0:rows, 3:4], scalar2=MAGIC, op0=ALU.mult, op1=ALU.add)
    pb.op("vector", "tensor_scalar", out=xv, in0=xv,
          scalar1=-MAGIC, scalar2=None, op0=ALU.add)
    pb.op("vector", "tensor_scalar", out=xv, in0=xv,
          scalar1=cols[0:rows, 4:5], scalar2=None, op0=ALU.mult, sem=sV)


def emit_ffn1(pb, xres, dram, fw, scratch, sems, dma_sem):
    sV, sS, sT = sems["sV"], sems["sS"], sems["sT"]
    ident = scratch["ident"]
    ps_tr, ps_h = scratch["ps_tr"], scratch["ps_h"]
    ps_y0, ps_y1 = scratch["ps_y0"], scratch["ps_y1"]
    xqT = scratch["xqT_all"]
    xlnb = scratch["xlnb"]
    cols = scratch["fcols"]
    hpre, hsb = scratch["hpre"], scratch["hsb"]
    hcv = scratch["hcv"]
    pm = scratch["pm"]
    sdq = scratch["sdq"]
    srow = scratch["srow"]
    w1s, w2s = scratch["w1s"], scratch["w2s"]
    cwc, cbc = fw["cwc"], fw["cbc"]
    dmaW = scratch["dmaW"]

    emit_ln_stats(pb, xres, scratch["stats"], scratch["rstd"], scratch["nbias"],
                  sems, dummy=xlnb)

    xqT3 = xqT.rearrange("p (j t) -> p j t", t=2050)
    pb.op("vector", "reduce_sum", out=cols[0:2, 5:6],
          in_=scratch["xhalo_sb"][0:2, 0:D], axis=AX.X, sem=sV)
    pb.wait("scalar", sV)
    pb.op("scalar", "activation", out=xlnb[0:2, 0:D],
          in_=scratch["xhalo_sb"][0:2, 0:D], func=AF.Square,
          accum_out=cols[0:2, 6:7], sem=sS)
    pb.wait("vector", sS)
    pb.op("vector", "tensor_scalar", out=cols[0:2, 5:6], in0=cols[0:2, 5:6],
          scalar1=1.0 / D, scalar2=None, op0=ALU.mult)
    pb.op("vector", "tensor_tensor", out=cols[0:2, 7:8], in0=cols[0:2, 5:6],
          in1=cols[0:2, 5:6], op=ALU.mult)
    pb.op("vector", "scalar_tensor_tensor", out=cols[0:2, 7:8],
          in0=cols[0:2, 6:7], scalar=1.0 / D, in1=cols[0:2, 7:8],
          op0=ALU.mult, op1=ALU.subtract, sem=sV)
    pb.wait("scalar", sV)
    pb.op("scalar", "activation", out=cols[0:2, 8:9], in_=cols[0:2, 7:8],
          func=AF.Ln, bias=pb.cf(1e-5))
    pb.op("scalar", "activation", out=cols[0:2, 8:9], in_=cols[0:2, 8:9],
          func=AF.Exp, scale=pb.cf(-0.5), sem=sS)
    pb.wait("vector", sS)
    pb.op("vector", "scalar_tensor_tensor", out=cols[0:2, 9:10],
          in0=cols[0:2, 5:6], scalar=-1.0, in1=cols[0:2, 8:9],
          op0=ALU.mult, op1=ALU.mult)
    pb.op("vector", "tensor_scalar", out=xlnb[0:2, 0:D],
          in0=scratch["xhalo_sb"][0:2, 0:D], scalar1=cols[0:2, 8:9],
          scalar2=cols[0:2, 9:10], op0=ALU.mult, op1=ALU.add, sem=sV)
    emit_actquant_token(pb, xlnb[0:2, 0:D], cols, 2, sems)
    pb.wait("tensor", sV)
    for j in range(ND):
        pb.op("tensor", "transpose", ps_tr[:, 0:2],
              xlnb[0:2, 128 * j:128 * (j + 1)], ident[0:2, 0:2], sem=sT)
        pb.wait("vector", sT)
        pb.op("vector", "tensor_copy", out=xqT3[:, j, 0:1], in_=ps_tr[:, 0:1])
        pb.op("vector", "tensor_copy", out=xqT3[:, j, 2049:2050],
              in_=ps_tr[:, 1:2], sem=sV)
        pb.wait("tensor", sV)
    for i in range(NT):
        half = i % 2
        if i >= 2:
            pb.wait("vector", sT)
        emit_ln_apply(pb, xres[:, D * i:D * (i + 1)],
                      xlnb[:, D * half:D * (half + 1)],
                      scratch["rstd"][:, i:i + 1], scratch["nbias"][:, i:i + 1])
        emit_actquant_token(pb, xlnb[:, D * half:D * (half + 1)], cols, 128, sems)
        pb.wait("tensor", sV)
        for g in range(2):
            pb.wait("tensor", sV)
            for jj in range(4):
                j = 4 * g + jj
                pb.op("tensor", "transpose", ps_tr[:, 128 * jj:128 * (jj + 1)],
                      xlnb[:, D * half + 128 * j:D * half + 128 * (j + 1)],
                      ident[:], sem=sT if jj == 3 else None)
            pb.wait("vector", sT)
            pb.op("vector", "tensor_copy",
                  out=xqT3[:, 4 * g:4 * g + 4, 1 + 128 * i:1 + 128 * (i + 1)],
                  in_=ps_tr[:].rearrange("p (j t) -> p j t", t=128), sem=sV)
    pb.wait("tensor", sV)

    for tc in range(4):
        c0 = 512 * tc
        w1q3 = dram["w1q_dram"].rearrange("(j p) c -> p j c", p=128)
        for hb in range(16):
            par = hb % 2
            psem = dmaW[par]
            if hb >= 2:
                pb.wait("sync", sT, pb.cnt[sT])
            pb.dma(w1s[:, 2048 * par:2048 * (par + 1)]
                   .rearrange("p (j c) -> p j c", c=256),
                   w1q3[:, :, 256 * hb:256 * (hb + 1)], psem)
            pb.wait("tensor", psem)
            for mm in range(2):
                m = 2 * hb + mm
                lts = [w1s[:, 2048 * par + 256 * j + 128 * mm:
                           2048 * par + 256 * j + 128 * (mm + 1)]
                       for j in range(ND)]
                pairs = [(lts[j], xqT[:, 2050 * j + c0:2050 * j + c0 + 512])
                         for j in range(ND)]
                emit_mm(pb, ps_h[:, 0:512], pairs)
                pairs = [(lts[j], xqT[:, 2050 * j + c0 + 512:2050 * j + c0 + 514])
                         for j in range(ND)]
                emit_mm(pb, scratch["ps_hs"][:, 0:2], pairs, sem=sT)
                pb.wait("vector", sT)
                pb.op("vector", "tensor_copy", out=hpre[:, 514 * m:514 * m + 512],
                      in_=ps_h[:, 0:512])
                pb.op("vector", "tensor_copy",
                      out=hpre[:, 514 * m + 512:514 * (m + 1)],
                      in_=scratch["ps_hs"][:, 0:2], sem=sV)
                pb.wait("tensor", sV)
        for m in range(32):
            pb.wait("vector", sS)
            pb.op("vector", "tensor_scalar", out=hcv[:],
                  in0=hpre[:, 514 * m + 1:514 * m + 513],
                  scalar1=cwc[:, 3 * m + 1:3 * m + 2], scalar2=None, op0=ALU.mult)
            pb.op("vector", "scalar_tensor_tensor", out=hcv[:],
                  in0=hpre[:, 514 * m + 0:514 * m + 512],
                  scalar=cwc[:, 3 * m + 0:3 * m + 1], in1=hcv[:],
                  op0=ALU.mult, op1=ALU.add)
            pb.op("vector", "scalar_tensor_tensor", out=hcv[:],
                  in0=hpre[:, 514 * m + 2:514 * m + 514],
                  scalar=cwc[:, 3 * m + 2:3 * m + 3], in1=hcv[:],
                  op0=ALU.mult, op1=ALU.add, sem=sV)
            pb.wait("scalar", sV)
            pb.op("scalar", "activation", out=hsb[:, 512 * m:512 * (m + 1)],
                  in_=hcv[:], func=AF.Silu, scale=fw["dq_col"][:, 0:1],
                  bias=cbc[:, m:m + 1], sem=sS)
            if m == 0:
                pb.op("scalar", "activation", out=pm[:],
                      in_=hsb[:, 512 * m:512 * (m + 1)], func=AF.Abs, sem=sS)
            else:
                pb.op("scalar", "activation", out=hcv[:],
                      in_=hsb[:, 512 * m:512 * (m + 1)], func=AF.Abs, sem=sS)
                pb.wait("vector", sS)
                pb.op("vector", "tensor_tensor", out=pm[:], in0=hcv[:],
                      in1=pm[:], op=ALU.max, sem=sV)
        pb.wait("tensor", sV)
        for q in range(4):
            pb.op("tensor", "transpose", ps_tr[:, 128 * q:128 * (q + 1)],
                  pm[:, 128 * q:128 * (q + 1)], ident[:],
                  sem=sT if q == 3 else None)
        pb.wait("vector", sT)
        for q in range(4):
            pb.op("vector", "tensor_reduce", out=cols[:, q:q + 1],
                  in_=ps_tr[:, 128 * q:128 * (q + 1)], axis=AX.X,
                  op=ALU.max, apply_absolute_value=False,
                  sem=sV if q == 3 else None)
        pb.wait("tensor", sV)
        pb.op("vector", "tensor_scalar", out=cols[:, 0:4], in0=cols[:, 0:4],
              scalar1=1e-5, scalar2=None, op0=ALU.max)
        pb.op("vector", "reciprocal", out=cols[:, 4:8], in_=cols[:, 0:4])
        pb.op("vector", "tensor_scalar", out=cols[:, 4:8], in0=cols[:, 4:8],
              scalar1=127.0, scalar2=None, op0=ALU.mult)
        pb.op("vector", "tensor_scalar", out=cols[:, 8:12], in0=cols[:, 0:4],
              scalar1=1.0 / 127.0, scalar2=None, op0=ALU.mult)
        pb.op("vector", "tensor_scalar", out=cols[:, 8:12], in0=cols[:, 8:12],
              scalar1=fw["dq_col"][:, 1:2], scalar2=None, op0=ALU.mult, sem=sV)
        pb.wait("tensor", sV)
        pb.op("tensor", "transpose", ps_tr[0:8, 0:128], cols[:, 4:12], ident[:], sem=sT)
        pb.wait("vector", sT)
        pb.op("vector", "tensor_copy", out=scratch["srow8"][0:8, :],
              in_=ps_tr[0:8, 0:128], sem=sV)
        pb.wait("sync", sV)
        pb.op("sync", "dma_start", out=srow[0:1, 0:1024],
              in_=scratch["srow8"][0:8, 0:128], sem=dma_sem, inc=16)
        pb.wait("tensor", dma_sem)
        pb.op("tensor", "matmul", ps_h[:, 0:512], fw["ones_row"][0:1, :],
              srow[0:1, 0:512], start=True, stop=True)
        pb.op("tensor", "matmul", ps_tr[:, 0:512], fw["ones_row"][0:1, :],
              srow[0:1, 512:1024], start=True, stop=True, sem=sT)
        pb.wait("vector", sT)
        pb.op("vector", "tensor_copy", out=sdq[:, 0:512], in_=ps_h[:, 0:512])
        pb.op("vector", "tensor_copy", out=sdq[:, 512:1024], in_=ps_tr[:, 0:512],
              sem=sV)
        for m in range(32):
            pb.op("vector", "tensor_tensor", out=hcv[:],
                  in0=hsb[:, 512 * m:512 * (m + 1)], in1=sdq[:, 0:512],
                  op=ALU.mult)
            pb.op("vector", "tensor_scalar", out=hcv[:], in0=hcv[:],
                  scalar1=MAGIC, scalar2=-MAGIC, op0=ALU.add, op1=ALU.add)
            pb.op("vector", "tensor_tensor", out=hsb[:, 512 * m:512 * (m + 1)],
                  in0=hcv[:], in1=sdq[:, 512:1024], op=ALU.mult,
                  sem=sV if m == 31 else None)
        pb.wait("tensor", sV)
        sG = sems["sG"]
        for wave in range(2):
            for m in range(32):
                par = m % 2
                psem = dmaW[par]
                if wave + m > 0:
                    pb.wait("sync", sG, max(0, pb.cnt[sG] - 1))
                pb.dma(w2s[:, 1024 * par:1024 * (par + 1)],
                       dram["w2q_dram"][128 * m:128 * (m + 1), :], psem)
                pb.wait("tensor", psem)
                for s2 in range(2):
                    s = 2 * wave + s2
                    psy = ps_y0 if s2 == 0 else ps_y1
                    for dh in range(2):
                        pb.op("tensor", "matmul", psy[:, 512 * dh:512 * (dh + 1)],
                              hsb[:, 512 * m + 128 * s:512 * m + 128 * (s + 1)],
                              w2s[:, 1024 * par + 512 * dh:1024 * par + 512 * (dh + 1)],
                              start=(m == 0), stop=(m == 31))
                pb.draininc("tensor", sG)
            pb.wait("vector", sG)
            for s2 in range(2):
                s = 2 * wave + s2
                i = 4 * tc + s
                psy = ps_y0 if s2 == 0 else ps_y1
                pb.op("vector", "tensor_tensor", out=xres[:, D * i:D * (i + 1)],
                      in0=xres[:, D * i:D * (i + 1)], in1=psy[:],
                      op=ALU.add, sem=sV)
            pb.wait("tensor", sV)


def alloc_ffn1(pb, tag=""):
    fw = {
        "wcols": pb.sbuf("f1wcols" + tag, [128, 8], F32),
        "acc": pb.sbuf("f1acc" + tag, [128, 2], F32),
        "ones_col": pb.sbuf("f1onesc" + tag, [128, 1], F32),
        "ones_row": pb.sbuf("f1onesr" + tag, [1, 128], F32),
        "dq_col": pb.sbuf("f1dq" + tag, [128, 2], F32),
        "sw_col": pb.sbuf("f1sw" + tag, [128, 2], F32),
        "cwc": pb.sbuf("f1cwc" + tag, [128, 96], F32),
        "cbc": pb.sbuf("f1cbc" + tag, [128, 32], F32),
    }
    scratch = {
        "stage": pb.sbuf("f1stage" + tag, [128, 2048], F32),
        "xqT_all": pb.sbuf("f1xqT" + tag, [128, ND * 2050], BF16),
        "xlnb": pb.sbuf("f1xlnb" + tag, [128, 2 * D], F32),
        "fcols": pb.sbuf("f1fcols" + tag, [128, 16], F32),
        "hpre": pb.sbuf("f1hpre" + tag, [128, 32 * 514], BF16),
        "hsb": pb.sbuf("f1hsb" + tag, [128, 32 * 512], BF16),
        "hcv": pb.sbuf("f1hcv" + tag, [128, 512], F32),
        "pm": pb.sbuf("f1pm" + tag, [128, 512], F32),
        "sdq": pb.sbuf("f1sdq" + tag, [128, 1024], F32),
        "srow": pb.sbuf("f1srow" + tag, [1, 1024], F32),
        "srow8": pb.sbuf("f1srow8" + tag, [8, 128], F32),
        "w1s": pb.sbuf("f1w1s" + tag, [128, 2 * 2048], BF16),
        "w2s": pb.sbuf("f1w2s" + tag, [128, 2 * 1024], BF16),
        "xhalo_sb": pb.sbuf("f1xhalo" + tag, [2, D], F32),
        "stats": pb.sbuf("f1stats" + tag, [128, 4 * NT], F32),
        "rstd": pb.sbuf("f1rstd" + tag, [128, NT], F32),
        "nbias": pb.sbuf("f1nbias" + tag, [128, NT], F32),
    }
    return fw, scratch


def build_ffn1_program():
    nc = bass.Bass("TRN2", target_bir_lowering=False, debug=False)
    pb = PB(nc)
    dram = {
        "xres": nc.dram_tensor("xres", [TOK, D], F32, kind="ExternalInput").ap(),
        "xhalo": nc.dram_tensor("xhalo", [2, D], F32, kind="ExternalInput").ap(),
        "f1w1T": nc.dram_tensor("f1w1T", [D, 4096], F32, kind="ExternalInput").ap(),
        "f1w2T": nc.dram_tensor("f1w2T", [4096, D], F32, kind="ExternalInput").ap(),
        "cw": nc.dram_tensor("cw", [128, 96], F32, kind="ExternalInput").ap(),
        "cb": nc.dram_tensor("cb", [128, 32], F32, kind="ExternalInput").ap(),
        "ident": nc.dram_tensor("ident", [128, 128], F32, kind="ExternalInput").ap(),
        "onesc": nc.dram_tensor("onesc", [128, 1], F32, kind="ExternalInput").ap(),
        "onesr": nc.dram_tensor("onesr", [1, 128], F32, kind="ExternalInput").ap(),
        "w1q_dram": nc.dram_tensor("w1q_dram", [D, 4096], BF16, kind="Internal").ap(),
        "w2q_dram": nc.dram_tensor("w2q_dram", [4096, D], BF16, kind="Internal").ap(),
        "xout": nc.dram_tensor("xout", [TOK, D], F32, kind="ExternalOutput").ap(),
    }
    fw, scratch = alloc_ffn1(pb)
    scratch["ident"] = pb.sbuf("identsb", [128, 128], F32)
    scratch["ps_tr"] = pb.psum("ps_tr", [128, 512], F32)
    scratch["ps_h"] = pb.psum("ps_h", [128, 512], F32)
    scratch["ps_hs"] = pb.psum("ps_hs", [128, 2], F32)
    scratch["ps_y0"] = pb.psum("ps_y0", [128, 1024], F32)
    scratch["ps_y1"] = pb.psum("ps_y1", [128, 1024], F32)
    scratch["ps_one"] = pb.psum("ps_one", [128, 8], F32)
    xres = pb.sbuf("xres_sb", [128, NT * D], F32)
    dma_sem = pb.sem("dma")
    sems = {"sV": pb.sem("sV"), "sS": pb.sem("sS"), "sT": pb.sem("sT"),
            "sG": pb.sem("sG")}
    scratch["dmaW"] = (pb.sem("dmaW0"), pb.sem("dmaW1"))
    st = pb.sem("st")

    pb.op("sync", "dma_start", out=xres.rearrange("p (i d) -> p i d", d=D),
          in_=dram["xres"].rearrange("(i p) d -> p i d", p=128),
          sem=dma_sem, inc=16)
    pb.op("sync", "dma_start", out=scratch["xhalo_sb"][:], in_=dram["xhalo"][:],
          sem=dma_sem, inc=16)
    pb.op("sync", "dma_start", out=scratch["ident"][:], in_=dram["ident"][:],
          sem=dma_sem, inc=16)
    pb.op("sync", "dma_start", out=fw["ones_col"][:], in_=dram["onesc"][:],
          sem=dma_sem, inc=16)
    pb.op("sync", "dma_start", out=fw["ones_row"][0:1, :], in_=dram["onesr"][:],
          sem=dma_sem, inc=16)
    pb.op("sync", "dma_start", out=fw["cwc"][:], in_=dram["cw"][:],
          sem=dma_sem, inc=16)
    pb.op("sync", "dma_start", out=fw["cbc"][:], in_=dram["cb"][:],
          sem=dma_sem, inc=16)
    pb.wait("vector", dma_sem)
    pb.wait("scalar", dma_sem)
    pb.wait("tensor", dma_sem)

    emit_ffn1_weightprep(pb, dram, fw, scratch, sems, dma_sem)
    emit_ffn1(pb, xres, dram, fw, scratch, sems, dma_sem)

    pb.wait("sync", sems["sV"])
    pb.op("sync", "dma_start",
          out=dram["xout"].rearrange("(i p) d -> p i d", p=128),
          in_=xres.rearrange("p (i d) -> p i d", d=D), sem=st, inc=16)
    pb.wait("sync", st)
    return pb.build()


def host_ffn1_inputs(w1, conv_w, conv_b, w2):
    f1w1T = np.ascontiguousarray(w1.T)
    f1w2T = np.ascontiguousarray(w2.T)
    cw = np.ascontiguousarray(
        conv_w.reshape(32, 128, 3).transpose(1, 0, 2).reshape(128, 96))
    cb = np.ascontiguousarray(conv_b.reshape(32, 128).T)
    return f1w1T, f1w2T, cw, cb




def emit_attnb1(pb, xres, aw, scratch, sems, dma_sem):
    sV, sS, sT = sems["sV"], sems["sS"], sems["sT"]
    ident = scratch["ident"]
    ps_a, ps_c, ps_u = scratch["ps_a"], scratch["ps_c"], scratch["ps_u"]
    ps_tr, ps_sst = scratch["ps_tr"], scratch["ps_sst"]
    x7T = scratch["x7T"]
    vbuf = scratch["vbuf"]
    Wst = scratch["Wst"]
    WwT, WrT = aw["WwT_bf"], aw["WrT_bf"]
    WpT = WwT
    xlnb = scratch["rc"]
    cols = scratch["acols"]
    corrf = scratch["corrf"]
    vt = scratch["rT"]
    smt = scratch["smt"]
    rc = scratch["rc"]
    wkc = scratch["wkc"]
    rT = scratch["rT"]
    MT = scratch["MT"]

    pb.op("scalar", "activation", out=cols[0:1, 0:1], in_=aw["decay_sb"][0:1, 0:1],
          func=AF.Sigmoid)
    pb.op("scalar", "activation", out=cols[0:1, 1:2], in_=cols[0:1, 0:1],
          func=AF.Ln)
    pb.op("scalar", "activation", out=cols[0:1, 2:3], in_=aw["la_sb"][0:1, 0:1],
          func=AF.Exp, sem=sS)
    pb.wait("tensor", sS)
    pb.op("tensor", "matmul", ps_sst[:, 0:2], aw["ones_row"][0:1, :],
          cols[0:1, 1:3], start=True, stop=True)
    pb.op("tensor", "matmul", ps_sst[:, 3:4], aw["ones_row"][0:1, :],
          aw["gflag_sb"][0:1, 0:1], start=True, stop=True, sem=sT)
    pb.wait("vector", sT)
    pb.op("vector", "tensor_copy", out=cols[:, 4:6], in_=ps_sst[:, 0:2])
    pb.op("vector", "tensor_copy", out=cols[:, 6:7], in_=ps_sst[:, 3:4], sem=sV)
    lg_bc, alpha_bc, gflag_bc = cols[:, 4:5], cols[:, 5:6], cols[:, 6:7]
    pb.wait("scalar", sV)
    pb.op("scalar", "activation", out=cols[:, 7:8], in_=aw["iotaDI"][:],
          func=AF.Exp, scale=lg_bc)
    pb.op("scalar", "activation", out=cols[:, 8:9], in_=aw["iotaGW"][:],
          func=AF.Exp, scale=lg_bc)
    pb.op("scalar", "activation", out=cols[:, 9:10], in_=aw["iota64"][:],
          func=AF.Exp, scale=lg_bc)
    pb.op("scalar", "activation", out=corrf[:], in_=aw["iotat"][:],
          func=AF.Exp, scale=lg_bc)
    pb.op("scalar", "activation", out=MT[:], in_=aw["dmat"][:],
          func=AF.Exp, scale=lg_bc, sem=sS)
    pb.wait("vector", sS)
    pb.op("vector", "tensor_scalar", out=corrf[:], in0=corrf[:],
          scalar1=gflag_bc, scalar2=None, op0=ALU.mult)
    pb.op("vector", "tensor_tensor", out=MT[:], in0=MT[:], in1=aw["causalT"][:],
          op=ALU.mult, sem=sV)
    dec_col, gw_col, gC_bc = cols[:, 7:8], cols[:, 8:9], cols[:, 9:10]

    emit_ln_stats(pb, xres, scratch["stats"], scratch["rstd"], scratch["nbias"],
                  sems, dummy=xlnb[:, 0:D])
    x7T3 = x7T.rearrange("p (j t) -> p j t", t=2049)
    pb.op("vector", "reduce_sum", out=cols[0:1, 10:11],
          in_=scratch["whalo_sb"][0:1, :], axis=AX.X, sem=sV)
    pb.wait("scalar", sV)
    pb.op("scalar", "activation", out=xlnb[0:1, 0:D],
          in_=scratch["whalo_sb"][0:1, :], func=AF.Square,
          accum_out=cols[0:1, 11:12], sem=sS)
    pb.wait("vector", sS)
    pb.op("vector", "tensor_scalar", out=cols[0:1, 10:11], in0=cols[0:1, 10:11],
          scalar1=1.0 / D, scalar2=None, op0=ALU.mult)
    pb.op("vector", "tensor_tensor", out=cols[0:1, 12:13], in0=cols[0:1, 10:11],
          in1=cols[0:1, 10:11], op=ALU.mult)
    pb.op("vector", "scalar_tensor_tensor", out=cols[0:1, 12:13],
          in0=cols[0:1, 11:12], scalar=1.0 / D, in1=cols[0:1, 12:13],
          op0=ALU.mult, op1=ALU.subtract, sem=sV)
    pb.wait("scalar", sV)
    pb.op("scalar", "activation", out=cols[0:1, 13:14], in_=cols[0:1, 12:13],
          func=AF.Ln, bias=pb.cf(1e-5))
    pb.op("scalar", "activation", out=cols[0:1, 13:14], in_=cols[0:1, 13:14],
          func=AF.Exp, scale=pb.cf(-0.5), sem=sS)
    pb.wait("vector", sS)
    pb.op("vector", "scalar_tensor_tensor", out=cols[0:1, 14:15],
          in0=cols[0:1, 10:11], scalar=-1.0, in1=cols[0:1, 13:14],
          op0=ALU.mult, op1=ALU.mult)
    pb.op("vector", "tensor_scalar", out=xlnb[0:1, 0:D],
          in0=scratch["whalo_sb"][0:1, :], scalar1=cols[0:1, 13:14],
          scalar2=cols[0:1, 14:15], op0=ALU.mult, op1=ALU.add, sem=sV)
    pb.wait("tensor", sV)
    for j in range(ND):
        pb.op("tensor", "transpose", ps_tr[:, 0:1],
              xlnb[0:1, 128 * j:128 * (j + 1)], ident[0:1, 0:1], sem=sT)
        pb.wait("vector", sT)
        pb.op("vector", "tensor_copy", out=x7T3[:, j, 0:1], in_=ps_tr[:, 0:1],
              sem=sV)
        pb.wait("tensor", sV)
    for i in range(NT):
        pb.wait("vector", sT)
        emit_ln_apply(pb, xres[:, D * i:D * (i + 1)], xlnb[:, 0:D],
                      scratch["rstd"][:, i:i + 1], scratch["nbias"][:, i:i + 1],
                      sem=sV)
        pb.wait("tensor", sV)
        for g in range(2):
            pb.wait("tensor", sV)
            for jj in range(4):
                j = 4 * g + jj
                pb.op("tensor", "transpose", ps_tr[:, 128 * jj:128 * (jj + 1)],
                      xlnb[:, 128 * j:128 * (j + 1)],
                      ident[:], sem=sT if jj == 3 else None)
            pb.wait("vector", sT)
            pb.op("vector", "tensor_copy",
                  out=x7T3[:, 4 * g:4 * g + 4, 1 + 128 * i:1 + 128 * (i + 1)],
                  in_=ps_tr[:].rearrange("p (j t) -> p j t", t=128), sem=sV)
    pb.wait("tensor", sV)

    for i in range(NT):
        for dh in range(2):
            pairs = [(x7T[:, 2049 * j + 1 + 128 * i:2049 * j + 1 + 128 * (i + 1)],
                      WwT[:, 1024 * j + 512 * dh:1024 * j + 512 * (dh + 1)])
                     for j in range(ND)]
            emit_mm(pb, ps_a[:, 512 * dh:512 * (dh + 1)], pairs,
                    sem=sT if dh == 1 else None)
        pb.wait("vector", sT)
        pb.op("vector", "tensor_copy", out=vbuf[:, D * i:D * (i + 1)],
              in_=ps_a[:, 0:1024], sem=sV)
        pb.wait("tensor", sV)

    pb.op("vector", "memset", Wst[:], 0.0, sem=sV)
    pb.wait("tensor", sV)

    for cg in range(32):
        i = cg // 2
        par = 64 * (cg % 2)
        tcol = 1 + 64 * cg
        wcol = 64 * cg
        pb.op("vector", "tensor_scalar", out=vt[par:par + 64, 0:D],
              in0=vbuf[par:par + 64, D * i:D * (i + 1)],
              scalar1=gw_col[par:par + 64, 0:1], scalar2=None, op0=ALU.mult,
              sem=sV)
        for j in range(ND):
            pb.op("tensor", "matmul", ps_sst[par:par + 64, 0:64],
                  x7T[:, 2049 * j + wcol:2049 * j + wcol + 64],
                  x7T[:, 2049 * j + tcol:2049 * j + tcol + 64],
                  start=(j == 0), stop=(j == ND - 1),
                  sem=sT if j == ND - 1 else None)
        pb.wait("vector", sT)
        pb.op("vector", "tensor_tensor", out=smt[par:par + 64, 0:64],
              in0=ps_sst[par:par + 64, 0:64], in1=MT[par:par + 64, 0:64],
              op=ALU.mult, sem=sV)
        for dh in range(2):
            pairs = [(x7T[:, 2049 * j + tcol:2049 * j + tcol + 64],
                      Wst[:, 1024 * j + 512 * dh:1024 * j + 512 * (dh + 1)])
                     for j in range(ND)]
            emit_mm(pb, ps_a[par:par + 64, 512 * dh:512 * (dh + 1)], pairs)
        pb.wait("tensor", sV)
        for dh in range(2):
            pb.op("tensor", "matmul", ps_c[par:par + 64, 512 * dh:512 * (dh + 1)],
                  smt[par:par + 64, 0:64],
                  vbuf[par:par + 64, D * i + 512 * dh:D * i + 512 * (dh + 1)],
                  start=True, stop=True, sem=sT if dh == 1 else None)
        for j in range(ND):
            pb.op("tensor", "transpose",
                  scratch["ps_trb"][par:par + 64, 128 * (j % 4):128 * (j % 4) + 128],
                  x7T[:, 2049 * j + wcol:2049 * j + wcol + 64],
                  scratch["identb"][:], sem=sT if (j % 4) == 3 else None)
            if (j % 4) == 3:
                pb.wait("vector", sT)
                pb.op("vector", "tensor_copy",
                      out=wkc[par:par + 64, 512 * (j // 4):512 * (j // 4) + 512],
                      in_=scratch["ps_trb"][par:par + 64, 0:512], sem=sV)
                pb.wait("tensor", sV)
        pb.wait("vector", sT)
        pb.op("vector", "tensor_scalar", out=rc[par:par + 64, 0:D],
              in0=ps_a[par:par + 64, 0:1024],
              scalar1=dec_col[par:par + 64, 0:1], scalar2=None, op0=ALU.mult)
        pb.op("vector", "tensor_tensor", out=rc[par:par + 64, 0:D],
              in0=rc[par:par + 64, 0:D], in1=ps_c[par:par + 64, 0:1024],
              op=ALU.add)
        pb.op("vector", "tensor_copy",
              out=vbuf[par:par + 64, D * i:D * (i + 1)],
              in_=rc[par:par + 64, 0:D], sem=sV)
        for j in range(ND):
            for dh in range(2):
                pb.op("tensor", "matmul", ps_u[:, 0:512],
                      wkc[par:par + 64, 128 * j:128 * (j + 1)],
                      vt[par:par + 64, 512 * dh:512 * (dh + 1)],
                      start=True, stop=True, sem=sT)
                pb.wait("vector", sT)
                pb.op("vector", "scalar_tensor_tensor",
                      out=Wst[:, 1024 * j + 512 * dh:1024 * j + 512 * (dh + 1)],
                      in0=Wst[:, 1024 * j + 512 * dh:1024 * j + 512 * (dh + 1)],
                      scalar=gC_bc, in1=ps_u[:, 0:512],
                      op0=ALU.mult, op1=ALU.add, sem=sV)
                pb.wait("tensor", sV)

    pb.wait("sync", sems["sT"], pb.cnt[sems["sT"]])
    pb.wait("sync", dma_sem, pb.cnt[dma_sem])
    pb.dma(WpT[:].rearrange("p (j d) -> p j d", d=1024),
           scratch["WpT_dram"].rearrange("(j p) d -> p j d", p=128), dma_sem)
    pb.wait("tensor", dma_sem)

    for i in range(NT):
        for dh in range(2):
            pairs = [(x7T[:, 2049 * j + 1 + 128 * i:2049 * j + 1 + 128 * (i + 1)],
                      WpT[:, 1024 * j + 512 * dh:1024 * j + 512 * (dh + 1)])
                     for j in range(ND)]
            emit_mm(pb, ps_a[:, 512 * dh:512 * (dh + 1)], pairs,
                    sem=sT if dh == 1 else None)
        pb.wait("vector", sT)
        pb.op("vector", "scalar_tensor_tensor", out=rc[:, 0:D],
              in0=ps_a[:, 0:1024], scalar=corrf[:, i:i + 1],
              in1=vbuf[:, D * i:D * (i + 1)], op0=ALU.mult, op1=ALU.add, sem=sV)
        pb.wait("tensor", sV)
        for g in range(2):
            pb.wait("tensor", sV)
            for jj in range(4):
                j = 4 * g + jj
                pb.op("tensor", "transpose",
                      ps_tr[:, 128 * jj:128 * (jj + 1)],
                      rc[:, 128 * j:128 * (j + 1)], ident[:],
                      sem=sT if jj == 3 else None)
            pb.wait("vector", sT)
            pb.op("vector", "tensor_copy",
                  out=rT[:, 512 * g:512 * (g + 1)]
                  .rearrange("p (j t) -> p j t", t=128),
                  in_=ps_tr[:].rearrange("p (j t) -> p j t", t=128),
                  sem=sV)
        pb.wait("tensor", sV)
        for dh in range(2):
            pairs = [(rT[:, 128 * j:128 * (j + 1)],
                      WrT[:, 1024 * j + 512 * dh:1024 * j + 512 * (dh + 1)])
                     for j in range(ND)]
            emit_mm(pb, ps_c[:, 512 * dh:512 * (dh + 1)], pairs,
                    sem=sT if dh == 1 else None)
        pb.op("vector", "tensor_scalar", out=rc[:],
              in0=xres[:, D * i:D * (i + 1)],
              scalar1=scratch["rstd"][:, i:i + 1],
              scalar2=scratch["nbias"][:, i:i + 1],
              op0=ALU.mult, op1=ALU.add)
        pb.op("vector", "tensor_tensor", out=xres[:, D * i:D * (i + 1)],
              in0=xres[:, D * i:D * (i + 1)], in1=rc[:], op=ALU.add)
        pb.wait("vector", sT)
        pb.op("vector", "scalar_tensor_tensor", out=xres[:, D * i:D * (i + 1)],
              in0=ps_c[:, 0:1024], scalar=alpha_bc,
              in1=xres[:, D * i:D * (i + 1)], op0=ALU.mult, op1=ALU.add, sem=sV)
        pb.wait("tensor", sV)
    pb.wait("sync", sems["sV"])


def attnb1_weight_prep(pb, dram, aw, stage, dma_sem, sems):
    sV = sems["sV"]
    for (key, nm) in (("WwT_bf", "WwT"), ("WrT_bf", "WrT")):
        rows = [dram[nm][128 * j:128 * (j + 1), :] for j in range(ND)]
        dsts = [aw[key][:, 1024 * j:1024 * (j + 1)] for j in range(ND)]
        emit_load_cast(pb, rows, stage, dsts, dma_sem, sems)
    for nm, sb in (("decay", "decay_sb"), ("log_alpha", "la_sb"),
                   ("gflag", "gflag_sb"), ("onesr", "ones_row"),
                   ("iotaDI", "iotaDI"), ("iotaGW", "iotaGW"),
                   ("iota64", "iota64"), ("iotat", "iotat"),
                   ("dmat", "dmat"), ("causalT", "causalT"),
                   ("whalo", "whalo_sb_src")):
        pb.op("sync", "dma_start", out=aw[nm + "_dst"] if False else None, in_=None)             if False else None
    pb.op("sync", "dma_start", out=aw["decay_sb"][0:1, :], in_=dram["decay"][:],
          sem=dma_sem, inc=16)
    pb.op("sync", "dma_start", out=aw["la_sb"][0:1, :], in_=dram["log_alpha"][:],
          sem=dma_sem, inc=16)
    pb.op("sync", "dma_start", out=aw["gflag_sb"][0:1, :], in_=dram["gflag"][:],
          sem=dma_sem, inc=16)
    pb.op("sync", "dma_start", out=aw["ones_row"][0:1, :], in_=dram["onesr"][:],
          sem=dma_sem, inc=16)
    pb.op("sync", "dma_start", out=aw["iotaDI"][:], in_=dram["iotaDI"][:],
          sem=dma_sem, inc=16)
    pb.op("sync", "dma_start", out=aw["iotaGW"][:], in_=dram["iotaGW"][:],
          sem=dma_sem, inc=16)
    pb.op("sync", "dma_start", out=aw["iota64"][:], in_=dram["iota64"][:],
          sem=dma_sem, inc=16)
    pb.op("sync", "dma_start", out=aw["iotat"][:], in_=dram["iotat"][:],
          sem=dma_sem, inc=16)
    pb.op("sync", "dma_start", out=aw["dmat"][:], in_=dram["dmat"][:],
          sem=dma_sem, inc=16)
    pb.op("sync", "dma_start", out=aw["causalT"][:], in_=dram["causalT"][:],
          sem=dma_sem, inc=16)


def alloc_attnb1(pb, tag=""):
    aw = {
        "WwT_bf": pb.sbuf("a1WwT" + tag, [128, ND * 1024], BF16),
        "WrT_bf": pb.sbuf("a1WrT" + tag, [128, ND * 1024], BF16),
        "decay_sb": pb.sbuf("a1dec" + tag, [1, 1], F32),
        "la_sb": pb.sbuf("a1la" + tag, [1, 1], F32),
        "gflag_sb": pb.sbuf("a1gf" + tag, [1, 1], F32),
        "ones_row": pb.sbuf("a1ones" + tag, [1, 128], F32),
        "iotaDI": pb.sbuf("a1iDI" + tag, [128, 1], F32),
        "iotaGW": pb.sbuf("a1iGW" + tag, [128, 1], F32),
        "iota64": pb.sbuf("a1i64" + tag, [128, 1], F32),
        "iotat": pb.sbuf("a1it" + tag, [128, 16], F32),
        "dmat": pb.sbuf("a1dm" + tag, [128, 64], F32),
        "causalT": pb.sbuf("a1ct" + tag, [128, 64], F32),
    }
    scratch = {
        "x7T": pb.sbuf("a1x7T" + tag, [128, ND * 2049], BF16),
        "vbuf": pb.sbuf("a1v" + tag, [128, NT * D], BF16),
        "Wst": pb.sbuf("a1W" + tag, [128, ND * 1024], BF16),
        "acols": pb.sbuf("a1cols" + tag, [128, 16], F32),
        "corrf": pb.sbuf("a1corrf" + tag, [128, 16], F32),
        "smt": pb.sbuf("a1smt" + tag, [128, 64], BF16),
        "rc": pb.sbuf("a1rc" + tag, [128, D], F32),
        "wkc": pb.sbuf("a1wkc" + tag, [128, D], BF16),
        "rT": pb.sbuf("a1rT" + tag, [128, D], BF16),
        "MT": pb.sbuf("a1MT" + tag, [128, 64], F32),
        "whalo_sb": pb.sbuf("a1whalo" + tag, [1, D], F32),
        "stats": pb.sbuf("a1stats" + tag, [128, 4 * NT], F32),
        "rstd": pb.sbuf("a1rstd" + tag, [128, NT], F32),
        "nbias": pb.sbuf("a1nbias" + tag, [128, NT], F32),
    }
    return aw, scratch


def build_attnb1_program():
    nc = bass.Bass("TRN2", target_bir_lowering=False, debug=False)
    pb = PB(nc)
    dram = {
        "xres": nc.dram_tensor("xres", [TOK, D], F32, kind="ExternalInput").ap(),
        "whalo": nc.dram_tensor("whalo", [1, D], F32, kind="ExternalInput").ap(),
        "WwT": nc.dram_tensor("WwT", [D, D], F32, kind="ExternalInput").ap(),
        "WrT": nc.dram_tensor("WrT", [D, D], F32, kind="ExternalInput").ap(),
        "WpT": nc.dram_tensor("WpT", [D, D], BF16, kind="ExternalInput").ap(),
        "decay": nc.dram_tensor("decay", [1, 1], F32, kind="ExternalInput").ap(),
        "log_alpha": nc.dram_tensor("log_alpha", [1, 1], F32, kind="ExternalInput").ap(),
        "gflag": nc.dram_tensor("gflag", [1, 1], F32, kind="ExternalInput").ap(),
        "onesr": nc.dram_tensor("onesr", [1, 128], F32, kind="ExternalInput").ap(),
        "iotaDI": nc.dram_tensor("iotaDI", [128, 1], F32, kind="ExternalInput").ap(),
        "iotaGW": nc.dram_tensor("iotaGW", [128, 1], F32, kind="ExternalInput").ap(),
        "iota64": nc.dram_tensor("iota64", [128, 1], F32, kind="ExternalInput").ap(),
        "iotat": nc.dram_tensor("iotat", [128, 16], F32, kind="ExternalInput").ap(),
        "dmat": nc.dram_tensor("dmat", [128, 64], F32, kind="ExternalInput").ap(),
        "causalT": nc.dram_tensor("causalT", [128, 64], F32, kind="ExternalInput").ap(),
        "ident": nc.dram_tensor("ident", [128, 128], F32, kind="ExternalInput").ap(),
        "identb": nc.dram_tensor("identb", [128, 128], BF16, kind="ExternalInput").ap(),
        "xout": nc.dram_tensor("xout", [TOK, D], F32, kind="ExternalOutput").ap(),
        "WT_out": nc.dram_tensor("WT_out", [D, D], BF16, kind="ExternalOutput").ap(),
    }
    aw, scratch = alloc_attnb1(pb)
    scratch["ident"] = pb.sbuf("identsb", [128, 128], F32)
    scratch["identb"] = pb.sbuf("identbsb", [128, 128], BF16)
    scratch["ps_a"] = pb.psum("ps_a", [128, 1024], F32)
    scratch["ps_c"] = pb.psum("ps_c", [128, 1024], F32)
    scratch["ps_u"] = pb.psum("ps_u", [128, 512], F32)
    scratch["ps_trb"] = pb.psum("ps_trb", [128, 512], BF16)
    scratch["ps_tr"] = pb.psum("ps_tr", [128, 512], F32)
    scratch["ps_sst"] = pb.psum("ps_sst", [128, 64], F32)
    xres = pb.sbuf("xres_sb", [128, NT * D], F32)
    dma_sem = pb.sem("dma")
    sems = {"sV": pb.sem("sV"), "sS": pb.sem("sS"), "sT": pb.sem("sT"),
            "sG": pb.sem("sG")}
    st = pb.sem("st")

    pb.op("sync", "dma_start", out=xres.rearrange("p (i d) -> p i d", d=D),
          in_=dram["xres"].rearrange("(i p) d -> p i d", p=128),
          sem=dma_sem, inc=16)
    pb.op("sync", "dma_start", out=scratch["whalo_sb"][0:1, :],
          in_=dram["whalo"][:], sem=dma_sem, inc=16)
    pb.op("sync", "dma_start", out=scratch["ident"][:], in_=dram["ident"][:],
          sem=dma_sem, inc=16)
    pb.op("sync", "dma_start", out=scratch["identb"][:], in_=dram["identb"][:],
          sem=dma_sem, inc=16)
    attnb1_weight_prep(pb, dram, aw, scratch["rc"], dma_sem, sems)
    pb.wait("vector", dma_sem)
    pb.wait("scalar", dma_sem)
    pb.wait("tensor", dma_sem)

    scratch["WpT_dram"] = dram["WpT"]
    emit_attnb1(pb, xres, aw, scratch, sems, dma_sem)

    pb.wait("sync", sems["sV"])
    pb.op("sync", "dma_start",
          out=dram["xout"].rearrange("(i p) d -> p i d", p=128),
          in_=xres.rearrange("p (i d) -> p i d", d=D), sem=st, inc=16)
    pb.op("sync", "dma_start",
          out=dram["WT_out"].rearrange("(j p) d -> p j d", p=128),
          in_=scratch["Wst"].rearrange("p (j d) -> p j d", d=D),
          sem=st, inc=16)
    pb.wait("sync", st)
    return pb.build()


def host_attnb1_consts():
    p = np.arange(128)
    iotaDI = (p % 64).astype(np.float32).reshape(128, 1)
    iotaGW = (63 - (p % 64)).astype(np.float32).reshape(128, 1)
    iota64 = np.full((128, 1), 64.0, np.float32)
    iotat = (p[:, None] + 128.0 * np.arange(16)[None, :]).astype(np.float32)
    cpr = p % 64
    c = np.arange(64)
    dmat = np.maximum(c[None, :] - 1 - cpr[:, None], 0).astype(np.float32)
    causalT = (c[None, :] > cpr[:, None]).astype(np.float32)
    return iotaDI, iotaGW, iota64, iotat, dmat, causalT




U5 = 2080


def emit_attna1(pb, xres, aw, scratch, sems, dma_sem):
    sV, sS, sT = sems["sV"], sems["sS"], sems["sT"]
    ident, identb = scratch["ident"], scratch["identb"]
    ps_a, ps_b, ps_c = scratch["ps_a"], scratch["ps_b"], scratch["ps_c"]
    ps_tr, ps_trb = scratch["ps_tr"], scratch["ps_trb"]
    ps_row, ps_dec = scratch["ps_row"], scratch["ps_dec"]
    qTh, qTl, vu = scratch["qTh"], scratch["qTl"], scratch["vu"]
    wsh = aw["wshare"]
    hebbT = aw["hebb_bf"]
    x5ch = scratch["x5ch"]
    xlnb = scratch["xlnb"]
    cols = scratch["ncols"]
    srw = scratch["srw"]
    zf = scratch["zf"]
    zh = scratch["zh"]
    zl = scratch["zl"]
    scr = scratch["scr"]
    scr2 = scratch["scr2"]
    bandf = scratch["bandf"]
    bandb = scratch["bandb"]
    bTa, bTb = scratch["bTa"], scratch["bTb"]
    phirow = scratch["phirow"]
    sbf = scratch["sbf"]
    qtm = scratch["qtm"]
    cA = scratch["cA"]
    ub = scratch["ub"]
    uT = scratch["uT"]
    sig = scratch["sig"]
    mdsq = scratch["mdsq"]
    vtmp = scratch["vtmp"]

    emit_ln_stats(pb, xres, scratch["stats"], scratch["rstd"], scratch["nbias"],
                  sems, dummy=xlnb)

    HR = 31
    x5h = scratch["x5h"]
    pb.op("vector", "reduce_sum", out=cols[0:HR, 0:1],
          in_=scratch["xhalo_sb"][0:HR, 0:D], axis=AX.X, sem=sV)
    pb.wait("scalar", sV)
    pb.op("scalar", "activation", out=xlnb[0:HR, 0:D],
          in_=scratch["xhalo_sb"][0:HR, 0:D], func=AF.Square,
          accum_out=cols[0:HR, 1:2], sem=sS)
    pb.wait("vector", sS)
    pb.op("vector", "tensor_scalar", out=cols[0:HR, 0:1], in0=cols[0:HR, 0:1],
          scalar1=1.0 / D, scalar2=None, op0=ALU.mult)
    pb.op("vector", "tensor_tensor", out=cols[0:HR, 2:3], in0=cols[0:HR, 0:1],
          in1=cols[0:HR, 0:1], op=ALU.mult)
    pb.op("vector", "scalar_tensor_tensor", out=cols[0:HR, 2:3],
          in0=cols[0:HR, 1:2], scalar=1.0 / D, in1=cols[0:HR, 2:3],
          op0=ALU.mult, op1=ALU.subtract, sem=sV)
    pb.wait("scalar", sV)
    pb.op("scalar", "activation", out=cols[0:HR, 3:4], in_=cols[0:HR, 2:3],
          func=AF.Ln, bias=pb.cf(1e-5))
    pb.op("scalar", "activation", out=cols[0:HR, 3:4], in_=cols[0:HR, 3:4],
          func=AF.Exp, scale=pb.cf(-0.5), sem=sS)
    pb.wait("vector", sS)
    pb.op("vector", "scalar_tensor_tensor", out=cols[0:HR, 4:5],
          in0=cols[0:HR, 0:1], scalar=-1.0, in1=cols[0:HR, 3:4],
          op0=ALU.mult, op1=ALU.mult)
    pb.op("vector", "tensor_scalar", out=xlnb[0:HR, 0:D],
          in0=scratch["xhalo_sb"][0:HR, 0:D], scalar1=cols[0:HR, 3:4],
          scalar2=cols[0:HR, 4:5], op0=ALU.mult, op1=ALU.add, sem=sV)
    pb.wait("tensor", sV)
    for j2 in range(ND):
        pb.op("tensor", "transpose", ps_tr[:, 0:31],
              xlnb[0:HR, 128 * j2:128 * (j2 + 1)], ident[0:HR, 0:HR], sem=sT)
        pb.wait("vector", sT)
        pb.op("vector", "tensor_copy", out=x5h[:, 32 * j2:32 * j2 + 31],
              in_=ps_tr[:, 0:31], sem=sV)
        pb.wait("tensor", sV)

    def build_x5ch(ck):
        for sub in range(4):
            i2 = 4 * ck + sub
            pb.wait("vector", sT)
            emit_ln_apply(pb, xres[:, D * i2:D * (i2 + 1)], xlnb[:, 0:D],
                          scratch["rstd"][:, i2:i2 + 1],
                          scratch["nbias"][:, i2:i2 + 1], sem=sV)
            pb.wait("tensor", sV)
            for g in range(2):
                pb.wait("tensor", sV)
                for jj in range(4):
                    j3 = 4 * g + jj
                    pb.op("tensor", "transpose", ps_tr[:, 128 * jj:128 * (jj + 1)],
                          xlnb[:, 128 * j3:128 * (j3 + 1)], ident[:],
                          sem=sT if jj == 3 else None)
                pb.wait("vector", sT)
                pb.op("vector", "tensor_copy",
                      out=x5ch.rearrange("p (j t) -> p j t", t=512)
                      [:, 4 * g:4 * g + 4, 128 * sub:128 * (sub + 1)],
                      in_=ps_tr[:].rearrange("p (j t) -> p j t", t=128), sem=sV)
        pb.wait("tensor", sV)

    rows = [scratch["dram"]["encq"][128 * j2:128 * (j2 + 1), :] for j2 in range(ND)]
    dsts = [wsh[:, 1024 * j2:1024 * (j2 + 1)] for j2 in range(ND)]
    attna1_load_wshare(pb, rows, aw, scratch["stage"], dma_sem, sems, dsts)
    pb.wait("tensor", sems["sV"])
    for h in range(8):
        pairs = [(wsh[:, 1024 * j2 + 128 * h:1024 * j2 + 128 * (h + 1)],
                  x5h[:, 32 * j2:32 * j2 + 31]) for j2 in range(ND)]
        emit_mm(pb, ps_a[:, 0:31], pairs, sem=sT)
        pb.wait("vector", sT)
        pb.op("vector", "tensor_scalar", out=zf[:, 0:31], in0=ps_a[:, 0:31],
              scalar1=0.0, scalar2=None, op0=ALU.max)
        pb.op("vector", "tensor_copy", out=qTh[:, U5 * h:U5 * h + 31],
              in_=zf[:, 0:31])
        pb.op("vector", "scalar_tensor_tensor", out=qTl[:, U5 * h:U5 * h + 31],
              in0=qTh[:, U5 * h:U5 * h + 31], scalar=-1.0, in1=zf[:, 0:31],
              op0=ALU.mult, op1=ALU.add, sem=sV)
        pb.wait("tensor", sV)
    for h in range(8):
        pb.op("vector", "memset", qTh[:, U5 * h + 2079:U5 * h + 2080], 0.0)
        pb.op("vector", "memset", qTl[:, U5 * h + 2079:U5 * h + 2080], 0.0,
              sem=sV if h == 7 else None)
    for ck in range(4):
        build_x5ch(ck)
        u0 = 31 + 512 * ck
        for h in range(8):
            pairs = [(wsh[:, 1024 * j2 + 128 * h:1024 * j2 + 128 * (h + 1)],
                      x5ch[:, 512 * j2:512 * (j2 + 1)]) for j2 in range(ND)]
            emit_mm(pb, ps_a[:, 0:512], pairs, sem=sT)
            pb.wait("vector", sT)
            pb.op("vector", "tensor_scalar", out=scratch["rq"][:, 0:512],
                  in0=ps_a[:, 0:512], scalar1=0.0, scalar2=None, op0=ALU.max)
            pb.op("vector", "tensor_copy",
                  out=qTh[:, U5 * h + u0:U5 * h + u0 + 512],
                  in_=scratch["rq"][:, 0:512])
            pb.op("vector", "scalar_tensor_tensor",
                  out=qTl[:, U5 * h + u0:U5 * h + u0 + 512],
                  in0=qTh[:, U5 * h + u0:U5 * h + u0 + 512], scalar=-1.0,
                  in1=scratch["rq"][:, 0:512], op0=ALU.mult, op1=ALU.add, sem=sV)
            pb.wait("tensor", sV)

    pb.op("vector", "memset", vu[:], 0.0, sem=sV)
    pb.wait("sync", sV, pb.cnt[sV])
    rows = [scratch["dram"]["encv"][128 * j2:128 * (j2 + 1), :] for j2 in range(ND)]
    attna1_load_wshare(pb, rows, aw, scratch["stage"], dma_sem, sems, dsts)
    pb.wait("tensor", sems["sV"])
    for dh in range(2):
        pairs = [(x5h[:, 32 * j2:32 * j2 + 31],
                  wsh[:, 1024 * j2 + 512 * dh:1024 * j2 + 512 * (dh + 1)])
                 for j2 in range(ND)]
        emit_mm(pb, ps_b[0:31, 0:512] if dh == 0 else ps_c[0:31, 0:512], pairs,
                sem=sT if dh == 1 else None)
    pb.wait("vector", sT)
    pb.op("vector", "tensor_scalar", out=vtmp[0:31, 0:512], in0=ps_b[0:31, 0:512],
          scalar1=0.0, scalar2=None, op0=ALU.max)
    pb.op("vector", "tensor_scalar", out=vtmp[0:31, 512:1024],
          in0=ps_c[0:31, 0:512], scalar1=0.0, scalar2=None, op0=ALU.max, sem=sV)
    pb.wait("sync", sV)
    pb.op("sync", "dma_start", out=vu[0:31, 0:1024], in_=vtmp[0:31, 0:1024],
          sem=dma_sem, inc=16)
    pb.wait("tensor", dma_sem)
    for ck in range(4):
        build_x5ch(ck)
        for sub in range(4):
            i2 = 4 * ck + sub
            for dh in range(2):
                pairs = [(x5ch[:, 512 * j2 + 128 * sub:512 * j2 + 128 * (sub + 1)],
                          wsh[:, 1024 * j2 + 512 * dh:1024 * j2 + 512 * (dh + 1)])
                         for j2 in range(ND)]
                emit_mm(pb, ps_b[:, 0:512] if dh == 0 else ps_c[:, 0:512], pairs,
                        sem=sT if dh == 1 else None)
            pb.wait("vector", sT)
            pb.op("vector", "tensor_scalar", out=vtmp[:, 0:512],
                  in0=ps_b[:, 0:512], scalar1=0.0, scalar2=None, op0=ALU.max)
            pb.op("vector", "tensor_scalar", out=vtmp[:, 512:1024],
                  in0=ps_c[:, 0:512], scalar1=0.0, scalar2=None, op0=ALU.max,
                  sem=sV)
            pb.wait("tensor", sV)
            pb.wait("sync", sV, pb.cnt[sV])
            pb.wait("sync", dma_sem, pb.cnt[dma_sem])
            pb.op("sync", "dma_start", out=vu[31:128, 1024 * i2:1024 * (i2 + 1)],
                  in_=vtmp[0:97, 0:1024], sem=dma_sem, inc=16)
            pb.wait("sync", dma_sem, pb.cnt[dma_sem])
            pb.op("sync", "dma_start",
                  out=vu[0:31, 1024 * (i2 + 1):1024 * (i2 + 1) + 1024],
                  in_=vtmp[97:128, 0:1024], sem=dma_sem, inc=16)
            pb.wait("vector", dma_sem)
    pb.wait("vector", dma_sem)
    pb.wait("tensor", dma_sem)

    for h in range(8):
        for ck in range(4):
            u0 = 31 + 512 * ck
            pb.op("vector", "tensor_tensor", out=scratch["rq"][:, 0:512],
                  in0=qTh[:, U5 * h + u0:U5 * h + u0 + 512],
                  in1=qTl[:, U5 * h + u0:U5 * h + u0 + 512], op=ALU.add)
            pb.op("vector", "scalar_tensor_tensor", out=scratch["rq"][:, 0:512],
                  in0=scratch["rq"][:, 0:512], scalar=1.0,
                  in1=aw["wvbc"][:, 512 * ck:512 * (ck + 1)],
                  op0=ALU.bypass, op1=ALU.mult,
                  accum_out=sig[:, 8 + (ck % 2):9 + (ck % 2)])
            if ck == 0:
                pb.op("vector", "tensor_copy", out=sig[:, h:h + 1],
                      in_=sig[:, 8:9])
            elif ck == 1:
                pb.op("vector", "tensor_tensor", out=sig[:, h:h + 1],
                      in0=sig[:, h:h + 1], in1=sig[:, 9:10], op=ALU.add)
            else:
                pb.op("vector", "tensor_tensor", out=sig[:, h:h + 1],
                      in0=sig[:, h:h + 1], in1=sig[:, 8 + (ck % 2):9 + (ck % 2)],
                      op=ALU.add, sem=sV if (h == 7 and ck == 3) else None)
    pb.wait("vector", dma_sem)
    pb.op("vector", "tensor_tensor", out=sig[:, 0:8], in0=sig[:, 0:8],
          in1=scratch["sigpeer_sb"][:, 0:8], op=ALU.add, sem=sV)

    rows = ([scratch["dram"]["mw1"][128 * j2:128 * (j2 + 1), :] for j2 in range(8)]
            + [scratch["dram"]["mw2"][128 * k:128 * (k + 1), :] for k in range(4)])
    dsts = ([wsh[:, 512 * j2:512 * (j2 + 1)] for j2 in range(8)]
            + [wsh[:, 4096 + 1024 * k:4096 + 1024 * (k + 1)] for k in range(4)])
    attna1_load_wshare(pb, rows, aw, scratch["stage"], dma_sem, sems, dsts)
    mw = wsh
    pb.op("vector", "tensor_copy", out=scratch["sigb"][:, 0:8], in_=sig[:, 0:8],
          sem=sV)
    pb.wait("tensor", sV)
    for mt in range(4):
        pairs = [(mw[:, 512 * j + 128 * mt:512 * j + 128 * (mt + 1)],
                  scratch["sigb"][:, j:j + 1]) for j in range(ND)]
        emit_mm(pb, ps_row[0:128, mt:mt + 1], pairs,
                sem=sT if mt == 3 else None)
    pb.wait("scalar", sT)
    for mt in range(4):
        pb.op("scalar", "activation", out=cols[:, 8 + mt:9 + mt],
              in_=ps_row[0:128, mt:mt + 1], func=AF.Gelu,
              bias=aw["mb1c"][:, mt:mt + 1], sem=sS if mt == 3 else None)
    pb.wait("vector", sS)
    pb.op("vector", "tensor_copy", out=scratch["sigb"][:, 8:12],
          in_=cols[:, 8:12], sem=sV)
    pb.wait("tensor", sV)
    for et in range(8):
        pairs = [(mw[:, 4096 + 1024 * k + 128 * et:
                     4096 + 1024 * k + 128 * (et + 1)],
                  scratch["sigb"][:, 8 + k:9 + k]) for k in range(4)]
        emit_mm(pb, ps_row[0:128, 4 + et:5 + et], pairs,
                sem=sT if et == 7 else None)
    pb.wait("vector", sT)
    pb.op("vector", "scalar_tensor_tensor", out=aw["mb2base"][:, 0:8],
          in0=aw["mb2c"][:, 0:8], scalar=0.1, in1=aw["basec"][:, 0:8],
          op0=ALU.mult, op1=ALU.add)
    pb.op("vector", "scalar_tensor_tensor", out=mdsq[:, 0:8],
          in0=ps_row[0:128, 4:12], scalar=0.1, in1=aw["mb2base"][:, 0:8],
          op0=ALU.mult, op1=ALU.add, sem=sV)
    pb.wait("scalar", sV)
    pb.op("scalar", "activation", out=cols[:, 8:16], in_=mdsq[:, 0:8], func=AF.Abs)
    pb.op("scalar", "activation", out=cols[:, 8:16], in_=cols[:, 8:16],
          func=AF.Exp, scale=-1.0)
    pb.op("scalar", "activation", out=cols[:, 8:16], in_=cols[:, 8:16],
          func=AF.Ln, bias=1.0, sem=sS)
    pb.wait("vector", sS)
    pb.op("vector", "scalar_tensor_tensor", out=mdsq[:, 0:8], in0=mdsq[:, 0:8],
          scalar=0.0, in1=cols[:, 8:16], op0=ALU.max, op1=ALU.add)
    pb.op("vector", "tensor_scalar", out=mdsq[:, 0:8], in0=mdsq[:, 0:8],
          scalar1=1e-6, scalar2=None, op0=ALU.add, sem=sV)
    pb.wait("scalar", sV)
    pb.op("scalar", "activation", out=mdsq[:, 0:8], in_=mdsq[:, 0:8],
          func=AF.Ln)
    pb.op("scalar", "activation", out=mdsq[:, 0:8], in_=mdsq[:, 0:8],
          func=AF.Exp, scale=pb.cf(0.5), sem=sS)
    pb.wait("vector", sS)

    rows = [scratch["dram"]["decoder"][128 * j2:128 * (j2 + 1), :] for j2 in range(ND)]
    dsts = [wsh[:, 1024 * j2:1024 * (j2 + 1)] for j2 in range(ND)]
    attna1_load_wshare(pb, rows, aw, scratch["stage"], dma_sem, sems, dsts)
    pb.wait("tensor", sems["sV"])
    for i in range(NT):
        w0 = 128 * i
        for h in range(8):
            q0 = U5 * h + w0
            pb.op("vector", "tensor_tensor", out=srw[:], in0=qTh[:, q0:q0 + 160],
                  in1=qTl[:, q0:q0 + 160], op=ALU.add)
            pb.op("vector", "tensor_scalar", out=zf[:], in0=srw[:],
                  scalar1=mdsq[:, h:h + 1], scalar2=None, op0=ALU.mult)
            pb.op("vector", "tensor_copy", out=zh[:], in_=zf[:])
            pb.op("vector", "scalar_tensor_tensor", out=zl[:], in0=zh[:],
                  scalar=-1.0, in1=zf[:], op0=ALU.mult, op1=ALU.add)
            pb.op("vector", "tensor_copy", out=sbf[:], in_=srw[:, 31:159],
                  sem=sV)
            pb.op("vector", "tensor_tensor", out=scr[:], in0=zh[:], in1=zh[:],
                  op=ALU.mult)
            pb.op("vector", "scalar_tensor_tensor", out=scr2[:], in0=zl[:],
                  scalar=2.0, in1=zh[:], op0=ALU.mult, op1=ALU.mult)
            pb.op("vector", "tensor_tensor", out=scr[:], in0=scr[:], in1=scr2[:],
                  op=ALU.add, sem=sV)
            pb.wait("tensor", sV)
            pb.op("tensor", "matmul", ps_b[:, 0:160], zh[:, 31:159], zh[:],
                  start=True, stop=False)
            pb.op("tensor", "matmul", ps_b[:, 0:160], zh[:, 31:159], zl[:],
                  start=False, stop=False)
            pb.op("tensor", "matmul", ps_b[:, 0:160], zl[:, 31:159], zh[:],
                  start=False, stop=True, sem=sT)
            pb.op("tensor", "matmul", ps_row[0:1, 0:160], aw["ones_col"][:, 0:1],
                  scr[:], start=True, stop=True)
            pb.op("vector", "tensor_tensor", out=scr2[:, 0:128],
                  in0=srw[:, 31:159], in1=srw[:, 31:159], op=ALU.mult, sem=sV)
            pb.wait("tensor", sV)
            pb.op("tensor", "matmul", ps_row[0:1, 160:288], aw["ones_col"][:, 0:1],
                  scr2[:, 0:128], start=True, stop=True, sem=sT)
            pb.wait("vector", sT)
            pb.op("vector", "tensor_copy", out=phirow[0:1, 0:160],
                  in_=ps_row[0:1, 0:160])
            pb.op("vector", "tensor_copy", out=scratch["s2row"][0:1, 0:128],
                  in_=ps_row[0:1, 160:288], sem=sV)
            pb.wait("tensor", sV)
            pb.op("tensor", "transpose", ps_tr[:, 0:1], phirow[0:1, 31:159],
                  ident[0:1, 0:1])
            pb.op("tensor", "transpose", ps_tr[:, 1:2], scratch["s2row"][0:1, 0:128],
                  ident[0:1, 0:1], sem=sT)
            pb.op("tensor", "matmul", ps_c[:, 0:160], aw["ones_row"][0:1, :],
                  phirow[0:1, 0:160], start=True, stop=True, sem=sT)
            pb.wait("vector", sT)
            pb.op("vector", "tensor_copy", out=cols[:, 8:10], in_=ps_tr[:, 0:2],
                  sem=sV)
            pb.wait("scalar", sV)
            pb.op("scalar", "activation", out=cols[:, 10:11], in_=cols[:, 9:10],
                  func=AF.Ln, bias=pb.cf(1e-30))
            pb.op("scalar", "activation", out=cols[:, 10:11], in_=cols[:, 10:11],
                  func=AF.Exp, scale=pb.cf(0.5), sem=sS)
            pb.wait("vector", sS)
            pb.op("vector", "tensor_scalar", out=cols[:, 10:11],
                  in0=cols[:, 10:11], scalar1=1e-12, scalar2=None, op0=ALU.max)
            pb.op("vector", "reciprocal", out=cols[:, 10:11], in_=cols[:, 10:11])
            pb.op("vector", "tensor_scalar", out=bandf[:], in0=ps_b[:, 0:160],
                  scalar1=-2.0, scalar2=cols[:, 8:9], op0=ALU.mult, op1=ALU.add)
            pb.op("vector", "tensor_tensor", out=bandf[:], in0=bandf[:],
                  in1=ps_c[:, 0:160], op=ALU.add)
            pb.op("vector", "tensor_scalar", out=bandf[:], in0=bandf[:],
                  scalar1=0.0, scalar2=1e-8, op0=ALU.max, op1=ALU.add, sem=sV)
            pb.wait("scalar", sV)
            pb.op("scalar", "activation", out=bandf[:], in_=bandf[:], func=AF.Ln)
            pb.op("scalar", "activation", out=bandf[:], in_=bandf[:],
                  func=AF.Exp, scale=pb.cf(0.5))
            pb.op("scalar", "activation", out=bandf[:], in_=bandf[:],
                  func=AF.Exp, scale=-1.0, sem=sS)
            pb.wait("vector", sS)
            pb.op("vector", "scalar_tensor_tensor", out=bandf[:], in0=bandf[:],
                  scalar=1.0, in1=aw["bandmask"][:], op0=ALU.bypass, op1=ALU.mult,
                  accum_out=cols[:, 11:12])
            pb.op("vector", "tensor_scalar", out=cols[:, 11:12],
                  in0=cols[:, 11:12], scalar1=1e-8, scalar2=None, op0=ALU.add)
            pb.op("vector", "reciprocal", out=cols[:, 12:13], in_=cols[:, 11:12])
            pb.op("vector", "tensor_scalar", out=bandb[:], in0=bandf[:],
                  scalar1=cols[:, 12:13], scalar2=None, op0=ALU.mult, sem=sV)
            pb.wait("tensor", sV)
            pb.op("tensor", "transpose", ps_trb[:, 0:128], bandb[:, 0:128],
                  identb[:])
            pb.op("tensor", "transpose", ps_trb[0:32, 128:256], bandb[:, 128:160],
                  identb[:])
            pb.op("tensor", "transpose", ps_trb[:, 256:384], sbf[:], identb[:],
                  sem=sT)
            pb.wait("vector", sT)
            pb.op("vector", "tensor_copy", out=bTa[:], in_=ps_trb[:, 0:128])
            pb.op("vector", "tensor_copy", out=bTb[0:32, 0:128],
                  in_=ps_trb[0:32, 128:256])
            pb.op("vector", "tensor_copy", out=qtm[:], in_=ps_trb[:, 256:384],
                  sem=sV)
            pb.wait("tensor", sV)
            pb.op("tensor", "matmul", ps_c[:, 0:128], bTa[:],
                  vu[:, 1024 * i + 128 * h:1024 * i + 128 * (h + 1)],
                  start=True, stop=False)
            pb.op("tensor", "matmul", ps_c[:, 0:128], bTb[0:32, 0:128],
                  vu[0:32, 1024 * (i + 1) + 128 * h:1024 * (i + 1) + 128 * (h + 1)],
                  start=False, stop=True)
            pb.op("tensor", "matmul", ps_c[:, 128:256], sbf[:],
                  hebbT[:, 128 * h:128 * (h + 1)], start=True, stop=True, sem=sT)
            pb.wait("vector", sT)
            pb.op("vector", "tensor_copy", out=cA[:], in_=ps_c[:, 0:128])
            pb.op("vector", "scalar_tensor_tensor", out=cA[:],
                  in0=ps_c[:, 128:256], scalar=cols[:, 10:11], in1=cA[:],
                  op0=ALU.mult, op1=ALU.add)
            pb.op("vector", "tensor_tensor", out=ub[:], in0=cA[:], in1=qtm[:],
                  op=ALU.mult, sem=sV)
            pb.wait("tensor", sV)
            pb.op("tensor", "transpose", ps_trb[:, 384:512], ub[:], identb[:],
                  sem=sT)
            pb.wait("vector", sT)
            pb.op("vector", "tensor_copy", out=uT[:, 128 * h:128 * (h + 1)],
                  in_=ps_trb[:, 384:512], sem=sV)
            pb.wait("tensor", sV)
        for dh in range(2):
            pairs = [(uT[:, 128 * h:128 * (h + 1)],
                      wsh[:, 1024 * h + 512 * dh:1024 * h + 512 * (dh + 1)])
                     for h in range(8)]
            emit_mm(pb, ps_dec[:, 512 * dh:512 * (dh + 1)], pairs,
                    sem=sT if dh == 1 else None)
        pb.wait("vector", sT)
        pb.op("vector", "tensor_tensor", out=xres[:, D * i:D * (i + 1)],
              in0=xres[:, D * i:D * (i + 1)], in1=ps_dec[:, 0:1024],
              op=ALU.add, sem=sV)
        pb.wait("tensor", sV)


def alloc_attna1(pb, tag=""):
    aw = {
        "wshare": pb.sbuf("a5wsh" + tag, [128, ND * 1024], BF16),
        "hebb_bf": pb.sbuf("a5hebb" + tag, [128, 8 * 128], BF16),
        "mb1c": pb.sbuf("a5mb1" + tag, [128, 4], F32),
        "mb2base": pb.sbuf("a5mb2" + tag, [128, 8], F32),
        "mb2c": pb.sbuf("a5mb2c" + tag, [128, 8], F32),
        "basec": pb.sbuf("a5base" + tag, [128, 8], F32),
        "wvbc": pb.sbuf("a5wv" + tag, [128, 2048], BF16),
        "bandmask": pb.sbuf("a5bm" + tag, [128, 160], F32),
        "ones_col": pb.sbuf("a5oc" + tag, [128, 1], F32),
        "ones_row": pb.sbuf("a5or" + tag, [1, 128], F32),
    }
    scratch = {
        "qTh": pb.sbuf("a5qTh" + tag, [128, 8 * U5], BF16),
        "qTl": pb.sbuf("a5qTl" + tag, [128, 8 * U5], F8E5),
        "vu": pb.sbuf("a5vu" + tag, [128, 17 * 1024], BF16),
        "x5ch": pb.sbuf("a5x5ch" + tag, [128, ND * 512], BF16),
        "x5h": pb.sbuf("a5x5h" + tag, [128, ND * 32], BF16),
        "xlnb": pb.sbuf("a5xlnb" + tag, [128, D], F32),
        "xhalo_sb": pb.sbuf("a5xhalo" + tag, [31, D], F32),
        "sigpeer_sb": pb.sbuf("a5sigp" + tag, [128, 8], F32),
        "ncols": pb.sbuf("a5cols" + tag, [128, 16], F32),
        "srw": pb.sbuf("a5srw" + tag, [128, 160], F32),
        "zf": pb.sbuf("a5zf" + tag, [128, 160], F32),
        "zh": pb.sbuf("a5zh" + tag, [128, 160], BF16),
        "zl": pb.sbuf("a5zl" + tag, [128, 160], BF16),
        "scr": pb.sbuf("a5scr" + tag, [128, 160], F32),
        "scr2": pb.sbuf("a5scr2" + tag, [128, 160], F32),
        "bandf": pb.sbuf("a5bandf" + tag, [128, 160], F32),
        "bandb": pb.sbuf("a5bandb" + tag, [128, 160], BF16),
        "bTa": pb.sbuf("a5bTa" + tag, [128, 128], BF16),
        "bTb": pb.sbuf("a5bTb" + tag, [32, 128], BF16),
        "phirow": pb.sbuf("a5phir" + tag, [1, 160], F32),
        "s2row": pb.sbuf("a5s2r" + tag, [1, 128], F32),
        "sbf": pb.sbuf("a5sbf" + tag, [128, 128], BF16),
        "qtm": pb.sbuf("a5qtm" + tag, [128, 128], BF16),
        "cA": pb.sbuf("a5cA" + tag, [128, 128], F32),
        "ub": pb.sbuf("a5ub" + tag, [128, 128], BF16),
        "uT": pb.sbuf("a5uT" + tag, [128, 8 * 128], BF16),
        "sig": pb.sbuf("a5sig" + tag, [128, 12], F32),
        "sigb": pb.sbuf("a5sigb" + tag, [128, 12], BF16),
        "mdsq": pb.sbuf("a5mdsq" + tag, [128, 8], F32),
        "vtmp": pb.sbuf("a5vtmp" + tag, [128, 1024], BF16),
        "rq": pb.sbuf("a5rq" + tag, [128, 512], F32),
        "stats": pb.sbuf("a5stats" + tag, [128, 4 * NT], F32),
        "rstd": pb.sbuf("a5rstd" + tag, [128, NT], F32),
        "nbias": pb.sbuf("a5nbias" + tag, [128, NT], F32),
    }
    return aw, scratch


def attna1_load_wshare(pb, dram_rows, aw, stage, dma_sem, sems, dsts):
    pb.wait("sync", sems["sT"], pb.cnt[sems["sT"]])
    emit_load_cast(pb, dram_rows, stage, dsts, dma_sem, sems)


def attna1_weight_prep(pb, dram, aw, stage, dma_sem, sems):
    rows = [dram["hebbc"][128 * h:128 * (h + 1), :] for h in range(8)]
    dsts = [aw["hebb_bf"][:, 128 * h:128 * (h + 1)] for h in range(8)]
    emit_load_cast(pb, rows, stage, dsts, dma_sem, sems)
    for nm, key in (("mb1c", "mb1c"), ("mb2c", "mb2c"), ("basec", "basec"),
                    ("wvbc", "wvbc"), ("bandmask", "bandmask"),
                    ("onesc", "ones_col"), ("onesr", "ones_row")):
        ap = aw[key][0:1, :] if key == "ones_row" else aw[key][:]
        pb.op("sync", "dma_start", out=ap, in_=dram[nm][:], sem=dma_sem, inc=16)


def build_attna1_program():
    nc = bass.Bass("TRN2", target_bir_lowering=False, debug=False)
    pb = PB(nc)
    dram = {
        "xres": nc.dram_tensor("xres", [TOK, D], F32, kind="ExternalInput").ap(),
        "xhalo": nc.dram_tensor("xhalo", [31, D], F32, kind="ExternalInput").ap(),
        "sigpeer": nc.dram_tensor("sigpeer", [128, 8], F32, kind="ExternalInput").ap(),
        "encq": nc.dram_tensor("encq", [D, 1024], F32, kind="ExternalInput").ap(),
        "encv": nc.dram_tensor("encv", [D, 1024], F32, kind="ExternalInput").ap(),
        "decoder": nc.dram_tensor("decoder", [1024, D], F32, kind="ExternalInput").ap(),
        "mw1": nc.dram_tensor("mw1", [1024, 512], F32, kind="ExternalInput").ap(),
        "mw2": nc.dram_tensor("mw2", [512, 1024], F32, kind="ExternalInput").ap(),
        "hebbc": nc.dram_tensor("hebbc", [1024, 128], F32, kind="ExternalInput").ap(),
        "mb1c": nc.dram_tensor("mb1c", [128, 4], F32, kind="ExternalInput").ap(),
        "mb2c": nc.dram_tensor("mb2c", [128, 8], F32, kind="ExternalInput").ap(),
        "basec": nc.dram_tensor("basec", [128, 8], F32, kind="ExternalInput").ap(),
        "wvbc": nc.dram_tensor("wvbc", [128, 2048], BF16, kind="ExternalInput").ap(),
        "bandmask": nc.dram_tensor("bandmask", [128, 160], F32, kind="ExternalInput").ap(),
        "onesc": nc.dram_tensor("onesc", [128, 1], F32, kind="ExternalInput").ap(),
        "onesr": nc.dram_tensor("onesr", [1, 128], F32, kind="ExternalInput").ap(),
        "ident": nc.dram_tensor("ident", [128, 128], F32, kind="ExternalInput").ap(),
        "identb": nc.dram_tensor("identb", [128, 128], BF16, kind="ExternalInput").ap(),
        "xout": nc.dram_tensor("xout", [TOK, D], F32, kind="ExternalOutput").ap(),
        "sig_out": nc.dram_tensor("sig_out", [128, 8], F32, kind="ExternalOutput").ap(),
    }
    aw, scratch = alloc_attna1(pb)
    scratch["ident"] = pb.sbuf("identsb", [128, 128], F32)
    scratch["identb"] = pb.sbuf("identbsb", [128, 128], BF16)
    scratch["ps_a"] = pb.psum("ps_a", [128, 512], F32)
    scratch["ps_b"] = pb.psum("ps_b", [128, 512], F32)
    scratch["ps_c"] = pb.psum("ps_c", [128, 512], F32)
    scratch["ps_dec"] = pb.psum("ps_dec", [128, 1024], F32)
    scratch["ps_tr"] = pb.psum("ps_tr", [128, 512], F32)
    scratch["ps_trb"] = pb.psum("ps_trb", [128, 512], BF16)
    scratch["ps_row"] = pb.psum("ps_row", [128, 512], F32)
    xres = pb.sbuf("xres_sb", [128, NT * D], F32)
    stage = pb.sbuf("stage", [128, 1024], F32)
    dma_sem = pb.sem("dma")
    sems = {"sV": pb.sem("sV"), "sS": pb.sem("sS"), "sT": pb.sem("sT"),
            "sG": pb.sem("sG")}
    st = pb.sem("st")

    pb.op("sync", "dma_start", out=xres.rearrange("p (i d) -> p i d", d=D),
          in_=dram["xres"].rearrange("(i p) d -> p i d", p=128),
          sem=dma_sem, inc=16)
    pb.op("sync", "dma_start", out=scratch["xhalo_sb"][:], in_=dram["xhalo"][:],
          sem=dma_sem, inc=16)
    pb.op("sync", "dma_start", out=scratch["sigpeer_sb"][:], in_=dram["sigpeer"][:],
          sem=dma_sem, inc=16)
    pb.op("sync", "dma_start", out=scratch["ident"][:], in_=dram["ident"][:],
          sem=dma_sem, inc=16)
    pb.op("sync", "dma_start", out=scratch["identb"][:], in_=dram["identb"][:],
          sem=dma_sem, inc=16)
    attna1_weight_prep(pb, dram, aw, stage, dma_sem, sems)
    pb.wait("vector", dma_sem)
    pb.wait("scalar", dma_sem)
    pb.wait("tensor", dma_sem)

    scratch["dram"] = dram
    scratch["stage"] = stage
    emit_attna1(pb, xres, aw, scratch, sems, dma_sem)

    pb.wait("sync", sems["sV"])
    pb.op("sync", "dma_start",
          out=dram["xout"].rearrange("(i p) d -> p i d", p=128),
          in_=xres.rearrange("p (i d) -> p i d", d=D), sem=st, inc=16)
    pb.op("sync", "dma_start", out=dram["sig_out"][:], in_=scratch["sig"][:, 0:8],
          sem=st, inc=16)
    pb.wait("sync", st)
    return pb.build()


def host_attna1_consts():
    wv = np.full((2048,), 1.0 / 4096, np.float32)
    wv[::8] += 0.5 / 512
    import ml_dtypes
    wvbc = np.broadcast_to(wv, (128, 2048)).astype(ml_dtypes.bfloat16)
    r = np.arange(128)[:, None]
    c = np.arange(160)[None, :]
    bandmask = (((c - r) >= 0) & ((c - r) <= 31)).astype(np.float32)
    return np.ascontiguousarray(wvbc), np.ascontiguousarray(bandmask)




CH1 = 512
CH2 = 256
TFULL = 4096


def emit_mamba_phase1(pb, dram, mw, scratch, sems, dma_sem):
    sV, sS, sT = sems["sV"], sems["sS"], sems["sT"]
    ident = scratch["ident"]
    ps_q, ps_xd, ps_tr = scratch["ps_q"], scratch["ps_xd"], scratch["ps_tr"]
    xch, xlnb, xlnT = scratch["xch"], scratch["xlnb"], scratch["xlnT"]
    xir = scratch["xi_roll"]
    xc = scratch["xc"]
    xdP = scratch["xdT_f32"]
    stats = scratch["stats"]
    rstd, nbias = scratch["rstd"], scratch["nbias"]
    W = 3 + CH1
    for j in range(ND):
        pb.op("vector", "memset", xir[:, W * j:W * j + 3], 0.0,
              sem=sV if j == ND - 1 else None)
    for ck in range(TFULL // CH1):
        for s4 in range(4):
            i = 4 * ck + s4
            if s4 % 2 == 0:
                pb.wait("sync", sT, pb.cnt[sT])
                pb.wait("sync", sV, pb.cnt[sV])
                pb.dma(xch.rearrange("p (i d) -> p i d", d=D),
                       dram["xfull"].rearrange("(i p) d -> p i d", p=128)
                       [:, i:i + 2, :], dma_sem)
                pb.wait("vector", dma_sem)
            sx = s4 % 2
            for hh in range(2):
                pb.op("vector", "reduce_sum", out=stats[:, 0:1] if hh == 0 else stats[:, 1:2],
                      in_=xch[:, D * sx + 512 * hh:D * sx + 512 * (hh + 1)],
                      axis=AX.X)
            pb.op("vector", "tensor_tensor", out=stats[:, 0:1], in0=stats[:, 0:1],
                  in1=stats[:, 1:2], op=ALU.add, sem=sV)
            pb.wait("scalar", sV)
            pb.op("scalar", "activation", out=xlnb[:, 0:D],
                  in_=xch[:, D * sx:D * (sx + 1)], func=AF.Square,
                  accum_out=stats[:, 2:3], sem=sS)
            pb.wait("vector", sS)
            pb.op("vector", "tensor_scalar", out=stats[:, 3:4], in0=stats[:, 0:1],
                  scalar1=1.0 / D, scalar2=None, op0=ALU.mult)
            pb.op("vector", "tensor_tensor", out=stats[:, 4:5], in0=stats[:, 3:4],
                  in1=stats[:, 3:4], op=ALU.mult)
            pb.op("vector", "scalar_tensor_tensor", out=stats[:, 4:5],
                  in0=stats[:, 2:3], scalar=1.0 / D, in1=stats[:, 4:5],
                  op0=ALU.mult, op1=ALU.subtract, sem=sV)
            pb.wait("scalar", sV)
            pb.op("scalar", "activation", out=rstd[:, i:i + 1], in_=stats[:, 4:5],
                  func=AF.Ln, bias=pb.cf(1e-5))
            pb.op("scalar", "activation", out=rstd[:, i:i + 1], in_=rstd[:, i:i + 1],
                  func=AF.Exp, scale=pb.cf(-0.5), sem=sS)
            pb.wait("vector", sS)
            pb.op("vector", "scalar_tensor_tensor", out=nbias[:, i:i + 1],
                  in0=stats[:, 3:4], scalar=-1.0, in1=rstd[:, i:i + 1],
                  op0=ALU.mult, op1=ALU.mult, sem=sV)
            emit_ln_apply(pb, xch[:, D * sx:D * (sx + 1)], xlnb[:, 0:D],
                          rstd[:, i:i + 1], nbias[:, i:i + 1], sem=sV)
            pb.wait("tensor", sV)
            for g in range(2):
                pb.wait("tensor", sV)
                for jj in range(4):
                    j = 4 * g + jj
                    pb.op("tensor", "transpose", ps_tr[:, 128 * jj:128 * (jj + 1)],
                          xlnb[:, 128 * j:128 * (j + 1)], ident[:],
                          sem=sT if jj == 3 else None)
                pb.wait("vector", sT)
                pb.op("vector", "tensor_copy",
                      out=xlnT.rearrange("p (j t) -> p j t", t=CH1)
                      [:, 4 * g:4 * g + 4, 128 * s4:128 * (s4 + 1)],
                      in_=ps_tr[:].rearrange("p (j t) -> p j t", t=128), sem=sV)
        pb.wait("tensor", sV)
        for j in range(ND):
            pairs = [(mw["inprojT_xi"][:, 1024 * k + 128 * j:1024 * k + 128 * (j + 1)],
                      xlnT[:, CH1 * k:CH1 * (k + 1)]) for k in range(ND)]
            emit_mm(pb, ps_q[:, 0:CH1], pairs, sem=sT)
            pb.wait("vector", sT)
            pb.op("vector", "tensor_copy", out=xir[:, W * j + 3:W * j + 3 + CH1],
                  in_=ps_q[:, 0:CH1], sem=sV)
            pb.wait("tensor", sV)
            pb.op("vector", "tensor_scalar", out=scratch["xlnb"][:, 0:CH1],
                  in0=xir[:, W * j + 0:W * j + CH1],
                  scalar1=mw["cwc"][:, 4 * j:4 * j + 1], scalar2=None, op0=ALU.mult)
            for k in range(1, 4):
                pb.op("vector", "scalar_tensor_tensor", out=scratch["xlnb"][:, 0:CH1],
                      in0=xir[:, W * j + k:W * j + k + CH1],
                      scalar=mw["cwc"][:, 4 * j + k:4 * j + k + 1],
                      in1=scratch["xlnb"][:, 0:CH1], op0=ALU.mult, op1=ALU.add,
                      sem=sV if k == 3 else None)
            pb.wait("scalar", sV)
            pb.op("scalar", "activation", out=xc[:, TFULL * j + CH1 * ck:TFULL * j + CH1 * (ck + 1)],
                  in_=scratch["xlnb"][:, 0:CH1], func=AF.Silu,
                  bias=mw["cbc"][:, j:j + 1], sem=sS)
            pb.wait("vector", sS)
            pb.op("vector", "tensor_copy", out=xir[:, W * j:W * j + 3],
                  in_=xir[:, W * j + CH1:W * j + CH1 + 3], sem=sV)
        pb.wait("tensor", sS)
        pairs = [(mw["xprojT"][:, 96 * j:96 * (j + 1)],
                  xc[:, TFULL * j + CH1 * ck:TFULL * j + CH1 * (ck + 1)])
                 for j in range(ND)]
        emit_mm(pb, ps_xd[0:96, 0:CH1], pairs, sem=sT)
        pb.wait("vector", sT)
        pb.op("vector", "tensor_copy", out=xdP[0:96, CH1 * ck:CH1 * (ck + 1)],
              in_=ps_xd[0:96, 0:CH1], sem=sV)
        pb.wait("tensor", sV)


def emit_mamba_phase2(pb, dram, mw, scratch, sems, dma_sem):
    sV, sS, sT = sems["sV"], sems["sS"], sems["sT"]
    ident = scratch["ident"]
    ps_q, ps_xd, ps_tr = scratch["ps_q"], scratch["ps_xd"], scratch["ps_tr"]
    ps_bc, ps_o = scratch["ps_bc"], scratch["ps_o"]
    xch, xlnb, xlnT = scratch["xch"], scratch["xlnb"], scratch["xlnT"]
    xc = scratch["xc"]
    xdP = scratch["xdT_f32"]
    xdb = scratch["xdT_bf"]
    rstd, nbias = scratch["rstd"], scratch["nbias"]
    dtT, sz = scratch["dtT"], scratch["sz"]
    wch, ybuf, scr = scratch["wch"], scratch["ybuf"], scratch["scr2"]
    dA, hbuf = scratch["dA"], scratch["hbuf"]
    gbuf = scratch["gbuf"]
    hcarry = scratch["hcarry"]
    dbx = scratch["dbx"]
    Acols, Dpc = scratch["Acols"], mw["Dpc"]

    for hh in range(2):
        pb.wait("sync", sV, pb.cnt[sV])
        pb.dma(xch[0:96, 0:2048], dram["xdbl_peer"][0:96, 2048 * hh:2048 * (hh + 1)],
               dma_sem)
        pb.wait("vector", dma_sem)
        pb.op("vector", "tensor_tensor", out=xdP[0:96, 2048 * hh:2048 * (hh + 1)],
              in0=xdP[0:96, 2048 * hh:2048 * (hh + 1)],
              in1=xch[0:96, 0:2048], op=ALU.add, sem=sV)
    pb.op("vector", "tensor_copy", out=xdb[0:96, :], in_=xdP[0:96, :], sem=sV)
    pb.wait("scalar", sV)
    pb.op("scalar", "activation", out=Acols[:], in_=mw["alogc"][:], func=AF.Exp,
          sem=sS)
    pb.wait("vector", sS)
    pb.op("vector", "tensor_scalar", out=Acols[:], in0=Acols[:], scalar1=-1.0,
          scalar2=None, op0=ALU.mult, sem=sV)
    pb.wait("scalar", sV)
    pb.wait("tensor", sV)

    NC2 = TFULL // CH2
    for ck in range(NC2):
        pb.wait("sync", sT, pb.cnt[sT])
        pb.wait("sync", sV, pb.cnt[sV])
        pb.dma(xch.rearrange("p (i d) -> p i d", d=D)[:, 0:2, :],
               dram["xfull"].rearrange("(i p) d -> p i d", p=128)
               [:, 2 * ck:2 * (ck + 1), :], dma_sem)
        pb.wait("vector", dma_sem)
        for s2 in range(2):
            i = 2 * ck + s2
            emit_ln_apply(pb, xch[:, D * s2:D * (s2 + 1)], xlnb[:, 0:D],
                          rstd[:, i:i + 1], nbias[:, i:i + 1], sem=sV)
            pb.wait("tensor", sV)
            for g in range(2):
                pb.wait("tensor", sV)
                for jj in range(4):
                    j = 4 * g + jj
                    pb.op("tensor", "transpose", ps_tr[:, 128 * jj:128 * (jj + 1)],
                          xlnb[:, 128 * j:128 * (j + 1)], ident[:],
                          sem=sT if jj == 3 else None)
                pb.wait("vector", sT)
                pb.op("vector", "tensor_copy",
                      out=xlnT.rearrange("p (j t) -> p j t", t=CH1)
                      [:, 4 * g:4 * g + 4, 128 * s2:128 * (s2 + 1)],
                      in_=ps_tr[:].rearrange("p (j t) -> p j t", t=128), sem=sV)
        pb.wait("tensor", sV)
        t0 = CH2 * ck
        for j in range(ND):
            pairs = [(mw["inprojT_z"][:, 1024 * k + 128 * j:1024 * k + 128 * (j + 1)],
                      xlnT[:, CH1 * k:CH1 * k + CH2]) for k in range(ND)]
            emit_mm(pb, ps_q[:, 0:CH2], pairs, sem=sT)
            pb.wait("scalar", sT)
            pb.op("scalar", "activation", out=sz[:, 0:CH2], in_=ps_q[:, 0:CH2],
                  func=AF.Silu, sem=sS)
            pb.wait("tensor", sS)
            pb.op("tensor", "matmul", ps_q[:, 256:256 + CH2],
                  mw["dtPT"][0:64, 128 * j:128 * (j + 1)],
                  xdb[0:64, t0:t0 + CH2], start=True, stop=True, sem=sT)
            pb.wait("scalar", sT)
            pb.op("scalar", "activation", out=dtT[:, 0:CH2],
                  in_=ps_q[:, 256:256 + CH2], func=AF.Abs,
                  bias=mw["dtbc"][:, j:j + 1])
            pb.op("scalar", "activation", out=dtT[:, 0:CH2], in_=dtT[:, 0:CH2],
                  func=AF.Exp, scale=-1.0)
            pb.op("scalar", "activation", out=dtT[:, 0:CH2], in_=dtT[:, 0:CH2],
                  func=AF.Ln, bias=1.0, sem=sS)
            pb.wait("vector", sS)
            pb.op("vector", "tensor_scalar", out=scr[:, 0:CH2],
                  in0=ps_q[:, 256:256 + CH2], scalar1=mw["dtbc"][:, j:j + 1],
                  scalar2=0.0, op0=ALU.add, op1=ALU.max)
            pb.op("vector", "tensor_tensor", out=dtT[:, 0:CH2], in0=dtT[:, 0:CH2],
                  in1=scr[:, 0:CH2], op=ALU.add)
            pb.op("vector", "tensor_tensor", out=wch[:, 0:CH2], in0=dtT[:, 0:CH2],
                  in1=xc[:, TFULL * j + t0:TFULL * j + t0 + CH2], op=ALU.mult,
                  sem=sV)
            pb.wait("scalar", sV)
            pb.wait("tensor", sV)
            for s in range(16):
                pb.wait("scalar", sV)
                pb.op("scalar", "activation", out=dA[:, 0:CH2],
                      in_=dtT[:, 0:CH2], func=AF.Exp,
                      scale=Acols[:, 16 * j + s:16 * j + s + 1], sem=sS)
                pb.wait("vector", sS)
                pb.op("tensor", "matmul", ps_bc[:, 0:CH2],
                      mw["oneh"][0:96, 128 * s:128 * (s + 1)],
                      xdb[0:96, t0:t0 + CH2], start=True, stop=True, sem=sT)
                pb.wait("vector", sT)
                pb.op("vector", "tensor_tensor", out=dbx[:, 0:CH2],
                      in0=wch[:, 0:CH2], in1=ps_bc[:, 0:CH2], op=ALU.mult,
                      sem=sV)
                ini = 0.0 if ck == 0 else hcarry[:, 16 * j + s:16 * j + s + 1]
                pb.op("vector", "tensor_tensor_scan",
                      out=hbuf[:, CH2 * s:CH2 * (s + 1)],
                      data0=dA[:, 0:CH2], data1=dbx[:, 0:CH2],
                      initial=ini, op0=ALU.mult, op1=ALU.add, sem=sV)
                pb.wait("tensor", sV, pb.cnt[sV] - 1)
                pb.op("tensor", "matmul", ps_bc[:, 256:256 + CH2],
                      mw["oneh"][0:96, 2048 + 128 * s:2048 + 128 * (s + 1)],
                      xdb[0:96, t0:t0 + CH2], start=True, stop=True, sem=sT)
                pb.wait("vector", sT)
                if s == 0:
                    pb.op("vector", "tensor_tensor", out=ybuf[:, 0:CH2],
                          in0=hbuf[:, CH2 * s:CH2 * (s + 1)],
                          in1=ps_bc[:, 256:256 + CH2], op=ALU.mult, sem=sV)
                else:
                    pb.op("vector", "tensor_tensor", out=scr[:, 0:CH2],
                          in0=hbuf[:, CH2 * s:CH2 * (s + 1)],
                          in1=ps_bc[:, 256:256 + CH2], op=ALU.mult)
                    pb.op("vector", "tensor_tensor", out=ybuf[:, 0:CH2],
                          in0=ybuf[:, 0:CH2], in1=scr[:, 0:CH2], op=ALU.add,
                          sem=sV)
                pb.wait("tensor", sV)
            pb.op("vector", "tensor_copy",
                  out=hcarry[:, 16 * j:16 * (j + 1)].rearrange("p (s o) -> p s o", o=1),
                  in_=hbuf.rearrange("p (s t) -> p s t", t=CH2)[:, :, CH2 - 1:CH2],
                  sem=sV)
            pb.op("vector", "scalar_tensor_tensor", out=scr[:, 0:CH2],
                  in0=xc[:, TFULL * j + t0:TFULL * j + t0 + CH2],
                  scalar=Dpc[:, j:j + 1], in1=ybuf[:, 0:CH2],
                  op0=ALU.mult, op1=ALU.add)
            pb.op("vector", "tensor_tensor", out=gbuf[:, CH2 * j:CH2 * (j + 1)],
                  in0=scr[:, 0:CH2], in1=sz[:, 0:CH2], op=ALU.mult, sem=sV)
            pb.wait("tensor", sV)
        for sub in range(2):
            for dh in range(2):
                pairs = [(gbuf[:, CH2 * j + 128 * sub:CH2 * j + 128 * (sub + 1)],
                          mw["outprojT"][:, 1024 * j + 512 * dh:1024 * j + 512 * (dh + 1)])
                         for j in range(ND)]
                emit_mm(pb, ps_o[:, 512 * dh:512 * (dh + 1)], pairs,
                        sem=sT if dh == 1 else None)
            pb.wait("vector", sT)
            pb.op("vector", "tensor_copy", out=scratch["xlnb"][:, 0:1024],
                  in_=ps_o[:, 0:1024], sem=sV)
            pb.wait("tensor", sV)
            pb.wait("sync", sV, pb.cnt[sV])
            pb.dma(dram["out_partial"].rearrange("(i p) d -> p i d", p=128)
                   [:, 2 * ck + sub:2 * ck + sub + 1, :],
                   scratch["xlnb"][:, 0:1024].rearrange("p (i d) -> p i d", d=D),
                   dma_sem)
            pb.wait("vector", dma_sem)
    pb.wait("sync", dma_sem, pb.cnt[dma_sem])


def alloc_mamba(pb, tag=""):
    mw = {
        "inprojT_xi": pb.sbuf("m1ipx" + tag, [128, ND * 1024], BF16),
        "inprojT_z": pb.sbuf("m1ipz" + tag, [128, ND * 1024], BF16),
        "outprojT": pb.sbuf("m1op" + tag, [128, ND * 1024], BF16),
        "xprojT": pb.sbuf("m1xp" + tag, [128, ND * 96], BF16),
        "dtPT": pb.sbuf("m1dtp" + tag, [64, ND * 128], BF16),
        "oneh": pb.sbuf("m1oneh" + tag, [96, 2 * 16 * 128], BF16),
        "cwc": pb.sbuf("m1cw" + tag, [128, 4 * ND], F32),
        "cbc": pb.sbuf("m1cb" + tag, [128, ND], F32),
        "dtbc": pb.sbuf("m1dtb" + tag, [128, ND], F32),
        "alogc": pb.sbuf("m1alog" + tag, [128, 16 * ND], F32),
        "Dpc": pb.sbuf("m1dp" + tag, [128, ND], F32),
    }
    scratch = {
        "xch": pb.sbuf("m1xch" + tag, [128, 2 * D], F32),
        "xlnb": pb.sbuf("m1xlnb" + tag, [128, D], F32),
        "xlnT": pb.sbuf("m1xlnT" + tag, [128, ND * CH1], BF16),
        "xi_roll": pb.sbuf("m1xir" + tag, [128, ND * (3 + CH1)], BF16),
        "xc": pb.sbuf("m1xc" + tag, [128, ND * TFULL], BF16),
        "xdT_f32": pb.sbuf("m1xdp" + tag, [96, TFULL], F32),
        "xdT_bf": pb.sbuf("m1xdb" + tag, [96, TFULL], BF16),
        "stats": pb.sbuf("m1stats" + tag, [128, 8], F32),
        "rstd": pb.sbuf("m1rstd" + tag, [128, 32], F32),
        "nbias": pb.sbuf("m1nbias" + tag, [128, 32], F32),
        "dtT": pb.sbuf("m1dtT" + tag, [128, CH2], F32),
        "sz": pb.sbuf("m1sz" + tag, [128, CH2], F32),
        "wch": pb.sbuf("m1wch" + tag, [128, CH2], F32),
        "ybuf": pb.sbuf("m1y" + tag, [128, CH2], F32),
        "scr2": pb.sbuf("m1scr" + tag, [128, CH2], F32),
        "dbx": pb.sbuf("m1dbx" + tag, [128, CH2], BF16),
        "dA": pb.sbuf("m1dA" + tag, [128, CH2], F32),
        "hbuf": pb.sbuf("m1h" + tag, [128, 16 * CH2], BF16),
        "gbuf": pb.sbuf("m1g" + tag, [128, ND * CH2], BF16),
        "hcarry": pb.sbuf("m1hc" + tag, [128, 16 * ND], F32),
        "Acols": pb.sbuf("m1acol" + tag, [128, 16 * ND], F32),
    }
    return mw, scratch


def mamba_weight_prep(pb, dram, mw, stage, dma_sem, sems):
    rows = [dram["inprojT_xi"][128 * k:128 * (k + 1), 512 * c:512 * (c + 1)]
            for k in range(ND) for c in range(2)]
    dsts = [mw["inprojT_xi"][:, 1024 * k + 512 * c:1024 * k + 512 * (c + 1)]
            for k in range(ND) for c in range(2)]
    emit_load_cast(pb, rows, stage, dsts, dma_sem, sems)
    rows = [dram["inprojT_z"][128 * k:128 * (k + 1), 512 * c:512 * (c + 1)]
            for k in range(ND) for c in range(2)]
    dsts = [mw["inprojT_z"][:, 1024 * k + 512 * c:1024 * k + 512 * (c + 1)]
            for k in range(ND) for c in range(2)]
    emit_load_cast(pb, rows, stage, dsts, dma_sem, sems)
    rows = [dram["outprojT"][128 * j:128 * (j + 1), 512 * c:512 * (c + 1)]
            for j in range(ND) for c in range(2)]
    dsts = [mw["outprojT"][:, 1024 * j + 512 * c:1024 * j + 512 * (c + 1)]
            for j in range(ND) for c in range(2)]
    emit_load_cast(pb, rows, stage, dsts, dma_sem, sems)
    rows = [dram["xprojT"][128 * j:128 * (j + 1), :] for j in range(ND)]
    dsts = [mw["xprojT"][:, 96 * j:96 * (j + 1)] for j in range(ND)]
    emit_load_cast(pb, rows, stage, dsts, dma_sem, sems)
    gsem = pb.sem("gdma")
    pb.op("gpsimd", "dma_start", out=mw["dtPT"][0:64, :], in_=dram["dtPT"][0:64, :],
          sem=gsem, inc=16)
    pb.wait("tensor", gsem)
    pb.op("sync", "dma_start", out=mw["oneh"][0:96, :], in_=dram["oneh"][:],
          sem=dma_sem, inc=16)
    for nm in ("cwc", "cbc", "dtbc", "alogc", "Dpc"):
        pb.op("sync", "dma_start", out=mw[nm][:], in_=dram[nm + "_in"][:],
              sem=dma_sem, inc=16)


def build_mamba_program():
    nc = bass.Bass("TRN2", target_bir_lowering=False, debug=False)
    pb = PB(nc)
    dram = {
        "xfull": nc.dram_tensor("xfull", [TFULL, D], F32, kind="ExternalInput").ap(),
        "xdbl_peer": nc.dram_tensor("xdbl_peer", [96, TFULL], F32, kind="ExternalInput").ap(),
        "inprojT_xi": nc.dram_tensor("inprojT_xi", [D, 1024], F32, kind="ExternalInput").ap(),
        "inprojT_z": nc.dram_tensor("inprojT_z", [D, 1024], F32, kind="ExternalInput").ap(),
        "outprojT": nc.dram_tensor("outprojT", [1024, 1024], F32, kind="ExternalInput").ap(),
        "xprojT": nc.dram_tensor("xprojT", [1024, 96], F32, kind="ExternalInput").ap(),
        "dtPT": nc.dram_tensor("dtPT", [64, 1024], F32, kind="ExternalInput").ap(),
        "oneh": nc.dram_tensor("oneh", [96, 4096], BF16, kind="ExternalInput").ap(),
        "cwc_in": nc.dram_tensor("cwc_in", [128, 32], F32, kind="ExternalInput").ap(),
        "cbc_in": nc.dram_tensor("cbc_in", [128, 8], F32, kind="ExternalInput").ap(),
        "dtbc_in": nc.dram_tensor("dtbc_in", [128, 8], F32, kind="ExternalInput").ap(),
        "alogc_in": nc.dram_tensor("alogc_in", [128, 128], F32, kind="ExternalInput").ap(),
        "Dpc_in": nc.dram_tensor("Dpc_in", [128, 8], F32, kind="ExternalInput").ap(),
        "ident": nc.dram_tensor("ident", [128, 128], F32, kind="ExternalInput").ap(),
        "xdbl_out": nc.dram_tensor("xdbl_out", [96, TFULL], F32, kind="ExternalOutput").ap(),
        "xc_out": nc.dram_tensor("xc_out", [1024, TFULL], BF16, kind="ExternalOutput").ap(),
        "out_partial": nc.dram_tensor("out_partial", [TFULL, D], F32, kind="ExternalOutput").ap(),
    }
    mw, scratch = alloc_mamba(pb)
    scratch["ident"] = pb.sbuf("identsb", [128, 128], F32)
    scratch["ps_q"] = pb.psum("ps_q", [128, 512], F32)
    scratch["ps_xd"] = pb.psum("ps_xd", [128, 512], F32)
    scratch["ps_tr"] = pb.psum("ps_tr", [128, 512], F32)
    scratch["ps_bc"] = pb.psum("ps_bc", [128, 512], F32)
    scratch["ps_o"] = pb.psum("ps_o", [128, 1024], F32)
    stage = pb.sbuf("stage", [128, 512], F32)
    dma_sem = pb.sem("dma")
    sems = {"sV": pb.sem("sV"), "sS": pb.sem("sS"), "sT": pb.sem("sT"),
            "sG": pb.sem("sG")}
    st = pb.sem("st")

    pb.op("sync", "dma_start", out=scratch["ident"][:], in_=dram["ident"][:],
          sem=dma_sem, inc=16)
    mamba_weight_prep(pb, dram, mw, stage, dma_sem, sems)
    pb.wait("vector", dma_sem)
    pb.wait("scalar", dma_sem)
    pb.wait("tensor", dma_sem)

    emit_mamba_phase1(pb, dram, mw, scratch, sems, dma_sem)
    pb.wait("sync", sems["sV"])
    pb.op("sync", "dma_start", out=dram["xdbl_out"][0:96, :],
          in_=scratch["xdT_f32"][0:96, :], sem=st, inc=16)
    pb.op("sync", "dma_start",
          out=dram["xc_out"].rearrange("(j p) t -> p j t", p=128),
          in_=scratch["xc"].rearrange("p (j t) -> p j t", t=TFULL),
          sem=st, inc=16)
    pb.wait("sync", st)
    emit_mamba_phase2(pb, dram, mw, scratch, sems, dma_sem)
    return pb.build()


def host_mamba_inputs(in_proj_w, conv_w, conv_b, x_proj_w, dt_proj_w, dt_proj_b,
                      A_log, Dp, out_proj_w, eh):
    e0 = 1024 * eh
    ipT = np.ascontiguousarray(in_proj_w.T)
    inprojT_xi = np.ascontiguousarray(ipT[:, e0:e0 + 1024])
    inprojT_z = np.ascontiguousarray(ipT[:, 2048 + e0:2048 + e0 + 1024])
    outprojT = np.ascontiguousarray(out_proj_w.T[e0:e0 + 1024, :])
    xprojT = np.ascontiguousarray(x_proj_w.T[e0:e0 + 1024, :])
    dtPT = np.ascontiguousarray(dt_proj_w.T[:, e0:e0 + 1024])
    oneh = np.zeros((96, 4096), np.float32)
    for s in range(16):
        oneh[64 + s, 128 * s:128 * (s + 1)] = 1.0
        oneh[80 + s, 2048 + 128 * s:2048 + 128 * (s + 1)] = 1.0
    import ml_dtypes
    oneh = oneh.astype(ml_dtypes.bfloat16)
    cwc = np.ascontiguousarray(
        conv_w[e0:e0 + 1024].reshape(8, 128, 4).transpose(1, 0, 2).reshape(128, 32))
    cbc = np.ascontiguousarray(conv_b[e0:e0 + 1024].reshape(8, 128).T)
    dtbc = np.ascontiguousarray(dt_proj_b[e0:e0 + 1024].reshape(8, 128).T)
    alogc = np.ascontiguousarray(
        A_log[e0:e0 + 1024].reshape(8, 128, 16).transpose(1, 0, 2).reshape(128, 128))
    Dpc = np.ascontiguousarray(Dp[e0:e0 + 1024].reshape(8, 128).T)
    return dict(inprojT_xi=inprojT_xi, inprojT_z=inprojT_z, outprojT=outprojT,
                xprojT=xprojT, dtPT=dtPT, oneh=np.ascontiguousarray(oneh),
                cwc_in=cwc, cbc_in=cbc, dtbc_in=dtbc, alogc_in=alogc, Dpc_in=Dpc)



def build_ffn0_program(add_partials=False):
    nc = bass.Bass("TRN2", target_bir_lowering=False, debug=False)
    pb = PB(nc)
    dram = {
        "xres": nc.dram_tensor("xres", [TOK, D], F32, kind="ExternalInput").ap(),
        "w1T": nc.dram_tensor("w1T", [D, 4096], F32, kind="ExternalInput").ap(),
        "w2T": nc.dram_tensor("w2T", [4096, D], F32, kind="ExternalInput").ap(),
        "b1c": nc.dram_tensor("b1c", [128, 32], F32, kind="ExternalInput").ap(),
        "b2r": nc.dram_tensor("b2r", [1, D], F32, kind="ExternalInput").ap(),
        "ones1": nc.dram_tensor("ones1", [1, 128], BF16, kind="ExternalInput").ap(),
        "ident": nc.dram_tensor("ident", [128, 128], F32, kind="ExternalInput").ap(),
        "maskbf": nc.dram_tensor("maskd", [TOK, D], F8E5, kind="ExternalInput").ap(),
        "xout": nc.dram_tensor("xout", [TOK, D], F32, kind="ExternalOutput").ap(),
    }
    weights = alloc_ffn0_weights(pb)
    weights["maskbf"] = pb.sbuf("maskbf_sb", [128, NT * D], F8E5)
    scratch = alloc_ffn0_scratch(pb)
    scratch["ident"] = pb.sbuf("identsb", [128, 128], F32)
    scratch["ps_tr"] = pb.psum("ps_tr", [128, 512], F32)
    scratch["ps_h"] = pb.psum("ps_h", [128, 512], F32)
    scratch["ps_y"] = pb.psum("ps_y", [128, 1024], F32)
    ps_b2 = pb.psum("ps_b2", [128, 1024], F32)
    xres = pb.sbuf("xres_sb", [128, NT * D], F32)
    stage = pb.sbuf("stage", [128, 2048], F32)
    dma_sem = pb.sem("dma")
    sems = {"sV": pb.sem("sV"), "sS": pb.sem("sS"), "sT": pb.sem("sT"),
            "sG": pb.sem("sG")}
    st = pb.sem("st")

    xr3 = dram["xres"].rearrange("(i p) d -> p i d", p=128)
    pb.op("sync", "dma_start", out=xres.rearrange("p (i d) -> p i d", d=D),
          in_=xr3, sem=dma_sem, inc=16)
    if add_partials:
        dram["pa"] = nc.dram_tensor("pa", [TOK, D], F32, kind="ExternalInput").ap()
        dram["pb2"] = nc.dram_tensor("pb2", [TOK, D], F32, kind="ExternalInput").ap()
        pstage = scratch["xlnb"]
        for i in range(NT):
            pb.wait("sync", sems["sV"], pb.cnt[sems["sV"]])
            pb.dma(pstage[:, 0:D],
                   dram["pa"].rearrange("(i p) d -> p i d", p=128)[:, i:i + 1, :],
                   dma_sem)
            pb.dma(pstage[:, D:2 * D],
                   dram["pb2"].rearrange("(i p) d -> p i d", p=128)[:, i:i + 1, :],
                   dma_sem)
            pb.wait("vector", dma_sem)
            pb.op("vector", "tensor_tensor", out=xres[:, D * i:D * (i + 1)],
                  in0=xres[:, D * i:D * (i + 1)], in1=pstage[:, 0:D], op=ALU.add)
            pb.op("vector", "tensor_tensor", out=xres[:, D * i:D * (i + 1)],
                  in0=xres[:, D * i:D * (i + 1)], in1=pstage[:, D:2 * D],
                  op=ALU.add, sem=sems["sV"])
        pb.wait("scalar", sems["sV"])
        pb.wait("tensor", sems["sV"])
    mk3 = dram["maskbf"].rearrange("(i p) d -> p i d", p=128)
    pb.op("sync", "dma_start",
          out=weights["maskbf"].rearrange("p (i d) -> p i d", d=D),
          in_=mk3, sem=dma_sem, inc=16)
    pb.op("sync", "dma_start", out=scratch["ident"][:], in_=dram["ident"][:],
          sem=dma_sem, inc=16)
    ffn0_weight_prep(pb, dram, weights, stage, ps_b2, dma_sem, sems)
    pb.wait("vector", dma_sem)
    pb.wait("scalar", dma_sem)
    pb.wait("tensor", dma_sem)

    emit_ffn0(pb, xres, weights, scratch, sems)

    pb.wait("sync", sems["sV"])
    pb.op("sync", "dma_start",
          out=dram["xout"].rearrange("(i p) d -> p i d", p=128),
          in_=xres.rearrange("p (i d) -> p i d", d=D), sem=st, inc=16)
    pb.wait("sync", st)
    return pb.build()



def host_ffn0_inputs(W1, b1, W2, b2):
    w1T = np.ascontiguousarray(W1.T)
    w2T = np.ascontiguousarray(W2.T)
    b1c = np.ascontiguousarray(b1.reshape(32, 128).T)
    b2r = np.ascontiguousarray(b2.reshape(1, D))
    return w1T, w2T, b1c, b2r


def host_dropout_mask():
    import jax
    import ml_dtypes
    with jax.default_device(jax.devices("cpu")[0]):
        key = jax.random.key(42, impl="threefry2x32")
        keep = np.asarray(jax.random.bernoulli(key, 0.8, (4, 4096, D)))
    return (keep.astype(np.float32) * 1.25).astype(ml_dtypes.float8_e5m2)


def _shards(x):
    out = []
    for c in range(NCORES):
        b, h = c // 2, c % 2
        out.append(np.ascontiguousarray(x[b, h * TOK:(h + 1) * TOK]))
    return out


def kernel(**inputs):
    import ml_dtypes
    x0 = np.asarray(inputs["L0_attn_a_x"], np.float32)
    ident = np.eye(128, dtype=np.float32)
    identb = np.eye(128, dtype=ml_dtypes.bfloat16)
    onesc = np.ones((128, 1), np.float32)
    onesr = np.ones((1, 128), np.float32)
    ones1b = np.ones((1, 128), dtype=ml_dtypes.bfloat16)
    cores = list(range(NCORES))

    def run(nc, maps):
        return run_bass_kernel_spmd(nc, maps, core_ids=cores)

    nc_m = build_mamba_program()
    mmaps = []
    for c in cores:
        b, eh = c // 2, c % 2
        hi = host_mamba_inputs(
            np.asarray(inputs["L0_attn_a_in_proj_w"], np.float32),
            np.asarray(inputs["L0_attn_a_conv_w"], np.float32),
            np.asarray(inputs["L0_attn_a_conv_b"], np.float32),
            np.asarray(inputs["L0_attn_a_x_proj_w"], np.float32),
            np.asarray(inputs["L0_attn_a_dt_proj_w"], np.float32),
            np.asarray(inputs["L0_attn_a_dt_proj_b"], np.float32),
            np.asarray(inputs["L0_attn_a_A_log"], np.float32),
            np.asarray(inputs["L0_attn_a_Dp"], np.float32),
            np.asarray(inputs["L0_attn_a_out_proj_w"], np.float32), eh)
        hi["xfull"] = np.ascontiguousarray(x0[b])
        hi["xdbl_peer"] = np.zeros((96, TFULL), np.float32)
        hi["ident"] = ident
        mmaps.append(hi)
    r1 = run(nc_m, mmaps)
    for c in cores:
        mmaps[c]["xdbl_peer"] = np.ascontiguousarray(r1.results[c ^ 1]["xdbl_out"])
    r2 = run(nc_m, mmaps)
    pouts = [r2.results[c]["out_partial"] for c in cores]

    W1 = np.asarray(inputs["L0_ffn_W1"], np.float32)
    b1 = np.asarray(inputs["L0_ffn_b1"], np.float32)
    W2 = np.asarray(inputs["L0_ffn_W2"], np.float32)
    b2 = np.asarray(inputs["L0_ffn_b2"], np.float32)
    w1T, w2T, b1c, b2r = host_ffn0_inputs(W1, b1, W2, b2)
    mask = host_dropout_mask()
    xsh = _shards(x0)
    nc_f0p = build_ffn0_program(add_partials=True)
    maps = []
    for c in cores:
        b, h = c // 2, c % 2
        sl = np.s_[h * TOK:(h + 1) * TOK]
        maps.append({
            "xres": xsh[c],
            "pa": np.ascontiguousarray(pouts[2 * b][sl]),
            "pb2": np.ascontiguousarray(pouts[2 * b + 1][sl]),
            "w1T": w1T, "w2T": w2T, "b1c": b1c, "b2r": b2r, "ones1": ones1b,
            "ident": ident,
            "maskd": np.ascontiguousarray(mask[b, h * TOK:(h + 1) * TOK]),
        })
    r = run(nc_f0p, maps)
    xsh = [np.ascontiguousarray(r.results[c]["xout"]) for c in cores]

    nc_ab = build_attnb0_program()
    maskneg = host_attnb0_masks()
    maps = []
    for c in cores:
        maps.append({
            "xres": xsh[c],
            "wqkv": np.asarray(inputs["L0_attn_b_w_qkv"], np.float32),
            "wproj": np.asarray(inputs["L0_attn_b_w_proj"], np.float32),
            "lnw": np.asarray(inputs["L0_attn_b_ln_w"], np.float32).reshape(1, -1),
            "ones32": onesr, "maskneg": maskneg, "ident": ident, "identb": identb,
        })
    r = run(nc_ab, maps)
    xsh = [np.ascontiguousarray(r.results[c]["xout"]) for c in cores]

    nc_f0 = build_ffn0_program(add_partials=False)
    maps = []
    for c in cores:
        b, h = c // 2, c % 2
        maps.append({
            "xres": xsh[c],
            "w1T": w1T, "w2T": w2T, "b1c": b1c, "b2r": b2r, "ones1": ones1b,
            "ident": ident,
            "maskd": np.ascontiguousarray(mask[b, h * TOK:(h + 1) * TOK]),
        })
    r = run(nc_f0, maps)
    xsh = [np.ascontiguousarray(r.results[c]["xout"]) for c in cores]

    nc_a5 = build_attna1_program()
    enc_q = np.asarray(inputs["L1_attn_a_enc_q"], np.float32)
    enc_v = np.asarray(inputs["L1_attn_a_enc_v"], np.float32)
    encq = np.ascontiguousarray(enc_q.transpose(1, 0, 2).reshape(D, 1024))
    encv = np.ascontiguousarray(enc_v.transpose(1, 0, 2).reshape(D, 1024))
    hebbc = np.ascontiguousarray(
        np.asarray(inputs["L1_attn_a_hebbian"], np.float32).reshape(1024, 128))
    mb1c = np.ascontiguousarray(
        np.asarray(inputs["L1_attn_a_m_b1"], np.float32).reshape(4, 128).T)
    mb2c = np.ascontiguousarray(
        np.asarray(inputs["L1_attn_a_m_b2"], np.float32).reshape(8, 128).T)
    basec = np.ascontiguousarray(
        np.asarray(inputs["L1_attn_a_base_metric"], np.float32).T)
    wvbc, bandmask = host_attna1_consts()
    maps = []
    for c in cores:
        b, h = c // 2, c % 2
        halo = np.zeros((31, D), np.float32)
        if h == 1:
            halo[:] = xsh[2 * b][TOK - 31:TOK]
        maps.append({
            "xres": xsh[c], "xhalo": halo,
            "sigpeer": np.zeros((128, 8), np.float32),
            "encq": encq, "encv": encv,
            "decoder": np.asarray(inputs["L1_attn_a_decoder"], np.float32),
            "mw1": np.asarray(inputs["L1_attn_a_m_w1"], np.float32),
            "mw2": np.asarray(inputs["L1_attn_a_m_w2"], np.float32),
            "hebbc": hebbc, "mb1c": mb1c, "mb2c": mb2c, "basec": basec,
            "wvbc": wvbc, "bandmask": bandmask, "onesc": onesc, "onesr": onesr,
            "ident": ident, "identb": identb,
        })
    ra = run(nc_a5, maps)
    for c in cores:
        maps[c]["sigpeer"] = np.ascontiguousarray(ra.results[c ^ 1]["sig_out"])
    r = run(nc_a5, maps)
    xsh = [np.ascontiguousarray(r.results[c]["xout"]) for c in cores]

    nc_f1 = build_ffn1_program()
    f1w1T, f1w2T, cw, cb = host_ffn1_inputs(
        np.asarray(inputs["L1_ffn_w1"], np.float32),
        np.asarray(inputs["L1_ffn_conv_w"], np.float32),
        np.asarray(inputs["L1_ffn_conv_b"], np.float32),
        np.asarray(inputs["L1_ffn_w2"], np.float32))

    def ffn1_maps():
        maps = []
        for c in cores:
            b, h = c // 2, c % 2
            halo = np.zeros((2, D), np.float32)
            if h == 1:
                halo[0] = xsh[2 * b][TOK - 1]
            if h == 0:
                halo[1] = xsh[2 * b + 1][0]
            maps.append({
                "xres": xsh[c], "xhalo": halo,
                "f1w1T": f1w1T, "f1w2T": f1w2T, "cw": cw, "cb": cb,
                "ident": ident, "onesc": onesc, "onesr": onesr,
            })
        return maps
    r = run(nc_f1, ffn1_maps())
    xsh = [np.ascontiguousarray(r.results[c]["xout"]) for c in cores]

    nc_a7 = build_attnb1_program()
    WwT = np.ascontiguousarray(np.asarray(inputs["L1_attn_b_W_write"], np.float32).T)
    WrT = np.ascontiguousarray(np.asarray(inputs["L1_attn_b_W_read"], np.float32).T)
    iotaDI, iotaGW, iota64, iotat, dmat, causalT = host_attnb1_consts()
    decay = np.asarray(inputs["L1_attn_b_decay"], np.float32).reshape(1, 1)
    la = np.asarray(inputs["L1_attn_b_log_alpha"], np.float32).reshape(1, 1)
    maps = []
    for c in cores:
        b, h = c // 2, c % 2
        whalo = np.zeros((1, D), np.float32)
        if h == 1:
            whalo[0] = xsh[2 * b][TOK - 1]
        maps.append({
            "xres": xsh[c], "whalo": whalo,
            "WwT": WwT, "WrT": WrT,
            "WpT": np.zeros((D, D), ml_dtypes.bfloat16),
            "decay": decay, "log_alpha": la,
            "gflag": np.full((1, 1), float(h), np.float32),
            "onesr": onesr, "iotaDI": iotaDI, "iotaGW": iotaGW,
            "iota64": iota64, "iotat": iotat, "dmat": dmat,
            "causalT": causalT, "ident": ident, "identb": identb,
        })
    ra = run(nc_a7, maps)
    for c in cores:
        peer = c - 1 if c % 2 == 1 else c
        maps[c]["WpT"] = np.ascontiguousarray(ra.results[peer]["WT_out"])
    rb = run(nc_a7, maps)
    xsh = [np.ascontiguousarray((rb if c % 2 == 1 else ra).results[c]["xout"])
           for c in cores]

    r = run(nc_f1, ffn1_maps())
    xsh = [np.ascontiguousarray(r.results[c]["xout"]) for c in cores]

    out = np.empty((4, TFULL, D), np.float32)
    for c in cores:
        b, h = c // 2, c % 2
        out[b, h * TOK:(h + 1) * TOK] = xsh[c]
    return out


# version 04 (final)
# speedup vs baseline: 1.5914x; 1.5914x over previous
import contextlib
import numpy as np
import concourse.bass as bass
import concourse.mybir as mybir
from concourse.bass_utils import run_bass_kernel_spmd

F32 = mybir.dt.float32
BF16 = mybir.dt.bfloat16
F8E5 = mybir.dt.float8e5
AF = mybir.ActivationFunctionType
ALU = mybir.AluOpType
AX = mybir.AxisListType

MAGIC = 1.5 * 2 ** 23
NCORES = 8
TOK = 2048
NT = TOK // 128
D = 1024
ND = D // 128

ENGINES = ("sync", "vector", "scalar", "tensor", "gpsimd")


class PB:

    def __init__(self, nc):
        self.nc = nc
        self.es = contextlib.ExitStack()
        self.ops = {e: [] for e in ENGINES}
        self.cnt = {}
        self._names = {}
        self.const_need = set()
        self._lastop = {e: None for e in ENGINES}

    def cf(self, v):
        v = float(v)
        if v not in (0.0, 1.0, -1.0):
            self.const_need.add(v)
        return v

    def sem(self, name):
        h = self.es.enter_context(self.nc.semaphore(name))
        self.cnt[h] = 0
        self._names[h] = name
        return h

    def sbuf(self, name, shape, dtype):
        if getattr(self, "arena_on", False):
            esz = {F32: 4, BF16: 2, F8E5: 1}[dtype]
            off = (self._aoff + 15) // 16 * 16
            self._aoff = off + shape[1] * esz
            if self._aoff > self._amax:
                raise RuntimeError(
                    f"sbuf arena overflow at {name}: {self._aoff} > {self._amax}")
            base = self._arena if dtype == F32 else self._arena.bitcast(dtype)
            return base[0:shape[0], off // esz:off // esz + shape[1]]
        return self.es.enter_context(self.nc.sbuf_tensor(name, shape, dtype))

    def psum(self, name, shape, dtype=F32):
        if getattr(self, "arena_on", False):
            banks = (shape[1] * (4 if dtype == F32 else 2) + 2047) // 2048
            if banks == 2:
                for ti, fr in enumerate(self._pfree):
                    if fr[0] and fr[1]:
                        fr[0] = fr[1] = False
                        return self._ppool[ti][0:shape[0], 0:shape[1]]
            else:
                for ti, fr in enumerate(self._pfree):
                    for b in range(2):
                        if fr[b]:
                            fr[b] = False
                            t = self._ppool[ti]
                            if dtype == F32:
                                return t[0:shape[0], 512 * b:512 * b + shape[1]]
                            return t.bitcast(dtype)[0:shape[0],
                                                    1024 * b:1024 * b + shape[1]]
            raise RuntimeError(f"psum pool exhausted at {name}")
        return self.es.enter_context(self.nc.psum_tensor(name, shape, dtype))

    def arena_init(self, nbytes):
        self._arena = self.sbuf("arena_sb", [128, nbytes // 4], F32)
        self._amax = nbytes
        self._abase = 0
        self._aoff = 0
        self._ppool = [self.psum(f"parena{i}", [128, 1024], F32) for i in range(4)]
        self._pfree = [[True, True] for _ in self._ppool]
        self.arena_on = True

    def arena_persist(self):
        self._abase = (self._aoff + 15) // 16 * 16

    def arena_reset(self, base=None):
        self._aoff = self._abase if base is None else base
        self._pfree = [[True, True] for _ in self._ppool]

    def emit(self, eng, fn):
        self.ops[eng].append(fn)

    def op(self, eng, method, *args, sem=None, inc=1, **kw):
        if eng in ("vector", "scalar", "gpsimd") and self._lastop[eng] == "compute" \
                and method != "dma_start":
            self.emit(eng, lambda e: e.drain())
        if method != "dma_start":
            self._lastop[eng] = "compute"
        if sem is not None:
            self.cnt[sem] += inc

        def fn(e, method=method, args=args, kw=kw, sem=sem, inc=inc):
            try:
                ins = getattr(e, method)(*args, **kw)
            except Exception as ex:
                raise RuntimeError(f"emit failed: {method} kw={list(kw)}") from ex
            if sem is not None:
                ins.then_inc(sem, inc)
        self.emit(eng, fn)
        return self.cnt[sem] if sem is not None else None

    def dma(self, dst, src, sem, eng="sync"):
        self.wait(eng, sem, self.cnt[sem])
        return self.op(eng, "dma_start", out=dst, in_=src, sem=sem, inc=16)

    def draininc(self, eng, sem, inc=1):
        self.cnt[sem] += inc

        def fn(e, sem=sem, inc=inc):
            e.drain().then_inc(sem, inc)
        self.emit(eng, fn)
        return self.cnt[sem]

    def wait(self, eng, sem, val=None):
        v = self.cnt[sem] if val is None else val
        self.emit(eng, lambda e, sem=sem, v=v: e.wait_ge(sem, v))

    def build(self):
        nc = self.nc
        if self.const_need:
            vals = sorted(self.const_need)
            pool = nc.alloc_sbuf_tensor("constpool_sb", [128, len(vals)], F32, side="right")
            csem = self.es.enter_context(nc.semaphore("constpool_sem"))
            for i, v in enumerate(vals):
                nc.const_aps.register(F32, float(v), pool.ap()[:, i:i + 1])

            def fills(e, vals=vals, pool=pool, csem=csem):
                for i, v in enumerate(vals):
                    ins = e.memset(pool.ap()[:, i:i + 1], float(v))
                ins.then_inc(csem, 1)
            self.ops["vector"].insert(0, fills)
            n = 1
            for eng in ENGINES:
                if eng == "vector" or not self.ops[eng]:
                    continue
                self.ops[eng].insert(0, lambda e, csem=csem, n=n: e.wait_ge(csem, n))
        with self.es:
            with nc.Block() as block:
                for eng in ENGINES:
                    lst = self.ops[eng]
                    if not lst:
                        continue
                    dec = getattr(block, eng)

                    def body(e, lst=lst):
                        for fn in lst:
                            fn(e)
                    dec(body)
        return nc



def emit_ln_stats(pb, xres, stats, rstd, nbias, sems, eps=1e-5, ntiles=NT, width=D,
                  dummy=None):
    sV, sS = sems["sV"], sems["sS"]
    nt = ntiles
    for i in range(ntiles):
        pb.op("vector", "reduce_sum", out=stats[:, i:i + 1],
              in_=xres[:, width * i:width * (i + 1)], axis=AX.X,
              sem=sV if i == ntiles - 1 else None)
    for i in range(ntiles):
        pb.op("scalar", "activation", out=dummy[:, 0:width],
              in_=xres[:, width * i:width * (i + 1)], func=AF.Square,
              accum_out=stats[:, nt + i:nt + i + 1],
              sem=sS if i == ntiles - 1 else None)
    pb.wait("vector", sS)
    pb.op("vector", "tensor_scalar", out=stats[:, 2 * nt:3 * nt],
          in0=stats[:, 0:nt], scalar1=1.0 / width, scalar2=None, op0=ALU.mult)
    pb.op("vector", "tensor_tensor", out=stats[:, 3 * nt:4 * nt],
          in0=stats[:, 2 * nt:3 * nt], in1=stats[:, 2 * nt:3 * nt], op=ALU.mult)
    pb.op("vector", "scalar_tensor_tensor", out=stats[:, 3 * nt:4 * nt],
          in0=stats[:, nt:2 * nt], scalar=1.0 / width,
          in1=stats[:, 3 * nt:4 * nt], op0=ALU.mult, op1=ALU.subtract, sem=sV)
    pb.wait("scalar", sV)
    pb.op("scalar", "activation", out=rstd[:, 0:nt], in_=stats[:, 3 * nt:4 * nt],
          func=AF.Ln, bias=pb.cf(eps))
    pb.op("scalar", "activation", out=rstd[:, 0:nt], in_=rstd[:, 0:nt],
          func=AF.Exp, scale=pb.cf(-0.5), sem=sS)
    pb.wait("vector", sS)
    pb.op("vector", "scalar_tensor_tensor", out=nbias[:, 0:nt],
          in0=stats[:, 2 * nt:3 * nt], scalar=-1.0,
          in1=rstd[:, 0:nt], op0=ALU.mult, op1=ALU.mult, sem=sV)
    pb.wait("scalar", sV)


def emit_ln_apply(pb, xres_slice, out_ap, rstd_col, nbias_col, sem=None):
    pb.op("vector", "tensor_scalar", out=out_ap, in0=xres_slice,
          scalar1=rstd_col, scalar2=nbias_col, op0=ALU.mult, op1=ALU.add,
          sem=sem)


def emit_mm(pb, psum_ap, pairs, sem=None, inc=1):
    nk = len(pairs)
    for k, (lt, rh) in enumerate(pairs):
        pb.op("tensor", "matmul", psum_ap, lt, rh,
              start=(k == 0), stop=(k == nk - 1),
              sem=sem if k == nk - 1 else None, inc=inc)


def emit_load_cast(pb, dram_ap_rows, stage, dst_cols, dma_sem, sems):
    sV = sems["sV"]
    for j in range(len(dram_ap_rows)):
        shp = dram_ap_rows[j].shape
        W = 1
        for s in shp[1:]:
            W *= int(s)
        st_view = stage[:, 0:W]
        if len(shp) == 3:
            st_view = st_view.rearrange("p (a d) -> p a d", d=int(shp[-1]))
        pb.wait("sync", sV, pb.cnt[sV])
        pb.op("sync", "dma_start", out=st_view, in_=dram_ap_rows[j],
              sem=dma_sem, inc=16)
        pb.wait("vector", dma_sem)
        pb.op("vector", "tensor_copy", out=dst_cols[j], in_=st_view, sem=sV)
    pb.wait("sync", sV, pb.cnt[sV])



def emit_ffn0(pb, xres, weights, scratch, sems):
    sV, sS, sT = sems["sV"], sems["sS"], sems["sT"]
    xqT, hq = scratch["xqT_f8"], scratch["hq_f8"]
    ps_tr, ps_h, ps_y = scratch["ps_tr"], scratch["ps_h"], scratch["ps_y"]
    ident = scratch["ident"]
    w1T, w2T = weights["w1T_f8"], weights["w2T_f8"]
    b1q, b2bc, maskbf = weights["b1q"], weights["b2bc"], weights["maskbf"]

    emit_ln_stats(pb, xres, scratch["stats"], scratch["rstd"], scratch["nbias"], sems, dummy=scratch["xlnb"])

    xqT3 = xqT.rearrange("p (j t) -> p j t", t=TOK)
    xlnb = scratch["xlnb"]
    y2d = xlnb[:, 0:D]
    for i in range(NT):
        half = i % 2
        if i >= 2:
            pb.wait("vector", sT)
        emit_ln_apply(pb, xres[:, D * i:D * (i + 1)],
                      xlnb[:, D * half:D * (half + 1)],
                      scratch["rstd"][:, i:i + 1], scratch["nbias"][:, i:i + 1],
                      sem=sV)
        pb.wait("tensor", sV)
        for g in range(2):
            pb.wait("tensor", sV)
            for jj in range(4):
                j = 4 * g + jj
                pb.op("tensor", "transpose", ps_tr[:, 128 * jj:128 * (jj + 1)],
                      xlnb[:, D * half + 128 * j:D * half + 128 * (j + 1)],
                      ident[:], sem=sT if jj == 3 else None)
            pb.wait("vector", sT)
            pb.op("vector", "tensor_copy",
                  out=xqT3[:, 4 * g:4 * g + 4, 128 * i:128 * (i + 1)],
                  in_=ps_tr[:].rearrange("p (j t) -> p j t", t=128), sem=sV)
    pb.wait("tensor", sV)

    for tc in range(4):
        t0 = 512 * tc
        for m in range(32):
            pairs = [(w1T[:, 4096 * j + 128 * m:4096 * j + 128 * (m + 1)],
                      xqT[:, TOK * j + t0:TOK * j + t0 + 512])
                     for j in range(ND)]
            emit_mm(pb, ps_h[:], pairs, sem=sT)
            pb.wait("vector", sT)
            pb.op("vector", "tensor_scalar", out=scratch["hbf"][:, 0:512],
                  in0=ps_h[:], scalar1=b1q[:, m:m + 1], scalar2=0.0,
                  op0=ALU.add, op1=ALU.max, sem=sV)
            pb.op("vector", "tensor_copy", out=hq[:, 512 * m:512 * (m + 1)],
                  in_=scratch["hbf"][:, 0:512], sem=sV)
            pb.wait("tensor", sV, pb.cnt[sV] - 1)
        pb.wait("tensor", sV)
        for sub in range(4):
            for dh in range(2):
                pairs = [(hq[:, 512 * m + 128 * sub:512 * m + 128 * (sub + 1)],
                          w2T[:, 1024 * m + 512 * dh:1024 * m + 512 * (dh + 1)])
                         for m in range(32)]
                emit_mm(pb, ps_y[:, 512 * dh:512 * (dh + 1)], pairs, sem=sT)
            pb.wait("vector", sT)
            i = 4 * tc + sub
            pb.op("vector", "tensor_tensor", out=scratch["y2bf"][:],
                  in0=ps_y[:], in1=b2bc[:], op=ALU.add, sem=sV)
            pb.op("vector", "tensor_tensor", out=y2d,
                  in0=scratch["y2bf"][:], in1=maskbf[:, D * i:D * (i + 1)],
                  op=ALU.mult, sem=sV)
            pb.op("vector", "tensor_tensor", out=xres[:, D * i:D * (i + 1)],
                  in0=xres[:, D * i:D * (i + 1)], in1=y2d,
                  op=ALU.add, sem=sV)
            pb.wait("tensor", sV, pb.cnt[sV] - 2)


def ffn0_weight_prep(pb, dram, weights, stage, ps_b2, dma_sem, sems):
    sV, sT = sems["sV"], sems["sT"]
    w1rows = [dram["w1T"][128 * j:128 * (j + 1), 2048 * c:2048 * (c + 1)]
              for j in range(ND) for c in range(2)]
    w1dst = [weights["w1T_f8"][:, 4096 * j + 2048 * c:4096 * j + 2048 * (c + 1)]
             for j in range(ND) for c in range(2)]
    emit_load_cast(pb, w1rows, stage, w1dst, dma_sem, sems)
    w2rows = [dram["w2T"][256 * m:256 * (m + 1), :].rearrange("(a p) d -> p a d", p=128)
              for m in range(16)]
    w2dst = [weights["w2T_f8"][:, 2048 * m:2048 * (m + 1)].rearrange("p (a d) -> p a d", d=D)
             for m in range(16)]
    emit_load_cast(pb, w2rows, stage, w2dst, dma_sem, sems)
    pb.op("sync", "dma_start", out=weights["b1_raw"][:], in_=dram["b1c"][:],
          sem=dma_sem, inc=16)
    pb.op("sync", "dma_start", out=weights["b2_raw"][0:1, :], in_=dram["b2r"][:],
          sem=dma_sem, inc=16)
    pb.op("sync", "dma_start", out=weights["onesbf"][0:1, :], in_=dram["ones1"][:],
          sem=dma_sem, inc=16)
    pb.wait("vector", dma_sem)
    pb.op("vector", "tensor_copy", out=weights["b1_f8"][:], in_=weights["b1_raw"][:])
    pb.op("vector", "tensor_copy", out=weights["b1q"][:], in_=weights["b1_f8"][:])
    pb.op("vector", "tensor_copy", out=weights["b2_f8"][0:1, :],
          in_=weights["b2_raw"][0:1, :])
    pb.op("vector", "tensor_copy", out=weights["b2q_bf"][0:1, :],
          in_=weights["b2_f8"][0:1, :], sem=sV)
    pb.wait("tensor", sV)
    for dh in range(2):
        pb.op("tensor", "matmul", ps_b2[:, 512 * dh:512 * (dh + 1)],
              weights["onesbf"][0:1, :],
              weights["b2q_bf"][0:1, 512 * dh:512 * (dh + 1)],
              start=True, stop=True, sem=sT if dh == 1 else None)
    pb.wait("vector", sT)
    pb.op("vector", "tensor_copy", out=weights["b2bc"][:], in_=ps_b2[:], sem=sV)


def alloc_ffn0_weights(pb, tag=""):
    return {
        "w1T_f8": pb.sbuf("f0w1T" + tag, [128, ND * 4096], F8E5),
        "w2T_f8": pb.sbuf("f0w2T" + tag, [128, 32 * 1024], F8E5),
        "b1_raw": pb.sbuf("f0b1r" + tag, [128, 32], F32),
        "b1_f8": pb.sbuf("f0b18" + tag, [128, 32], F8E5),
        "b1q": pb.sbuf("f0b1q" + tag, [128, 32], F32),
        "b2_raw": pb.sbuf("f0b2r" + tag, [1, D], F32),
        "b2_f8": pb.sbuf("f0b28" + tag, [1, D], F8E5),
        "b2q_bf": pb.sbuf("f0b2b" + tag, [1, D], BF16),
        "onesbf": pb.sbuf("f0ones" + tag, [1, 128], BF16),
        "b2bc": pb.sbuf("f0b2bc" + tag, [128, D], F32),
    }


def alloc_ffn0_scratch(pb, tag=""):
    return {
        "xlnb": pb.sbuf("f0xlnb" + tag, [128, 2 * D], F32),
        "xqT_f8": pb.sbuf("f0xqT" + tag, [128, ND * TOK], F8E5),
        "hq_f8": pb.sbuf("f0hq" + tag, [128, 32 * 512], F8E5),
        "stats": pb.sbuf("f0stats" + tag, [128, 4 * NT], F32),
        "rstd": pb.sbuf("f0rstd" + tag, [128, NT], F32),
        "nbias": pb.sbuf("f0nbias" + tag, [128, NT], F32),
        "hbf": pb.sbuf("f0hbf" + tag, [128, 512], BF16),
        "y2bf": pb.sbuf("f0y2bf" + tag, [128, D], BF16),
    }



def emit_transpose_batch(pb, srcs, dsts, ident, ps_tr, sems, width=128):
    sV, sT = sems["sV"], sems["sT"]
    n = len(srcs)
    for g in range(0, n, 4):
        gn = min(4, n - g)
        pb.wait("tensor", sV)
        for jj in range(gn):
            pb.op("tensor", "transpose", ps_tr[:, 128 * jj:128 * (jj + 1)],
                  srcs[g + jj], ident[:], sem=sT if jj == gn - 1 else None)
        pb.wait("vector", sT)
        for jj in range(gn):
            pb.op("vector", "tensor_copy", out=dsts[g + jj],
                  in_=ps_tr[:, 128 * jj:128 * (jj + 1)],
                  sem=sV if jj == gn - 1 else None)



def emit_attnb0(pb, xres, weights, scratch, sems):
    sV, sS, sT = sems["sV"], sems["sS"], sems["sT"]
    ident, identb = scratch["ident"], scratch["identb"]
    ps_tr, ps_trb = scratch["ps_tr"], scratch["ps_trb"]
    ps_q, ps_att = scratch["ps_q"], scratch["ps_att"]
    ps_yg, ps_p = scratch["ps_yg"], scratch["ps_p"]
    wqkv, wproj = weights["wqkv_bf"], weights["wproj_bf"]
    lnwbc, masks = weights["lnwbc"], weights["maskneg"]
    xlnb, xlnT = scratch["xlnb"], scratch["xlnT"]
    qf, kf, vbf = scratch["qf"], scratch["kf"], scratch["vbf"]
    qnT, knT = scratch["qnT"], scratch["knT"]
    ybf, yT = scratch["ybf"], scratch["yT"]
    cols = scratch["cols"]
    attm, attp = scratch["attm"], scratch["attp"]
    attpT = scratch["attpT"]

    emit_ln_stats(pb, xres, scratch["stats"], scratch["rstd"], scratch["nbias"],
                  sems, dummy=xlnb)

    for g in range(8):
        for s in range(2):
            i = 2 * g + s
            emit_ln_apply(pb, xres[:, D * i:D * (i + 1)],
                          xlnb[:, D * s:D * (s + 1)],
                          scratch["rstd"][:, i:i + 1], scratch["nbias"][:, i:i + 1],
                          sem=sV)
            pb.wait("tensor", sV)
            srcs = [xlnb[:, D * s + 128 * j:D * s + 128 * (j + 1)] for j in range(8)]
            dsts = [xlnT[:, 256 * j + 128 * s:256 * j + 128 * (s + 1)] for j in range(8)]
            emit_transpose_batch(pb, srcs, dsts, ident, ps_tr, sems)
        pb.wait("tensor", sV)
        for s in range(2):
            for nck in range(6):
                pairs = [(xlnT[:, 256 * j + 128 * s:256 * j + 128 * (s + 1)],
                          wqkv[:, 3072 * j + 512 * nck:3072 * j + 512 * (nck + 1)])
                         for j in range(8)]
                emit_mm(pb, ps_q[:], pairs, sem=sT)
                pb.wait("vector", sT)
                off = 512 * (nck % 2)
                if nck < 2:
                    pb.op("vector", "tensor_copy",
                          out=qf[:, D * s + off:D * s + off + 512], in_=ps_q[:], sem=sV)
                elif nck < 4:
                    pb.op("vector", "tensor_copy",
                          out=kf[:, D * s + off:D * s + off + 512], in_=ps_q[:], sem=sV)
                else:
                    pb.op("vector", "tensor_copy",
                          out=vbf[:, D * s + off:D * s + off + 512], in_=ps_q[:], sem=sV)
                pb.wait("tensor", sV)
        for s in range(2):
            for (src, dstT, ci) in ((qf, qnT, 0), (kf, knT, 1)):
                pb.wait("scalar", sV)
                pb.op("scalar", "activation", out=xlnb[:, 0:D],
                      in_=src[:, D * s:D * (s + 1)], func=AF.Square,
                      accum_out=cols[:, ci:ci + 1], sem=sS)
                pb.wait("scalar", sS)
                pb.op("scalar", "activation", out=cols[:, 2 + ci:3 + ci],
                      in_=cols[:, ci:ci + 1], func=AF.Ln,
                      scale=pb.cf(1.0 / D), bias=pb.cf(1e-6))
                pb.op("scalar", "activation", out=cols[:, 2 + ci:3 + ci],
                      in_=cols[:, 2 + ci:3 + ci], func=AF.Exp, scale=pb.cf(-0.5),
                      sem=sS)
                pb.wait("vector", sS)
                pb.op("vector", "scalar_tensor_tensor", out=scratch["qnb"][:, 0:D],
                      in0=src[:, D * s:D * (s + 1)], scalar=cols[:, 2 + ci:3 + ci],
                      in1=lnwbc[:], op0=ALU.mult, op1=ALU.mult, sem=sV)
                pb.wait("tensor", sV)
                srcs = [scratch["qnb"][:, 128 * j:128 * (j + 1)] for j in range(8)]
                dsts = [dstT[:, 256 * j + 128 * s:256 * j + 128 * (s + 1)]
                        for j in range(8)]
                emit_transpose_batch(pb, srcs, dsts, identb, ps_trb, sems)
        pb.wait("tensor", sV)
        for s in range(2):
            for h in range(16):
                j, rb = h // 2, 64 * (h % 2)
                pb.op("tensor", "matmul", ps_att[:],
                      qnT[rb:rb + 64, 256 * j + 128 * s:256 * j + 128 * (s + 1)],
                      knT[rb:rb + 64, 256 * j:256 * (j + 1)],
                      start=True, stop=True, sem=sT)
                pb.wait("vector", sT)
                pb.op("vector", "scalar_tensor_tensor", out=attm[:],
                      in0=ps_att[:], scalar=0.125, in1=masks[:, 256 * s:256 * (s + 1)],
                      op0=ALU.mult, op1=ALU.add, sem=sV)
                pb.op("vector", "reduce_max", out=cols[:, 4:5], in_=attm[:],
                      axis=AX.X, negate=True, sem=sV)
                pb.wait("scalar", sV)
                pb.op("scalar", "activation", out=attm[:], in_=attm[:], func=AF.Exp,
                      bias=cols[:, 4:5], accum_out=cols[:, 5:6], sem=sS)
                pb.wait("vector", sS)
                pb.op("vector", "reciprocal", out=cols[:, 6:7], in_=cols[:, 5:6],
                      sem=sV)
                pb.op("vector", "tensor_scalar", out=attp[:], in0=attm[:],
                      scalar1=cols[:, 6:7], scalar2=None, op0=ALU.mult, sem=sV)
                pb.wait("tensor", sV)
                srcs = [attp[:, 0:128], attp[:, 128:256]]
                dsts = [attpT[:, 0:128], attpT[:, 128:256]]
                emit_transpose_batch(pb, srcs, dsts, identb, ps_trb, sems)
                pb.wait("tensor", sV)
                hh = h % 8
                pairs = [(attpT[:, 0:128], vbf[:, 0 * D + 64 * h:0 * D + 64 * h + 64]),
                         (attpT[:, 128:256], vbf[:, 1 * D + 64 * h:1 * D + 64 * h + 64])]
                emit_mm(pb, ps_yg[:, 64 * hh:64 * (hh + 1)], pairs,
                        sem=sT if hh == 7 else None)
                if hh == 7:
                    pb.wait("vector", sT)
                    half = h // 8
                    pb.op("vector", "tensor_copy",
                          out=ybf[:, D * s + 512 * half:D * s + 512 * (half + 1)],
                          in_=ps_yg[:], sem=sV)
                    pb.wait("tensor", sV)
        for s in range(2):
            i = 2 * g + s
            srcs = [ybf[:, D * s + 128 * j:D * s + 128 * (j + 1)] for j in range(8)]
            dsts = [yT[:, 256 * j + 128 * s:256 * j + 128 * (s + 1)] for j in range(8)]
            emit_transpose_batch(pb, srcs, dsts, identb, ps_trb, sems)
            pb.wait("tensor", sV)
            for dc in range(2):
                pairs = [(yT[:, 256 * j + 128 * s:256 * j + 128 * (s + 1)],
                          wproj[:, 1024 * j + 512 * dc:1024 * j + 512 * (dc + 1)])
                         for j in range(8)]
                emit_mm(pb, ps_p[:], pairs, sem=sT)
                pb.wait("vector", sT)
                pb.op("vector", "tensor_tensor",
                      out=xres[:, D * i + 512 * dc:D * i + 512 * (dc + 1)],
                      in0=xres[:, D * i + 512 * dc:D * i + 512 * (dc + 1)],
                      in1=ps_p[:], op=ALU.add, sem=sV)
                pb.wait("tensor", sV)


def attnb0_weight_prep(pb, dram, weights, stage, ps_b2, dma_sem, sems):
    sV, sT = sems["sV"], sems["sT"]
    rows = [dram["wqkv"][128 * j:128 * (j + 1), 1536 * c:1536 * (c + 1)]
            for j in range(8) for c in range(2)]
    dsts = [weights["wqkv_bf"][:, 3072 * j + 1536 * c:3072 * j + 1536 * (c + 1)]
            for j in range(8) for c in range(2)]
    emit_load_cast(pb, rows, stage, dsts, dma_sem, sems)
    rows = [dram["wproj"][128 * j:128 * (j + 1), :] for j in range(8)]
    dsts = [weights["wproj_bf"][:, 1024 * j:1024 * (j + 1)] for j in range(8)]
    emit_load_cast(pb, rows, stage, dsts, dma_sem, sems)
    pb.op("sync", "dma_start", out=weights["lnw_raw"][0:1, :], in_=dram["lnw"][:],
          sem=dma_sem, inc=16)
    pb.op("sync", "dma_start", out=weights["ones32"][0:1, :], in_=dram["ones32"][:],
          sem=dma_sem, inc=16)
    pb.op("sync", "dma_start", out=weights["maskneg"][:], in_=dram["maskneg"][:],
          sem=dma_sem, inc=16)
    pb.wait("tensor", dma_sem)
    for dh in range(2):
        pb.op("tensor", "matmul", ps_b2[:, 512 * dh:512 * (dh + 1)],
              weights["ones32"][0:1, :],
              weights["lnw_raw"][0:1, 512 * dh:512 * (dh + 1)],
              start=True, stop=True, sem=sT if dh == 1 else None)
    pb.wait("vector", sT)
    pb.op("vector", "tensor_copy", out=weights["lnwbc"][:], in_=ps_b2[:], sem=sV)


def alloc_attnb0(pb, tag=""):
    weights = {
        "wqkv_bf": pb.sbuf("abwqkv" + tag, [128, 8 * 3072], BF16),
        "wproj_bf": pb.sbuf("abwproj" + tag, [128, 8 * 1024], BF16),
        "lnw_raw": pb.sbuf("ablnw" + tag, [1, D], F32),
        "ones32": pb.sbuf("abones" + tag, [1, 128], F32),
        "lnwbc": pb.sbuf("ablnwbc" + tag, [128, D], F32),
        "maskneg": pb.sbuf("abmask" + tag, [128, 512], F32),
    }
    scratch = {
        "xlnb": pb.sbuf("abxlnb" + tag, [128, 2 * D], F32),
        "xlnT": pb.sbuf("abxlnT" + tag, [128, 8 * 256], BF16),
        "qf": pb.sbuf("abqf" + tag, [128, 2 * D], F32),
        "kf": pb.sbuf("abkf" + tag, [128, 2 * D], F32),
        "vbf": pb.sbuf("abvbf" + tag, [128, 2 * D], BF16),
        "qnb": pb.sbuf("abqnb" + tag, [128, D], BF16),
        "qnT": pb.sbuf("abqnT" + tag, [128, 8 * 256], BF16),
        "knT": pb.sbuf("abknT" + tag, [128, 8 * 256], BF16),
        "ybf": pb.sbuf("abybf" + tag, [128, 2 * D], BF16),
        "yT": pb.sbuf("abyT" + tag, [128, 8 * 256], BF16),
        "attm": pb.sbuf("abattm" + tag, [128, 256], F32),
        "attp": pb.sbuf("abattp" + tag, [128, 256], BF16),
        "attpT": pb.sbuf("abattpT" + tag, [128, 256], BF16),
        "cols": pb.sbuf("abcols" + tag, [128, 8], F32),
        "stats": pb.sbuf("abstats" + tag, [128, 4 * NT], F32),
        "rstd": pb.sbuf("abrstd" + tag, [128, NT], F32),
        "nbias": pb.sbuf("abnbias" + tag, [128, NT], F32),
    }
    return weights, scratch


def build_attnb0_program():
    nc = bass.Bass("TRN2", target_bir_lowering=False, debug=False)
    pb = PB(nc)
    dram = {
        "xres": nc.dram_tensor("xres", [TOK, D], F32, kind="ExternalInput").ap(),
        "wqkv": nc.dram_tensor("wqkv", [D, 3072], F32, kind="ExternalInput").ap(),
        "wproj": nc.dram_tensor("wproj", [D, D], F32, kind="ExternalInput").ap(),
        "lnw": nc.dram_tensor("lnw", [1, D], F32, kind="ExternalInput").ap(),
        "ones32": nc.dram_tensor("ones32", [1, 128], F32, kind="ExternalInput").ap(),
        "maskneg": nc.dram_tensor("maskneg", [128, 512], F32, kind="ExternalInput").ap(),
        "ident": nc.dram_tensor("ident", [128, 128], F32, kind="ExternalInput").ap(),
        "identb": nc.dram_tensor("identb", [128, 128], BF16, kind="ExternalInput").ap(),
        "xout": nc.dram_tensor("xout", [TOK, D], F32, kind="ExternalOutput").ap(),
    }
    weights, scratch = alloc_attnb0(pb)
    scratch["ident"] = pb.sbuf("identsb", [128, 128], F32)
    scratch["identb"] = pb.sbuf("identbsb", [128, 128], BF16)
    scratch["ps_tr"] = pb.psum("ps_tr", [128, 512], F32)
    scratch["ps_trb"] = pb.psum("ps_trb", [128, 512], BF16)
    scratch["ps_q"] = pb.psum("ps_q", [128, 512], F32)
    scratch["ps_att"] = pb.psum("ps_att", [128, 256], F32)
    scratch["ps_yg"] = pb.psum("ps_yg", [128, 512], F32)
    scratch["ps_p"] = pb.psum("ps_p", [128, 512], F32)
    ps_b2 = pb.psum("ps_b2", [128, 1024], F32)
    xres = pb.sbuf("xres_sb", [128, NT * D], F32)
    stage = pb.sbuf("stage", [128, 2048], F32)
    dma_sem = pb.sem("dma")
    sems = {"sV": pb.sem("sV"), "sS": pb.sem("sS"), "sT": pb.sem("sT"),
            "sG": pb.sem("sG")}
    st = pb.sem("st")

    xr3 = dram["xres"].rearrange("(i p) d -> p i d", p=128)
    pb.op("sync", "dma_start", out=xres.rearrange("p (i d) -> p i d", d=D),
          in_=xr3, sem=dma_sem, inc=16)
    pb.op("sync", "dma_start", out=scratch["ident"][:], in_=dram["ident"][:],
          sem=dma_sem, inc=16)
    pb.op("sync", "dma_start", out=scratch["identb"][:], in_=dram["identb"][:],
          sem=dma_sem, inc=16)
    attnb0_weight_prep(pb, dram, weights, stage, ps_b2, dma_sem, sems)
    pb.wait("vector", dma_sem)
    pb.wait("scalar", dma_sem)
    pb.wait("tensor", dma_sem)

    emit_attnb0(pb, xres, weights, scratch, sems)

    pb.wait("sync", sems["sV"])
    pb.op("sync", "dma_start",
          out=dram["xout"].rearrange("(i p) d -> p i d", p=128),
          in_=xres.rearrange("p (i d) -> p i d", d=D), sem=st, inc=16)
    pb.wait("sync", st)
    return pb.build()


def host_attnb0_masks():
    m = np.zeros((128, 512), np.float32)
    for s in range(2):
        for r in range(128):
            for c in range(256):
                t_q = 128 * s + r
                if c > t_q:
                    m[r, 256 * s + c] = -1e30
    return m




def emit_ffn1_weightprep(pb, dram, fw, scratch, sems, dma_sem):
    sV, sS, sT = sems["sV"], sems["sS"], sems["sT"]
    stage, stageb = scratch["stage"], scratch["hsb"]
    cols = fw["wcols"]
    ps1 = scratch["ps_one"]
    pb.wait("sync", dma_sem, pb.cnt[dma_sem])
    for wi, (wsrc, wdst, nrow, ncol) in enumerate(
            ((dram["f1w1T"], dram["w1q_dram"], 8, 4096),
             (dram["f1w2T"], dram["w2q_dram"], 32, 1024))):
        W = min(ncol, 2048)
        nck = max(1, ncol // W)
        chunks = [(r, c) for r in range(nrow) for c in range(nck)]
        pb.op("vector", "memset", fw["acc"][:, 0:1], 0.0, sem=sV)
        for (r, c) in chunks:
            pb.wait("sync", sS, pb.cnt[sS])
            pb.op("sync", "dma_start", out=stage[:, 0:W],
                  in_=wsrc[128 * r:128 * (r + 1), W * c:W * (c + 1)],
                  sem=dma_sem, inc=16)
            pb.wait("scalar", dma_sem)
            pb.wait("scalar", sV)
            pb.op("scalar", "activation", out=stageb[:, 0:W], in_=stage[:, 0:W],
                  func=AF.Abs, accum_out=cols[:, 0:1], sem=sS)
            pb.wait("vector", sS)
            pb.op("vector", "tensor_tensor", out=fw["acc"][:, 0:1],
                  in0=fw["acc"][:, 0:1], in1=cols[:, 0:1], op=ALU.add, sem=sV)
        pb.wait("tensor", sV)
        pb.op("tensor", "matmul", ps1[0:1, 0:1], fw["acc"][:, 0:1],
              fw["ones_col"][:, 0:1], start=True, stop=True, sem=sT)
        pb.wait("vector", sT)
        pb.op("vector", "tensor_scalar", out=cols[0:1, 1:2], in0=ps1[0:1, 0:1],
              scalar1=1.0 / (nrow * 128 * ncol), scalar2=1e-5,
              op0=ALU.mult, op1=ALU.max)
        pb.op("vector", "reciprocal", out=cols[0:1, 2:3], in_=cols[0:1, 1:2],
              sem=sV)
        pb.wait("tensor", sV)
        pb.op("tensor", "matmul", ps1[:, 2:4], fw["ones_row"][0:1, :],
              cols[0:1, 1:3], start=True, stop=True, sem=sT)
        pb.wait("vector", sT)
        pb.op("vector", "tensor_copy", out=fw["dq_col"][:, wi:wi + 1],
              in_=ps1[:, 2:3])
        pb.op("vector", "tensor_copy", out=fw["sw_col"][:, wi:wi + 1],
              in_=ps1[:, 3:4], sem=sV)
        for (r, c) in chunks:
            pb.wait("sync", sV, pb.cnt[sV])
            pb.op("sync", "dma_start", out=stage[:, 0:W],
                  in_=wsrc[128 * r:128 * (r + 1), W * c:W * (c + 1)],
                  sem=dma_sem, inc=16)
            pb.wait("vector", dma_sem)
            pb.op("vector", "tensor_scalar", out=stage[:, 0:W], in0=stage[:, 0:W],
                  scalar1=fw["sw_col"][:, wi:wi + 1], scalar2=MAGIC,
                  op0=ALU.mult, op1=ALU.add)
            pb.op("vector", "tensor_scalar", out=stage[:, 0:W], in0=stage[:, 0:W],
                  scalar1=-MAGIC, scalar2=None, op0=ALU.add)
            pb.op("vector", "tensor_scalar", out=stageb[:, 0:W], in0=stage[:, 0:W],
                  scalar1=-1.0, scalar2=1.0, op0=ALU.max, op1=ALU.min, sem=sV)
            pb.wait("sync", sV)
            pb.op("sync", "dma_start",
                  out=wdst[128 * r:128 * (r + 1), W * c:W * (c + 1)],
                  in_=stageb[:, 0:W], sem=dma_sem, inc=16)
        pb.wait("sync", dma_sem)


def emit_actquant_token(pb, xv, cols, rows, sems):
    sV = sems["sV"]
    pb.op("vector", "tensor_reduce", out=cols[0:rows, 0:1], in_=xv,
          axis=AX.X, op=ALU.max, apply_absolute_value=True)
    pb.op("vector", "tensor_scalar", out=cols[0:rows, 1:2],
          in0=cols[0:rows, 0:1], scalar1=1e-5, scalar2=None, op0=ALU.max)
    pb.op("vector", "reciprocal", out=cols[0:rows, 2:3], in_=cols[0:rows, 1:2])
    pb.op("vector", "tensor_scalar", out=cols[0:rows, 3:4],
          in0=cols[0:rows, 2:3], scalar1=127.0, scalar2=None, op0=ALU.mult)
    pb.op("vector", "tensor_scalar", out=cols[0:rows, 4:5],
          in0=cols[0:rows, 1:2], scalar1=1.0 / 127.0, scalar2=None, op0=ALU.mult)
    pb.op("vector", "tensor_scalar", out=xv, in0=xv,
          scalar1=cols[0:rows, 3:4], scalar2=MAGIC, op0=ALU.mult, op1=ALU.add)
    pb.op("vector", "tensor_scalar", out=xv, in0=xv,
          scalar1=-MAGIC, scalar2=None, op0=ALU.add)
    pb.op("vector", "tensor_scalar", out=xv, in0=xv,
          scalar1=cols[0:rows, 4:5], scalar2=None, op0=ALU.mult, sem=sV)


def emit_ffn1(pb, xres, dram, fw, scratch, sems, dma_sem):
    sV, sS, sT = sems["sV"], sems["sS"], sems["sT"]
    ident = scratch["ident"]
    ps_tr, ps_h = scratch["ps_tr"], scratch["ps_h"]
    ps_y0, ps_y1 = scratch["ps_y0"], scratch["ps_y1"]
    xqT = scratch["xqT_all"]
    xlnb = scratch["xlnb"]
    cols = scratch["fcols"]
    hpre, hsb = scratch["hpre"], scratch["hsb"]
    hcv = scratch["hcv"]
    pm = scratch["pm"]
    sdq = scratch["sdq"]
    srow = scratch["srow"]
    w1s, w2s = scratch["w1s"], scratch["w2s"]
    cwc, cbc = fw["cwc"], fw["cbc"]
    dmaW = scratch["dmaW"]

    emit_ln_stats(pb, xres, scratch["stats"], scratch["rstd"], scratch["nbias"],
                  sems, dummy=xlnb)

    xqT3 = xqT.rearrange("p (j t) -> p j t", t=2050)
    pb.op("vector", "reduce_sum", out=cols[0:2, 5:6],
          in_=scratch["xhalo_sb"][0:2, 0:D], axis=AX.X, sem=sV)
    pb.wait("scalar", sV)
    pb.op("scalar", "activation", out=xlnb[0:2, 0:D],
          in_=scratch["xhalo_sb"][0:2, 0:D], func=AF.Square,
          accum_out=cols[0:2, 6:7], sem=sS)
    pb.wait("vector", sS)
    pb.op("vector", "tensor_scalar", out=cols[0:2, 5:6], in0=cols[0:2, 5:6],
          scalar1=1.0 / D, scalar2=None, op0=ALU.mult)
    pb.op("vector", "tensor_tensor", out=cols[0:2, 7:8], in0=cols[0:2, 5:6],
          in1=cols[0:2, 5:6], op=ALU.mult)
    pb.op("vector", "scalar_tensor_tensor", out=cols[0:2, 7:8],
          in0=cols[0:2, 6:7], scalar=1.0 / D, in1=cols[0:2, 7:8],
          op0=ALU.mult, op1=ALU.subtract, sem=sV)
    pb.wait("scalar", sV)
    pb.op("scalar", "activation", out=cols[0:2, 8:9], in_=cols[0:2, 7:8],
          func=AF.Ln, bias=pb.cf(1e-5))
    pb.op("scalar", "activation", out=cols[0:2, 8:9], in_=cols[0:2, 8:9],
          func=AF.Exp, scale=pb.cf(-0.5), sem=sS)
    pb.wait("vector", sS)
    pb.op("vector", "scalar_tensor_tensor", out=cols[0:2, 9:10],
          in0=cols[0:2, 5:6], scalar=-1.0, in1=cols[0:2, 8:9],
          op0=ALU.mult, op1=ALU.mult)
    pb.op("vector", "tensor_scalar", out=xlnb[0:2, 0:D],
          in0=scratch["xhalo_sb"][0:2, 0:D], scalar1=cols[0:2, 8:9],
          scalar2=cols[0:2, 9:10], op0=ALU.mult, op1=ALU.add, sem=sV)
    emit_actquant_token(pb, xlnb[0:2, 0:D], cols, 2, sems)
    pb.wait("tensor", sV)
    for j in range(ND):
        pb.op("tensor", "transpose", ps_tr[:, 0:2],
              xlnb[0:2, 128 * j:128 * (j + 1)], ident[0:2, 0:2], sem=sT)
        pb.wait("vector", sT)
        pb.op("vector", "tensor_copy", out=xqT3[:, j, 0:1], in_=ps_tr[:, 0:1])
        pb.op("vector", "tensor_copy", out=xqT3[:, j, 2049:2050],
              in_=ps_tr[:, 1:2], sem=sV)
        pb.wait("tensor", sV)
    for i in range(NT):
        half = i % 2
        if i >= 2:
            pb.wait("vector", sT)
        emit_ln_apply(pb, xres[:, D * i:D * (i + 1)],
                      xlnb[:, D * half:D * (half + 1)],
                      scratch["rstd"][:, i:i + 1], scratch["nbias"][:, i:i + 1])
        emit_actquant_token(pb, xlnb[:, D * half:D * (half + 1)], cols, 128, sems)
        pb.wait("tensor", sV)
        for g in range(2):
            pb.wait("tensor", sV)
            for jj in range(4):
                j = 4 * g + jj
                pb.op("tensor", "transpose", ps_tr[:, 128 * jj:128 * (jj + 1)],
                      xlnb[:, D * half + 128 * j:D * half + 128 * (j + 1)],
                      ident[:], sem=sT if jj == 3 else None)
            pb.wait("vector", sT)
            pb.op("vector", "tensor_copy",
                  out=xqT3[:, 4 * g:4 * g + 4, 1 + 128 * i:1 + 128 * (i + 1)],
                  in_=ps_tr[:].rearrange("p (j t) -> p j t", t=128), sem=sV)
    pb.wait("tensor", sV)

    for tc in range(4):
        c0 = 512 * tc
        w1q3 = dram["w1q_dram"].rearrange("(j p) c -> p j c", p=128)
        for hb in range(16):
            par = hb % 2
            psem = dmaW[par]
            if hb >= 2:
                pb.wait("sync", sT, pb.cnt[sT])
            pb.dma(w1s[:, 2048 * par:2048 * (par + 1)]
                   .rearrange("p (j c) -> p j c", c=256),
                   w1q3[:, :, 256 * hb:256 * (hb + 1)], psem)
            pb.wait("tensor", psem)
            for mm in range(2):
                m = 2 * hb + mm
                lts = [w1s[:, 2048 * par + 256 * j + 128 * mm:
                           2048 * par + 256 * j + 128 * (mm + 1)]
                       for j in range(ND)]
                pairs = [(lts[j], xqT[:, 2050 * j + c0:2050 * j + c0 + 512])
                         for j in range(ND)]
                emit_mm(pb, ps_h[:, 0:512], pairs)
                pairs = [(lts[j], xqT[:, 2050 * j + c0 + 512:2050 * j + c0 + 514])
                         for j in range(ND)]
                emit_mm(pb, scratch["ps_hs"][:, 0:2], pairs, sem=sT)
                pb.wait("vector", sT)
                pb.op("vector", "tensor_copy", out=hpre[:, 514 * m:514 * m + 512],
                      in_=ps_h[:, 0:512])
                pb.op("vector", "tensor_copy",
                      out=hpre[:, 514 * m + 512:514 * (m + 1)],
                      in_=scratch["ps_hs"][:, 0:2], sem=sV)
                pb.wait("tensor", sV)
        for m in range(32):
            pb.wait("vector", sS)
            pb.op("vector", "tensor_scalar", out=hcv[:],
                  in0=hpre[:, 514 * m + 1:514 * m + 513],
                  scalar1=cwc[:, 3 * m + 1:3 * m + 2], scalar2=None, op0=ALU.mult)
            pb.op("vector", "scalar_tensor_tensor", out=hcv[:],
                  in0=hpre[:, 514 * m + 0:514 * m + 512],
                  scalar=cwc[:, 3 * m + 0:3 * m + 1], in1=hcv[:],
                  op0=ALU.mult, op1=ALU.add)
            pb.op("vector", "scalar_tensor_tensor", out=hcv[:],
                  in0=hpre[:, 514 * m + 2:514 * m + 514],
                  scalar=cwc[:, 3 * m + 2:3 * m + 3], in1=hcv[:],
                  op0=ALU.mult, op1=ALU.add, sem=sV)
            pb.wait("scalar", sV)
            pb.op("scalar", "activation", out=hsb[:, 512 * m:512 * (m + 1)],
                  in_=hcv[:], func=AF.Silu, scale=fw["dq_col"][:, 0:1],
                  bias=cbc[:, m:m + 1], sem=sS)
            if m == 0:
                pb.op("scalar", "activation", out=pm[:],
                      in_=hsb[:, 512 * m:512 * (m + 1)], func=AF.Abs, sem=sS)
            else:
                pb.op("scalar", "activation", out=hcv[:],
                      in_=hsb[:, 512 * m:512 * (m + 1)], func=AF.Abs, sem=sS)
                pb.wait("vector", sS)
                pb.op("vector", "tensor_tensor", out=pm[:], in0=hcv[:],
                      in1=pm[:], op=ALU.max, sem=sV)
        pb.wait("tensor", sV)
        for q in range(4):
            pb.op("tensor", "transpose", ps_tr[:, 128 * q:128 * (q + 1)],
                  pm[:, 128 * q:128 * (q + 1)], ident[:],
                  sem=sT if q == 3 else None)
        pb.wait("vector", sT)
        for q in range(4):
            pb.op("vector", "tensor_reduce", out=cols[:, q:q + 1],
                  in_=ps_tr[:, 128 * q:128 * (q + 1)], axis=AX.X,
                  op=ALU.max, apply_absolute_value=False,
                  sem=sV if q == 3 else None)
        pb.wait("tensor", sV)
        pb.op("vector", "tensor_scalar", out=cols[:, 0:4], in0=cols[:, 0:4],
              scalar1=1e-5, scalar2=None, op0=ALU.max)
        pb.op("vector", "reciprocal", out=cols[:, 4:8], in_=cols[:, 0:4])
        pb.op("vector", "tensor_scalar", out=cols[:, 4:8], in0=cols[:, 4:8],
              scalar1=127.0, scalar2=None, op0=ALU.mult)
        pb.op("vector", "tensor_scalar", out=cols[:, 8:12], in0=cols[:, 0:4],
              scalar1=1.0 / 127.0, scalar2=None, op0=ALU.mult)
        pb.op("vector", "tensor_scalar", out=cols[:, 8:12], in0=cols[:, 8:12],
              scalar1=fw["dq_col"][:, 1:2], scalar2=None, op0=ALU.mult, sem=sV)
        pb.wait("tensor", sV)
        pb.op("tensor", "transpose", ps_tr[0:8, 0:128], cols[:, 4:12], ident[:], sem=sT)
        pb.wait("vector", sT)
        pb.op("vector", "tensor_copy", out=scratch["srow8"][0:8, :],
              in_=ps_tr[0:8, 0:128], sem=sV)
        pb.wait("sync", sV)
        pb.op("sync", "dma_start", out=srow[0:1, 0:1024],
              in_=scratch["srow8"][0:8, 0:128], sem=dma_sem, inc=16)
        pb.wait("tensor", dma_sem)
        pb.op("tensor", "matmul", ps_h[:, 0:512], fw["ones_row"][0:1, :],
              srow[0:1, 0:512], start=True, stop=True)
        pb.op("tensor", "matmul", ps_tr[:, 0:512], fw["ones_row"][0:1, :],
              srow[0:1, 512:1024], start=True, stop=True, sem=sT)
        pb.wait("vector", sT)
        pb.op("vector", "tensor_copy", out=sdq[:, 0:512], in_=ps_h[:, 0:512])
        pb.op("vector", "tensor_copy", out=sdq[:, 512:1024], in_=ps_tr[:, 0:512],
              sem=sV)
        for m in range(32):
            pb.op("vector", "tensor_tensor", out=hcv[:],
                  in0=hsb[:, 512 * m:512 * (m + 1)], in1=sdq[:, 0:512],
                  op=ALU.mult)
            pb.op("vector", "tensor_scalar", out=hcv[:], in0=hcv[:],
                  scalar1=MAGIC, scalar2=-MAGIC, op0=ALU.add, op1=ALU.add)
            pb.op("vector", "tensor_tensor", out=hsb[:, 512 * m:512 * (m + 1)],
                  in0=hcv[:], in1=sdq[:, 512:1024], op=ALU.mult,
                  sem=sV if m == 31 else None)
        pb.wait("tensor", sV)
        sG = sems["sG"]
        for wave in range(2):
            for m in range(32):
                par = m % 2
                psem = dmaW[par]
                if wave + m > 0:
                    pb.wait("sync", sG, max(0, pb.cnt[sG] - 1))
                pb.dma(w2s[:, 1024 * par:1024 * (par + 1)],
                       dram["w2q_dram"][128 * m:128 * (m + 1), :], psem)
                pb.wait("tensor", psem)
                for s2 in range(2):
                    s = 2 * wave + s2
                    psy = ps_y0 if s2 == 0 else ps_y1
                    for dh in range(2):
                        pb.op("tensor", "matmul", psy[:, 512 * dh:512 * (dh + 1)],
                              hsb[:, 512 * m + 128 * s:512 * m + 128 * (s + 1)],
                              w2s[:, 1024 * par + 512 * dh:1024 * par + 512 * (dh + 1)],
                              start=(m == 0), stop=(m == 31))
                pb.draininc("tensor", sG)
            pb.wait("vector", sG)
            for s2 in range(2):
                s = 2 * wave + s2
                i = 4 * tc + s
                psy = ps_y0 if s2 == 0 else ps_y1
                pb.op("vector", "tensor_tensor", out=xres[:, D * i:D * (i + 1)],
                      in0=xres[:, D * i:D * (i + 1)], in1=psy[:],
                      op=ALU.add, sem=sV)
            pb.wait("tensor", sV)


def alloc_ffn1(pb, tag=""):
    fw = {
        "wcols": pb.sbuf("f1wcols" + tag, [128, 8], F32),
        "acc": pb.sbuf("f1acc" + tag, [128, 2], F32),
        "ones_col": pb.sbuf("f1onesc" + tag, [128, 1], F32),
        "ones_row": pb.sbuf("f1onesr" + tag, [1, 128], F32),
        "dq_col": pb.sbuf("f1dq" + tag, [128, 2], F32),
        "sw_col": pb.sbuf("f1sw" + tag, [128, 2], F32),
        "cwc": pb.sbuf("f1cwc" + tag, [128, 96], F32),
        "cbc": pb.sbuf("f1cbc" + tag, [128, 32], F32),
    }
    scratch = {
        "stage": pb.sbuf("f1stage" + tag, [128, 2048], F32),
        "xqT_all": pb.sbuf("f1xqT" + tag, [128, ND * 2050], BF16),
        "xlnb": pb.sbuf("f1xlnb" + tag, [128, 2 * D], F32),
        "fcols": pb.sbuf("f1fcols" + tag, [128, 16], F32),
        "hpre": pb.sbuf("f1hpre" + tag, [128, 32 * 514], BF16),
        "hsb": pb.sbuf("f1hsb" + tag, [128, 32 * 512], BF16),
        "hcv": pb.sbuf("f1hcv" + tag, [128, 512], F32),
        "pm": pb.sbuf("f1pm" + tag, [128, 512], F32),
        "sdq": pb.sbuf("f1sdq" + tag, [128, 1024], F32),
        "srow": pb.sbuf("f1srow" + tag, [1, 1024], F32),
        "srow8": pb.sbuf("f1srow8" + tag, [8, 128], F32),
        "w1s": pb.sbuf("f1w1s" + tag, [128, 2 * 2048], BF16),
        "w2s": pb.sbuf("f1w2s" + tag, [128, 2 * 1024], BF16),
        "xhalo_sb": pb.sbuf("f1xhalo" + tag, [2, D], F32),
        "stats": pb.sbuf("f1stats" + tag, [128, 4 * NT], F32),
        "rstd": pb.sbuf("f1rstd" + tag, [128, NT], F32),
        "nbias": pb.sbuf("f1nbias" + tag, [128, NT], F32),
    }
    return fw, scratch


def build_ffn1_program():
    nc = bass.Bass("TRN2", target_bir_lowering=False, debug=False)
    pb = PB(nc)
    dram = {
        "xres": nc.dram_tensor("xres", [TOK, D], F32, kind="ExternalInput").ap(),
        "xhalo": nc.dram_tensor("xhalo", [2, D], F32, kind="ExternalInput").ap(),
        "f1w1T": nc.dram_tensor("f1w1T", [D, 4096], F32, kind="ExternalInput").ap(),
        "f1w2T": nc.dram_tensor("f1w2T", [4096, D], F32, kind="ExternalInput").ap(),
        "cw": nc.dram_tensor("cw", [128, 96], F32, kind="ExternalInput").ap(),
        "cb": nc.dram_tensor("cb", [128, 32], F32, kind="ExternalInput").ap(),
        "ident": nc.dram_tensor("ident", [128, 128], F32, kind="ExternalInput").ap(),
        "onesc": nc.dram_tensor("onesc", [128, 1], F32, kind="ExternalInput").ap(),
        "onesr": nc.dram_tensor("onesr", [1, 128], F32, kind="ExternalInput").ap(),
        "w1q_dram": nc.dram_tensor("w1q_dram", [D, 4096], BF16, kind="Internal").ap(),
        "w2q_dram": nc.dram_tensor("w2q_dram", [4096, D], BF16, kind="Internal").ap(),
        "xout": nc.dram_tensor("xout", [TOK, D], F32, kind="ExternalOutput").ap(),
    }
    fw, scratch = alloc_ffn1(pb)
    scratch["ident"] = pb.sbuf("identsb", [128, 128], F32)
    scratch["ps_tr"] = pb.psum("ps_tr", [128, 512], F32)
    scratch["ps_h"] = pb.psum("ps_h", [128, 512], F32)
    scratch["ps_hs"] = pb.psum("ps_hs", [128, 2], F32)
    scratch["ps_y0"] = pb.psum("ps_y0", [128, 1024], F32)
    scratch["ps_y1"] = pb.psum("ps_y1", [128, 1024], F32)
    scratch["ps_one"] = pb.psum("ps_one", [128, 8], F32)
    xres = pb.sbuf("xres_sb", [128, NT * D], F32)
    dma_sem = pb.sem("dma")
    sems = {"sV": pb.sem("sV"), "sS": pb.sem("sS"), "sT": pb.sem("sT"),
            "sG": pb.sem("sG")}
    scratch["dmaW"] = (pb.sem("dmaW0"), pb.sem("dmaW1"))
    st = pb.sem("st")

    pb.op("sync", "dma_start", out=xres.rearrange("p (i d) -> p i d", d=D),
          in_=dram["xres"].rearrange("(i p) d -> p i d", p=128),
          sem=dma_sem, inc=16)
    pb.op("sync", "dma_start", out=scratch["xhalo_sb"][:], in_=dram["xhalo"][:],
          sem=dma_sem, inc=16)
    pb.op("sync", "dma_start", out=scratch["ident"][:], in_=dram["ident"][:],
          sem=dma_sem, inc=16)
    pb.op("sync", "dma_start", out=fw["ones_col"][:], in_=dram["onesc"][:],
          sem=dma_sem, inc=16)
    pb.op("sync", "dma_start", out=fw["ones_row"][0:1, :], in_=dram["onesr"][:],
          sem=dma_sem, inc=16)
    pb.op("sync", "dma_start", out=fw["cwc"][:], in_=dram["cw"][:],
          sem=dma_sem, inc=16)
    pb.op("sync", "dma_start", out=fw["cbc"][:], in_=dram["cb"][:],
          sem=dma_sem, inc=16)
    pb.wait("vector", dma_sem)
    pb.wait("scalar", dma_sem)
    pb.wait("tensor", dma_sem)

    emit_ffn1_weightprep(pb, dram, fw, scratch, sems, dma_sem)
    emit_ffn1(pb, xres, dram, fw, scratch, sems, dma_sem)

    pb.wait("sync", sems["sV"])
    pb.op("sync", "dma_start",
          out=dram["xout"].rearrange("(i p) d -> p i d", p=128),
          in_=xres.rearrange("p (i d) -> p i d", d=D), sem=st, inc=16)
    pb.wait("sync", st)
    return pb.build()


def host_ffn1_inputs(w1, conv_w, conv_b, w2):
    f1w1T = np.ascontiguousarray(w1.T)
    f1w2T = np.ascontiguousarray(w2.T)
    cw = np.ascontiguousarray(
        conv_w.reshape(32, 128, 3).transpose(1, 0, 2).reshape(128, 96))
    cb = np.ascontiguousarray(conv_b.reshape(32, 128).T)
    return f1w1T, f1w2T, cw, cb




def emit_attnb1(pb, xres, aw, scratch, sems, dma_sem):
    sV, sS, sT = sems["sV"], sems["sS"], sems["sT"]
    ident = scratch["ident"]
    ps_a, ps_c, ps_u = scratch["ps_a"], scratch["ps_c"], scratch["ps_u"]
    ps_tr, ps_sst = scratch["ps_tr"], scratch["ps_sst"]
    x7T = scratch["x7T"]
    vbuf = scratch["vbuf"]
    Wst = scratch["Wst"]
    WwT, WrT = aw["WwT_bf"], aw["WrT_bf"]
    WpT = WwT
    xlnb = scratch["rc"]
    cols = scratch["acols"]
    corrf = scratch["corrf"]
    vt = scratch["rT"]
    smt = scratch["smt"]
    rc = scratch["rc"]
    wkc = scratch["wkc"]
    rT = scratch["rT"]
    MT = scratch["MT"]

    pb.op("scalar", "activation", out=cols[0:1, 0:1], in_=aw["decay_sb"][0:1, 0:1],
          func=AF.Sigmoid)
    pb.op("scalar", "activation", out=cols[0:1, 1:2], in_=cols[0:1, 0:1],
          func=AF.Ln)
    pb.op("scalar", "activation", out=cols[0:1, 2:3], in_=aw["la_sb"][0:1, 0:1],
          func=AF.Exp, sem=sS)
    pb.wait("tensor", sS)
    pb.op("tensor", "matmul", ps_sst[:, 0:2], aw["ones_row"][0:1, :],
          cols[0:1, 1:3], start=True, stop=True)
    pb.op("tensor", "matmul", ps_sst[:, 3:4], aw["ones_row"][0:1, :],
          aw["gflag_sb"][0:1, 0:1], start=True, stop=True, sem=sT)
    pb.wait("vector", sT)
    pb.op("vector", "tensor_copy", out=cols[:, 4:6], in_=ps_sst[:, 0:2])
    pb.op("vector", "tensor_copy", out=cols[:, 6:7], in_=ps_sst[:, 3:4], sem=sV)
    lg_bc, alpha_bc, gflag_bc = cols[:, 4:5], cols[:, 5:6], cols[:, 6:7]
    pb.wait("scalar", sV)
    pb.op("scalar", "activation", out=cols[:, 7:8], in_=aw["iotaDI"][:],
          func=AF.Exp, scale=lg_bc)
    pb.op("scalar", "activation", out=cols[:, 8:9], in_=aw["iotaGW"][:],
          func=AF.Exp, scale=lg_bc)
    pb.op("scalar", "activation", out=cols[:, 9:10], in_=aw["iota64"][:],
          func=AF.Exp, scale=lg_bc)
    pb.op("scalar", "activation", out=corrf[:], in_=aw["iotat"][:],
          func=AF.Exp, scale=lg_bc)
    pb.op("scalar", "activation", out=MT[:], in_=aw["dmat"][:],
          func=AF.Exp, scale=lg_bc, sem=sS)
    pb.wait("vector", sS)
    pb.op("vector", "tensor_scalar", out=corrf[:], in0=corrf[:],
          scalar1=gflag_bc, scalar2=None, op0=ALU.mult)
    pb.op("vector", "tensor_tensor", out=MT[:], in0=MT[:], in1=aw["causalT"][:],
          op=ALU.mult, sem=sV)
    dec_col, gw_col, gC_bc = cols[:, 7:8], cols[:, 8:9], cols[:, 9:10]

    emit_ln_stats(pb, xres, scratch["stats"], scratch["rstd"], scratch["nbias"],
                  sems, dummy=xlnb[:, 0:D])
    x7T3 = x7T.rearrange("p (j t) -> p j t", t=2049)
    pb.op("vector", "reduce_sum", out=cols[0:1, 10:11],
          in_=scratch["whalo_sb"][0:1, :], axis=AX.X, sem=sV)
    pb.wait("scalar", sV)
    pb.op("scalar", "activation", out=xlnb[0:1, 0:D],
          in_=scratch["whalo_sb"][0:1, :], func=AF.Square,
          accum_out=cols[0:1, 11:12], sem=sS)
    pb.wait("vector", sS)
    pb.op("vector", "tensor_scalar", out=cols[0:1, 10:11], in0=cols[0:1, 10:11],
          scalar1=1.0 / D, scalar2=None, op0=ALU.mult)
    pb.op("vector", "tensor_tensor", out=cols[0:1, 12:13], in0=cols[0:1, 10:11],
          in1=cols[0:1, 10:11], op=ALU.mult)
    pb.op("vector", "scalar_tensor_tensor", out=cols[0:1, 12:13],
          in0=cols[0:1, 11:12], scalar=1.0 / D, in1=cols[0:1, 12:13],
          op0=ALU.mult, op1=ALU.subtract, sem=sV)
    pb.wait("scalar", sV)
    pb.op("scalar", "activation", out=cols[0:1, 13:14], in_=cols[0:1, 12:13],
          func=AF.Ln, bias=pb.cf(1e-5))
    pb.op("scalar", "activation", out=cols[0:1, 13:14], in_=cols[0:1, 13:14],
          func=AF.Exp, scale=pb.cf(-0.5), sem=sS)
    pb.wait("vector", sS)
    pb.op("vector", "scalar_tensor_tensor", out=cols[0:1, 14:15],
          in0=cols[0:1, 10:11], scalar=-1.0, in1=cols[0:1, 13:14],
          op0=ALU.mult, op1=ALU.mult)
    pb.op("vector", "tensor_scalar", out=xlnb[0:1, 0:D],
          in0=scratch["whalo_sb"][0:1, :], scalar1=cols[0:1, 13:14],
          scalar2=cols[0:1, 14:15], op0=ALU.mult, op1=ALU.add, sem=sV)
    pb.wait("tensor", sV)
    for j in range(ND):
        pb.op("tensor", "transpose", ps_tr[:, 0:1],
              xlnb[0:1, 128 * j:128 * (j + 1)], ident[0:1, 0:1], sem=sT)
        pb.wait("vector", sT)
        pb.op("vector", "tensor_copy", out=x7T3[:, j, 0:1], in_=ps_tr[:, 0:1],
              sem=sV)
        pb.wait("tensor", sV)
    for i in range(NT):
        pb.wait("vector", sT)
        emit_ln_apply(pb, xres[:, D * i:D * (i + 1)], xlnb[:, 0:D],
                      scratch["rstd"][:, i:i + 1], scratch["nbias"][:, i:i + 1],
                      sem=sV)
        pb.wait("tensor", sV)
        for g in range(2):
            pb.wait("tensor", sV)
            for jj in range(4):
                j = 4 * g + jj
                pb.op("tensor", "transpose", ps_tr[:, 128 * jj:128 * (jj + 1)],
                      xlnb[:, 128 * j:128 * (j + 1)],
                      ident[:], sem=sT if jj == 3 else None)
            pb.wait("vector", sT)
            pb.op("vector", "tensor_copy",
                  out=x7T3[:, 4 * g:4 * g + 4, 1 + 128 * i:1 + 128 * (i + 1)],
                  in_=ps_tr[:].rearrange("p (j t) -> p j t", t=128), sem=sV)
    pb.wait("tensor", sV)

    for i in range(NT):
        for dh in range(2):
            pairs = [(x7T[:, 2049 * j + 1 + 128 * i:2049 * j + 1 + 128 * (i + 1)],
                      WwT[:, 1024 * j + 512 * dh:1024 * j + 512 * (dh + 1)])
                     for j in range(ND)]
            emit_mm(pb, ps_a[:, 512 * dh:512 * (dh + 1)], pairs,
                    sem=sT if dh == 1 else None)
        pb.wait("vector", sT)
        pb.op("vector", "tensor_copy", out=vbuf[:, D * i:D * (i + 1)],
              in_=ps_a[:, 0:1024], sem=sV)
        pb.wait("tensor", sV)

    pb.op("vector", "memset", Wst[:], 0.0, sem=sV)
    pb.wait("tensor", sV)

    for cg in range(32):
        i = cg // 2
        par = 64 * (cg % 2)
        tcol = 1 + 64 * cg
        wcol = 64 * cg
        pb.op("vector", "tensor_scalar", out=vt[par:par + 64, 0:D],
              in0=vbuf[par:par + 64, D * i:D * (i + 1)],
              scalar1=gw_col[par:par + 64, 0:1], scalar2=None, op0=ALU.mult,
              sem=sV)
        for j in range(ND):
            pb.op("tensor", "matmul", ps_sst[par:par + 64, 0:64],
                  x7T[:, 2049 * j + wcol:2049 * j + wcol + 64],
                  x7T[:, 2049 * j + tcol:2049 * j + tcol + 64],
                  start=(j == 0), stop=(j == ND - 1),
                  sem=sT if j == ND - 1 else None)
        pb.wait("vector", sT)
        pb.op("vector", "tensor_tensor", out=smt[par:par + 64, 0:64],
              in0=ps_sst[par:par + 64, 0:64], in1=MT[par:par + 64, 0:64],
              op=ALU.mult, sem=sV)
        for dh in range(2):
            pairs = [(x7T[:, 2049 * j + tcol:2049 * j + tcol + 64],
                      Wst[:, 1024 * j + 512 * dh:1024 * j + 512 * (dh + 1)])
                     for j in range(ND)]
            emit_mm(pb, ps_a[par:par + 64, 512 * dh:512 * (dh + 1)], pairs)
        pb.wait("tensor", sV)
        for dh in range(2):
            pb.op("tensor", "matmul", ps_c[par:par + 64, 512 * dh:512 * (dh + 1)],
                  smt[par:par + 64, 0:64],
                  vbuf[par:par + 64, D * i + 512 * dh:D * i + 512 * (dh + 1)],
                  start=True, stop=True, sem=sT if dh == 1 else None)
        for j in range(ND):
            pb.op("tensor", "transpose",
                  scratch["ps_trb"][par:par + 64, 128 * (j % 4):128 * (j % 4) + 128],
                  x7T[:, 2049 * j + wcol:2049 * j + wcol + 64],
                  scratch["identb"][:], sem=sT if (j % 4) == 3 else None)
            if (j % 4) == 3:
                pb.wait("vector", sT)
                pb.op("vector", "tensor_copy",
                      out=wkc[par:par + 64, 512 * (j // 4):512 * (j // 4) + 512],
                      in_=scratch["ps_trb"][par:par + 64, 0:512], sem=sV)
                pb.wait("tensor", sV)
        pb.wait("vector", sT)
        pb.op("vector", "tensor_scalar", out=rc[par:par + 64, 0:D],
              in0=ps_a[par:par + 64, 0:1024],
              scalar1=dec_col[par:par + 64, 0:1], scalar2=None, op0=ALU.mult)
        pb.op("vector", "tensor_tensor", out=rc[par:par + 64, 0:D],
              in0=rc[par:par + 64, 0:D], in1=ps_c[par:par + 64, 0:1024],
              op=ALU.add)
        pb.op("vector", "tensor_copy",
              out=vbuf[par:par + 64, D * i:D * (i + 1)],
              in_=rc[par:par + 64, 0:D], sem=sV)
        for j in range(ND):
            for dh in range(2):
                pb.op("tensor", "matmul", ps_u[:, 0:512],
                      wkc[par:par + 64, 128 * j:128 * (j + 1)],
                      vt[par:par + 64, 512 * dh:512 * (dh + 1)],
                      start=True, stop=True, sem=sT)
                pb.wait("vector", sT)
                pb.op("vector", "scalar_tensor_tensor",
                      out=Wst[:, 1024 * j + 512 * dh:1024 * j + 512 * (dh + 1)],
                      in0=Wst[:, 1024 * j + 512 * dh:1024 * j + 512 * (dh + 1)],
                      scalar=gC_bc, in1=ps_u[:, 0:512],
                      op0=ALU.mult, op1=ALU.add, sem=sV)
                pb.wait("tensor", sV)

    if scratch.get("W_xchg") is not None:
        scratch["W_xchg"](pb)
    pb.wait("sync", sems["sT"], pb.cnt[sems["sT"]])
    pb.wait("sync", dma_sem, pb.cnt[dma_sem])
    pb.dma(WpT[:].rearrange("p (j d) -> p j d", d=1024),
           scratch["WpT_dram"].rearrange("(j p) d -> p j d", p=128), dma_sem)
    pb.wait("tensor", dma_sem)

    for i in range(NT):
        for dh in range(2):
            pairs = [(x7T[:, 2049 * j + 1 + 128 * i:2049 * j + 1 + 128 * (i + 1)],
                      WpT[:, 1024 * j + 512 * dh:1024 * j + 512 * (dh + 1)])
                     for j in range(ND)]
            emit_mm(pb, ps_a[:, 512 * dh:512 * (dh + 1)], pairs,
                    sem=sT if dh == 1 else None)
        pb.wait("vector", sT)
        pb.op("vector", "scalar_tensor_tensor", out=rc[:, 0:D],
              in0=ps_a[:, 0:1024], scalar=corrf[:, i:i + 1],
              in1=vbuf[:, D * i:D * (i + 1)], op0=ALU.mult, op1=ALU.add, sem=sV)
        pb.wait("tensor", sV)
        for g in range(2):
            pb.wait("tensor", sV)
            for jj in range(4):
                j = 4 * g + jj
                pb.op("tensor", "transpose",
                      ps_tr[:, 128 * jj:128 * (jj + 1)],
                      rc[:, 128 * j:128 * (j + 1)], ident[:],
                      sem=sT if jj == 3 else None)
            pb.wait("vector", sT)
            pb.op("vector", "tensor_copy",
                  out=rT[:, 512 * g:512 * (g + 1)]
                  .rearrange("p (j t) -> p j t", t=128),
                  in_=ps_tr[:].rearrange("p (j t) -> p j t", t=128),
                  sem=sV)
        pb.wait("tensor", sV)
        for dh in range(2):
            pairs = [(rT[:, 128 * j:128 * (j + 1)],
                      WrT[:, 1024 * j + 512 * dh:1024 * j + 512 * (dh + 1)])
                     for j in range(ND)]
            emit_mm(pb, ps_c[:, 512 * dh:512 * (dh + 1)], pairs,
                    sem=sT if dh == 1 else None)
        pb.op("vector", "tensor_scalar", out=rc[:],
              in0=xres[:, D * i:D * (i + 1)],
              scalar1=scratch["rstd"][:, i:i + 1],
              scalar2=scratch["nbias"][:, i:i + 1],
              op0=ALU.mult, op1=ALU.add)
        pb.op("vector", "tensor_tensor", out=xres[:, D * i:D * (i + 1)],
              in0=xres[:, D * i:D * (i + 1)], in1=rc[:], op=ALU.add)
        pb.wait("vector", sT)
        pb.op("vector", "scalar_tensor_tensor", out=xres[:, D * i:D * (i + 1)],
              in0=ps_c[:, 0:1024], scalar=alpha_bc,
              in1=xres[:, D * i:D * (i + 1)], op0=ALU.mult, op1=ALU.add, sem=sV)
        pb.wait("tensor", sV)
    pb.wait("sync", sems["sV"])


def attnb1_weight_prep(pb, dram, aw, stage, dma_sem, sems):
    sV = sems["sV"]
    for (key, nm) in (("WwT_bf", "WwT"), ("WrT_bf", "WrT")):
        rows = [dram[nm][128 * j:128 * (j + 1), :] for j in range(ND)]
        dsts = [aw[key][:, 1024 * j:1024 * (j + 1)] for j in range(ND)]
        emit_load_cast(pb, rows, stage, dsts, dma_sem, sems)
    for nm, sb in (("decay", "decay_sb"), ("log_alpha", "la_sb"),
                   ("gflag", "gflag_sb"), ("onesr", "ones_row"),
                   ("iotaDI", "iotaDI"), ("iotaGW", "iotaGW"),
                   ("iota64", "iota64"), ("iotat", "iotat"),
                   ("dmat", "dmat"), ("causalT", "causalT"),
                   ("whalo", "whalo_sb_src")):
        pb.op("sync", "dma_start", out=aw[nm + "_dst"] if False else None, in_=None)             if False else None
    pb.op("sync", "dma_start", out=aw["decay_sb"][0:1, :], in_=dram["decay"][:],
          sem=dma_sem, inc=16)
    pb.op("sync", "dma_start", out=aw["la_sb"][0:1, :], in_=dram["log_alpha"][:],
          sem=dma_sem, inc=16)
    pb.op("sync", "dma_start", out=aw["gflag_sb"][0:1, :], in_=dram["gflag"][:],
          sem=dma_sem, inc=16)
    pb.op("sync", "dma_start", out=aw["ones_row"][0:1, :], in_=dram["onesr"][:],
          sem=dma_sem, inc=16)
    pb.op("sync", "dma_start", out=aw["iotaDI"][:], in_=dram["iotaDI"][:],
          sem=dma_sem, inc=16)
    pb.op("sync", "dma_start", out=aw["iotaGW"][:], in_=dram["iotaGW"][:],
          sem=dma_sem, inc=16)
    pb.op("sync", "dma_start", out=aw["iota64"][:], in_=dram["iota64"][:],
          sem=dma_sem, inc=16)
    pb.op("sync", "dma_start", out=aw["iotat"][:], in_=dram["iotat"][:],
          sem=dma_sem, inc=16)
    pb.op("sync", "dma_start", out=aw["dmat"][:], in_=dram["dmat"][:],
          sem=dma_sem, inc=16)
    pb.op("sync", "dma_start", out=aw["causalT"][:], in_=dram["causalT"][:],
          sem=dma_sem, inc=16)


def alloc_attnb1(pb, tag=""):
    aw = {
        "WwT_bf": pb.sbuf("a1WwT" + tag, [128, ND * 1024], BF16),
        "WrT_bf": pb.sbuf("a1WrT" + tag, [128, ND * 1024], BF16),
        "decay_sb": pb.sbuf("a1dec" + tag, [1, 1], F32),
        "la_sb": pb.sbuf("a1la" + tag, [1, 1], F32),
        "gflag_sb": pb.sbuf("a1gf" + tag, [1, 1], F32),
        "ones_row": pb.sbuf("a1ones" + tag, [1, 128], F32),
        "iotaDI": pb.sbuf("a1iDI" + tag, [128, 1], F32),
        "iotaGW": pb.sbuf("a1iGW" + tag, [128, 1], F32),
        "iota64": pb.sbuf("a1i64" + tag, [128, 1], F32),
        "iotat": pb.sbuf("a1it" + tag, [128, 16], F32),
        "dmat": pb.sbuf("a1dm" + tag, [128, 64], F32),
        "causalT": pb.sbuf("a1ct" + tag, [128, 64], F32),
    }
    scratch = {
        "x7T": pb.sbuf("a1x7T" + tag, [128, ND * 2049], BF16),
        "vbuf": pb.sbuf("a1v" + tag, [128, NT * D], BF16),
        "Wst": pb.sbuf("a1W" + tag, [128, ND * 1024], BF16),
        "acols": pb.sbuf("a1cols" + tag, [128, 16], F32),
        "corrf": pb.sbuf("a1corrf" + tag, [128, 16], F32),
        "smt": pb.sbuf("a1smt" + tag, [128, 64], BF16),
        "rc": pb.sbuf("a1rc" + tag, [128, D], F32),
        "wkc": pb.sbuf("a1wkc" + tag, [128, D], BF16),
        "rT": pb.sbuf("a1rT" + tag, [128, D], BF16),
        "MT": pb.sbuf("a1MT" + tag, [128, 64], F32),
        "whalo_sb": pb.sbuf("a1whalo" + tag, [1, D], F32),
        "stats": pb.sbuf("a1stats" + tag, [128, 4 * NT], F32),
        "rstd": pb.sbuf("a1rstd" + tag, [128, NT], F32),
        "nbias": pb.sbuf("a1nbias" + tag, [128, NT], F32),
    }
    return aw, scratch


def build_attnb1_program():
    nc = bass.Bass("TRN2", target_bir_lowering=False, debug=False)
    pb = PB(nc)
    dram = {
        "xres": nc.dram_tensor("xres", [TOK, D], F32, kind="ExternalInput").ap(),
        "whalo": nc.dram_tensor("whalo", [1, D], F32, kind="ExternalInput").ap(),
        "WwT": nc.dram_tensor("WwT", [D, D], F32, kind="ExternalInput").ap(),
        "WrT": nc.dram_tensor("WrT", [D, D], F32, kind="ExternalInput").ap(),
        "WpT": nc.dram_tensor("WpT", [D, D], BF16, kind="ExternalInput").ap(),
        "decay": nc.dram_tensor("decay", [1, 1], F32, kind="ExternalInput").ap(),
        "log_alpha": nc.dram_tensor("log_alpha", [1, 1], F32, kind="ExternalInput").ap(),
        "gflag": nc.dram_tensor("gflag", [1, 1], F32, kind="ExternalInput").ap(),
        "onesr": nc.dram_tensor("onesr", [1, 128], F32, kind="ExternalInput").ap(),
        "iotaDI": nc.dram_tensor("iotaDI", [128, 1], F32, kind="ExternalInput").ap(),
        "iotaGW": nc.dram_tensor("iotaGW", [128, 1], F32, kind="ExternalInput").ap(),
        "iota64": nc.dram_tensor("iota64", [128, 1], F32, kind="ExternalInput").ap(),
        "iotat": nc.dram_tensor("iotat", [128, 16], F32, kind="ExternalInput").ap(),
        "dmat": nc.dram_tensor("dmat", [128, 64], F32, kind="ExternalInput").ap(),
        "causalT": nc.dram_tensor("causalT", [128, 64], F32, kind="ExternalInput").ap(),
        "ident": nc.dram_tensor("ident", [128, 128], F32, kind="ExternalInput").ap(),
        "identb": nc.dram_tensor("identb", [128, 128], BF16, kind="ExternalInput").ap(),
        "xout": nc.dram_tensor("xout", [TOK, D], F32, kind="ExternalOutput").ap(),
        "WT_out": nc.dram_tensor("WT_out", [D, D], BF16, kind="ExternalOutput").ap(),
    }
    aw, scratch = alloc_attnb1(pb)
    scratch["ident"] = pb.sbuf("identsb", [128, 128], F32)
    scratch["identb"] = pb.sbuf("identbsb", [128, 128], BF16)
    scratch["ps_a"] = pb.psum("ps_a", [128, 1024], F32)
    scratch["ps_c"] = pb.psum("ps_c", [128, 1024], F32)
    scratch["ps_u"] = pb.psum("ps_u", [128, 512], F32)
    scratch["ps_trb"] = pb.psum("ps_trb", [128, 512], BF16)
    scratch["ps_tr"] = pb.psum("ps_tr", [128, 512], F32)
    scratch["ps_sst"] = pb.psum("ps_sst", [128, 64], F32)
    xres = pb.sbuf("xres_sb", [128, NT * D], F32)
    dma_sem = pb.sem("dma")
    sems = {"sV": pb.sem("sV"), "sS": pb.sem("sS"), "sT": pb.sem("sT"),
            "sG": pb.sem("sG")}
    st = pb.sem("st")

    pb.op("sync", "dma_start", out=xres.rearrange("p (i d) -> p i d", d=D),
          in_=dram["xres"].rearrange("(i p) d -> p i d", p=128),
          sem=dma_sem, inc=16)
    pb.op("sync", "dma_start", out=scratch["whalo_sb"][0:1, :],
          in_=dram["whalo"][:], sem=dma_sem, inc=16)
    pb.op("sync", "dma_start", out=scratch["ident"][:], in_=dram["ident"][:],
          sem=dma_sem, inc=16)
    pb.op("sync", "dma_start", out=scratch["identb"][:], in_=dram["identb"][:],
          sem=dma_sem, inc=16)
    attnb1_weight_prep(pb, dram, aw, scratch["rc"], dma_sem, sems)
    pb.wait("vector", dma_sem)
    pb.wait("scalar", dma_sem)
    pb.wait("tensor", dma_sem)

    scratch["WpT_dram"] = dram["WpT"]
    emit_attnb1(pb, xres, aw, scratch, sems, dma_sem)

    pb.wait("sync", sems["sV"])
    pb.op("sync", "dma_start",
          out=dram["xout"].rearrange("(i p) d -> p i d", p=128),
          in_=xres.rearrange("p (i d) -> p i d", d=D), sem=st, inc=16)
    pb.op("sync", "dma_start",
          out=dram["WT_out"].rearrange("(j p) d -> p j d", p=128),
          in_=scratch["Wst"].rearrange("p (j d) -> p j d", d=D),
          sem=st, inc=16)
    pb.wait("sync", st)
    return pb.build()


def host_attnb1_consts():
    p = np.arange(128)
    iotaDI = (p % 64).astype(np.float32).reshape(128, 1)
    iotaGW = (63 - (p % 64)).astype(np.float32).reshape(128, 1)
    iota64 = np.full((128, 1), 64.0, np.float32)
    iotat = (p[:, None] + 128.0 * np.arange(16)[None, :]).astype(np.float32)
    cpr = p % 64
    c = np.arange(64)
    dmat = np.maximum(c[None, :] - 1 - cpr[:, None], 0).astype(np.float32)
    causalT = (c[None, :] > cpr[:, None]).astype(np.float32)
    return iotaDI, iotaGW, iota64, iotat, dmat, causalT




U5 = 2080


def emit_attna1(pb, xres, aw, scratch, sems, dma_sem):
    sV, sS, sT = sems["sV"], sems["sS"], sems["sT"]
    ident, identb = scratch["ident"], scratch["identb"]
    ps_a, ps_b, ps_c = scratch["ps_a"], scratch["ps_b"], scratch["ps_c"]
    ps_tr, ps_trb = scratch["ps_tr"], scratch["ps_trb"]
    ps_row, ps_dec = scratch["ps_row"], scratch["ps_dec"]
    qTh, qTl, vu = scratch["qTh"], scratch["qTl"], scratch["vu"]
    wsh = aw["wshare"]
    hebbT = aw["hebb_bf"]
    x5ch = scratch["x5ch"]
    xlnb = scratch["xlnb"]
    cols = scratch["ncols"]
    srw = scratch["srw"]
    zf = scratch["zf"]
    zh = scratch["zh"]
    zl = scratch["zl"]
    scr = scratch["scr"]
    scr2 = scratch["scr2"]
    bandf = scratch["bandf"]
    bandb = scratch["bandb"]
    bTa, bTb = scratch["bTa"], scratch["bTb"]
    phirow = scratch["phirow"]
    sbf = scratch["sbf"]
    qtm = scratch["qtm"]
    cA = scratch["cA"]
    ub = scratch["ub"]
    uT = scratch["uT"]
    sig = scratch["sig"]
    mdsq = scratch["mdsq"]
    vtmp = scratch["vtmp"]

    emit_ln_stats(pb, xres, scratch["stats"], scratch["rstd"], scratch["nbias"],
                  sems, dummy=xlnb)

    HR = 31
    x5h = scratch["x5h"]
    pb.op("vector", "reduce_sum", out=cols[0:HR, 0:1],
          in_=scratch["xhalo_sb"][0:HR, 0:D], axis=AX.X, sem=sV)
    pb.wait("scalar", sV)
    pb.op("scalar", "activation", out=xlnb[0:HR, 0:D],
          in_=scratch["xhalo_sb"][0:HR, 0:D], func=AF.Square,
          accum_out=cols[0:HR, 1:2], sem=sS)
    pb.wait("vector", sS)
    pb.op("vector", "tensor_scalar", out=cols[0:HR, 0:1], in0=cols[0:HR, 0:1],
          scalar1=1.0 / D, scalar2=None, op0=ALU.mult)
    pb.op("vector", "tensor_tensor", out=cols[0:HR, 2:3], in0=cols[0:HR, 0:1],
          in1=cols[0:HR, 0:1], op=ALU.mult)
    pb.op("vector", "scalar_tensor_tensor", out=cols[0:HR, 2:3],
          in0=cols[0:HR, 1:2], scalar=1.0 / D, in1=cols[0:HR, 2:3],
          op0=ALU.mult, op1=ALU.subtract, sem=sV)
    pb.wait("scalar", sV)
    pb.op("scalar", "activation", out=cols[0:HR, 3:4], in_=cols[0:HR, 2:3],
          func=AF.Ln, bias=pb.cf(1e-5))
    pb.op("scalar", "activation", out=cols[0:HR, 3:4], in_=cols[0:HR, 3:4],
          func=AF.Exp, scale=pb.cf(-0.5), sem=sS)
    pb.wait("vector", sS)
    pb.op("vector", "scalar_tensor_tensor", out=cols[0:HR, 4:5],
          in0=cols[0:HR, 0:1], scalar=-1.0, in1=cols[0:HR, 3:4],
          op0=ALU.mult, op1=ALU.mult)
    pb.op("vector", "tensor_scalar", out=xlnb[0:HR, 0:D],
          in0=scratch["xhalo_sb"][0:HR, 0:D], scalar1=cols[0:HR, 3:4],
          scalar2=cols[0:HR, 4:5], op0=ALU.mult, op1=ALU.add, sem=sV)
    pb.wait("tensor", sV)
    for j2 in range(ND):
        pb.op("tensor", "transpose", ps_tr[:, 0:31],
              xlnb[0:HR, 128 * j2:128 * (j2 + 1)], ident[0:HR, 0:HR], sem=sT)
        pb.wait("vector", sT)
        pb.op("vector", "tensor_copy", out=x5h[:, 32 * j2:32 * j2 + 31],
              in_=ps_tr[:, 0:31], sem=sV)
        pb.wait("tensor", sV)

    def build_x5ch(ck):
        for sub in range(4):
            i2 = 4 * ck + sub
            pb.wait("vector", sT)
            emit_ln_apply(pb, xres[:, D * i2:D * (i2 + 1)], xlnb[:, 0:D],
                          scratch["rstd"][:, i2:i2 + 1],
                          scratch["nbias"][:, i2:i2 + 1], sem=sV)
            pb.wait("tensor", sV)
            for g in range(2):
                pb.wait("tensor", sV)
                for jj in range(4):
                    j3 = 4 * g + jj
                    pb.op("tensor", "transpose", ps_tr[:, 128 * jj:128 * (jj + 1)],
                          xlnb[:, 128 * j3:128 * (j3 + 1)], ident[:],
                          sem=sT if jj == 3 else None)
                pb.wait("vector", sT)
                pb.op("vector", "tensor_copy",
                      out=x5ch.rearrange("p (j t) -> p j t", t=512)
                      [:, 4 * g:4 * g + 4, 128 * sub:128 * (sub + 1)],
                      in_=ps_tr[:].rearrange("p (j t) -> p j t", t=128), sem=sV)
        pb.wait("tensor", sV)

    rows = [scratch["dram"]["encq"][128 * j2:128 * (j2 + 1), :] for j2 in range(ND)]
    dsts = [wsh[:, 1024 * j2:1024 * (j2 + 1)] for j2 in range(ND)]
    attna1_load_wshare(pb, rows, aw, scratch["stage"], dma_sem, sems, dsts)
    pb.wait("tensor", sems["sV"])
    for h in range(8):
        pairs = [(wsh[:, 1024 * j2 + 128 * h:1024 * j2 + 128 * (h + 1)],
                  x5h[:, 32 * j2:32 * j2 + 31]) for j2 in range(ND)]
        emit_mm(pb, ps_a[:, 0:31], pairs, sem=sT)
        pb.wait("vector", sT)
        pb.op("vector", "tensor_scalar", out=zf[:, 0:31], in0=ps_a[:, 0:31],
              scalar1=0.0, scalar2=None, op0=ALU.max)
        pb.op("vector", "tensor_copy", out=qTh[:, U5 * h:U5 * h + 31],
              in_=zf[:, 0:31])
        pb.op("vector", "scalar_tensor_tensor", out=qTl[:, U5 * h:U5 * h + 31],
              in0=qTh[:, U5 * h:U5 * h + 31], scalar=-1.0, in1=zf[:, 0:31],
              op0=ALU.mult, op1=ALU.add, sem=sV)
        pb.wait("tensor", sV)
    for h in range(8):
        pb.op("vector", "memset", qTh[:, U5 * h + 2079:U5 * h + 2080], 0.0)
        pb.op("vector", "memset", qTl[:, U5 * h + 2079:U5 * h + 2080], 0.0,
              sem=sV if h == 7 else None)
    for ck in range(4):
        build_x5ch(ck)
        u0 = 31 + 512 * ck
        for h in range(8):
            pairs = [(wsh[:, 1024 * j2 + 128 * h:1024 * j2 + 128 * (h + 1)],
                      x5ch[:, 512 * j2:512 * (j2 + 1)]) for j2 in range(ND)]
            emit_mm(pb, ps_a[:, 0:512], pairs, sem=sT)
            pb.wait("vector", sT)
            pb.op("vector", "tensor_scalar", out=scratch["rq"][:, 0:512],
                  in0=ps_a[:, 0:512], scalar1=0.0, scalar2=None, op0=ALU.max)
            pb.op("vector", "tensor_copy",
                  out=qTh[:, U5 * h + u0:U5 * h + u0 + 512],
                  in_=scratch["rq"][:, 0:512])
            pb.op("vector", "scalar_tensor_tensor",
                  out=qTl[:, U5 * h + u0:U5 * h + u0 + 512],
                  in0=qTh[:, U5 * h + u0:U5 * h + u0 + 512], scalar=-1.0,
                  in1=scratch["rq"][:, 0:512], op0=ALU.mult, op1=ALU.add, sem=sV)
            pb.wait("tensor", sV)

    pb.op("vector", "memset", vu[:], 0.0, sem=sV)
    pb.wait("sync", sV, pb.cnt[sV])
    rows = [scratch["dram"]["encv"][128 * j2:128 * (j2 + 1), :] for j2 in range(ND)]
    attna1_load_wshare(pb, rows, aw, scratch["stage"], dma_sem, sems, dsts)
    pb.wait("tensor", sems["sV"])
    for dh in range(2):
        pairs = [(x5h[:, 32 * j2:32 * j2 + 31],
                  wsh[:, 1024 * j2 + 512 * dh:1024 * j2 + 512 * (dh + 1)])
                 for j2 in range(ND)]
        emit_mm(pb, ps_b[0:31, 0:512] if dh == 0 else ps_c[0:31, 0:512], pairs,
                sem=sT if dh == 1 else None)
    pb.wait("vector", sT)
    pb.op("vector", "tensor_scalar", out=vtmp[0:31, 0:512], in0=ps_b[0:31, 0:512],
          scalar1=0.0, scalar2=None, op0=ALU.max)
    pb.op("vector", "tensor_scalar", out=vtmp[0:31, 512:1024],
          in0=ps_c[0:31, 0:512], scalar1=0.0, scalar2=None, op0=ALU.max, sem=sV)
    pb.wait("sync", sV)
    pb.op("sync", "dma_start", out=vu[0:31, 0:1024], in_=vtmp[0:31, 0:1024],
          sem=dma_sem, inc=16)
    pb.wait("tensor", dma_sem)
    for ck in range(4):
        build_x5ch(ck)
        for sub in range(4):
            i2 = 4 * ck + sub
            for dh in range(2):
                pairs = [(x5ch[:, 512 * j2 + 128 * sub:512 * j2 + 128 * (sub + 1)],
                          wsh[:, 1024 * j2 + 512 * dh:1024 * j2 + 512 * (dh + 1)])
                         for j2 in range(ND)]
                emit_mm(pb, ps_b[:, 0:512] if dh == 0 else ps_c[:, 0:512], pairs,
                        sem=sT if dh == 1 else None)
            pb.wait("vector", sT)
            pb.op("vector", "tensor_scalar", out=vtmp[:, 0:512],
                  in0=ps_b[:, 0:512], scalar1=0.0, scalar2=None, op0=ALU.max)
            pb.op("vector", "tensor_scalar", out=vtmp[:, 512:1024],
                  in0=ps_c[:, 0:512], scalar1=0.0, scalar2=None, op0=ALU.max,
                  sem=sV)
            pb.wait("tensor", sV)
            pb.wait("sync", sV, pb.cnt[sV])
            pb.wait("sync", dma_sem, pb.cnt[dma_sem])
            pb.op("sync", "dma_start", out=vu[31:128, 1024 * i2:1024 * (i2 + 1)],
                  in_=vtmp[0:97, 0:1024], sem=dma_sem, inc=16)
            pb.wait("sync", dma_sem, pb.cnt[dma_sem])
            pb.op("sync", "dma_start",
                  out=vu[0:31, 1024 * (i2 + 1):1024 * (i2 + 1) + 1024],
                  in_=vtmp[97:128, 0:1024], sem=dma_sem, inc=16)
            pb.wait("vector", dma_sem)
    pb.wait("vector", dma_sem)
    pb.wait("tensor", dma_sem)

    for h in range(8):
        for ck in range(4):
            u0 = 31 + 512 * ck
            pb.op("vector", "tensor_tensor", out=scratch["rq"][:, 0:512],
                  in0=qTh[:, U5 * h + u0:U5 * h + u0 + 512],
                  in1=qTl[:, U5 * h + u0:U5 * h + u0 + 512], op=ALU.add)
            pb.op("vector", "scalar_tensor_tensor", out=scratch["rq"][:, 0:512],
                  in0=scratch["rq"][:, 0:512], scalar=1.0,
                  in1=aw["wvbc"][:, 512 * ck:512 * (ck + 1)],
                  op0=ALU.bypass, op1=ALU.mult,
                  accum_out=sig[:, 8 + (ck % 2):9 + (ck % 2)])
            if ck == 0:
                pb.op("vector", "tensor_copy", out=sig[:, h:h + 1],
                      in_=sig[:, 8:9])
            elif ck == 1:
                pb.op("vector", "tensor_tensor", out=sig[:, h:h + 1],
                      in0=sig[:, h:h + 1], in1=sig[:, 9:10], op=ALU.add)
            else:
                pb.op("vector", "tensor_tensor", out=sig[:, h:h + 1],
                      in0=sig[:, h:h + 1], in1=sig[:, 8 + (ck % 2):9 + (ck % 2)],
                      op=ALU.add, sem=sV if (h == 7 and ck == 3) else None)
    pb.wait("vector", dma_sem)
    if scratch.get("sig_xchg") is not None:
        scratch["sig_xchg"](pb, sig)
    else:
        pb.op("vector", "tensor_tensor", out=sig[:, 0:8], in0=sig[:, 0:8],
              in1=scratch["sigpeer_sb"][:, 0:8], op=ALU.add, sem=sV)

    rows = ([scratch["dram"]["mw1"][128 * j2:128 * (j2 + 1), :] for j2 in range(8)]
            + [scratch["dram"]["mw2"][128 * k:128 * (k + 1), :] for k in range(4)])
    dsts = ([wsh[:, 512 * j2:512 * (j2 + 1)] for j2 in range(8)]
            + [wsh[:, 4096 + 1024 * k:4096 + 1024 * (k + 1)] for k in range(4)])
    attna1_load_wshare(pb, rows, aw, scratch["stage"], dma_sem, sems, dsts)
    mw = wsh
    pb.op("vector", "tensor_copy", out=scratch["sigb"][:, 0:8], in_=sig[:, 0:8],
          sem=sV)
    pb.wait("tensor", sV)
    for mt in range(4):
        pairs = [(mw[:, 512 * j + 128 * mt:512 * j + 128 * (mt + 1)],
                  scratch["sigb"][:, j:j + 1]) for j in range(ND)]
        emit_mm(pb, ps_row[0:128, mt:mt + 1], pairs,
                sem=sT if mt == 3 else None)
    pb.wait("scalar", sT)
    for mt in range(4):
        pb.op("scalar", "activation", out=cols[:, 8 + mt:9 + mt],
              in_=ps_row[0:128, mt:mt + 1], func=AF.Gelu,
              bias=aw["mb1c"][:, mt:mt + 1], sem=sS if mt == 3 else None)
    pb.wait("vector", sS)
    pb.op("vector", "tensor_copy", out=scratch["sigb"][:, 8:12],
          in_=cols[:, 8:12], sem=sV)
    pb.wait("tensor", sV)
    for et in range(8):
        pairs = [(mw[:, 4096 + 1024 * k + 128 * et:
                     4096 + 1024 * k + 128 * (et + 1)],
                  scratch["sigb"][:, 8 + k:9 + k]) for k in range(4)]
        emit_mm(pb, ps_row[0:128, 4 + et:5 + et], pairs,
                sem=sT if et == 7 else None)
    pb.wait("vector", sT)
    pb.op("vector", "scalar_tensor_tensor", out=aw["mb2base"][:, 0:8],
          in0=aw["mb2c"][:, 0:8], scalar=0.1, in1=aw["basec"][:, 0:8],
          op0=ALU.mult, op1=ALU.add)
    pb.op("vector", "scalar_tensor_tensor", out=mdsq[:, 0:8],
          in0=ps_row[0:128, 4:12], scalar=0.1, in1=aw["mb2base"][:, 0:8],
          op0=ALU.mult, op1=ALU.add, sem=sV)
    pb.wait("scalar", sV)
    pb.op("scalar", "activation", out=cols[:, 8:16], in_=mdsq[:, 0:8], func=AF.Abs)
    pb.op("scalar", "activation", out=cols[:, 8:16], in_=cols[:, 8:16],
          func=AF.Exp, scale=-1.0)
    pb.op("scalar", "activation", out=cols[:, 8:16], in_=cols[:, 8:16],
          func=AF.Ln, bias=1.0, sem=sS)
    pb.wait("vector", sS)
    pb.op("vector", "scalar_tensor_tensor", out=mdsq[:, 0:8], in0=mdsq[:, 0:8],
          scalar=0.0, in1=cols[:, 8:16], op0=ALU.max, op1=ALU.add)
    pb.op("vector", "tensor_scalar", out=mdsq[:, 0:8], in0=mdsq[:, 0:8],
          scalar1=1e-6, scalar2=None, op0=ALU.add, sem=sV)
    pb.wait("scalar", sV)
    pb.op("scalar", "activation", out=mdsq[:, 0:8], in_=mdsq[:, 0:8],
          func=AF.Ln)
    pb.op("scalar", "activation", out=mdsq[:, 0:8], in_=mdsq[:, 0:8],
          func=AF.Exp, scale=pb.cf(0.5), sem=sS)
    pb.wait("vector", sS)

    rows = [scratch["dram"]["decoder"][128 * j2:128 * (j2 + 1), :] for j2 in range(ND)]
    dsts = [wsh[:, 1024 * j2:1024 * (j2 + 1)] for j2 in range(ND)]
    attna1_load_wshare(pb, rows, aw, scratch["stage"], dma_sem, sems, dsts)
    pb.wait("tensor", sems["sV"])
    for i in range(NT):
        w0 = 128 * i
        for h in range(8):
            q0 = U5 * h + w0
            pb.op("vector", "tensor_tensor", out=srw[:], in0=qTh[:, q0:q0 + 160],
                  in1=qTl[:, q0:q0 + 160], op=ALU.add)
            pb.op("vector", "tensor_scalar", out=zf[:], in0=srw[:],
                  scalar1=mdsq[:, h:h + 1], scalar2=None, op0=ALU.mult)
            pb.op("vector", "tensor_copy", out=zh[:], in_=zf[:])
            pb.op("vector", "scalar_tensor_tensor", out=zl[:], in0=zh[:],
                  scalar=-1.0, in1=zf[:], op0=ALU.mult, op1=ALU.add)
            pb.op("vector", "tensor_copy", out=sbf[:], in_=srw[:, 31:159],
                  sem=sV)
            pb.op("vector", "tensor_tensor", out=scr[:], in0=zh[:], in1=zh[:],
                  op=ALU.mult)
            pb.op("vector", "scalar_tensor_tensor", out=scr2[:], in0=zl[:],
                  scalar=2.0, in1=zh[:], op0=ALU.mult, op1=ALU.mult)
            pb.op("vector", "tensor_tensor", out=scr[:], in0=scr[:], in1=scr2[:],
                  op=ALU.add, sem=sV)
            pb.wait("tensor", sV)
            pb.op("tensor", "matmul", ps_b[:, 0:160], zh[:, 31:159], zh[:],
                  start=True, stop=False)
            pb.op("tensor", "matmul", ps_b[:, 0:160], zh[:, 31:159], zl[:],
                  start=False, stop=False)
            pb.op("tensor", "matmul", ps_b[:, 0:160], zl[:, 31:159], zh[:],
                  start=False, stop=True, sem=sT)
            pb.op("tensor", "matmul", ps_row[0:1, 0:160], aw["ones_col"][:, 0:1],
                  scr[:], start=True, stop=True)
            pb.op("vector", "tensor_tensor", out=scr2[:, 0:128],
                  in0=srw[:, 31:159], in1=srw[:, 31:159], op=ALU.mult, sem=sV)
            pb.wait("tensor", sV)
            pb.op("tensor", "matmul", ps_row[0:1, 160:288], aw["ones_col"][:, 0:1],
                  scr2[:, 0:128], start=True, stop=True, sem=sT)
            pb.wait("vector", sT)
            pb.op("vector", "tensor_copy", out=phirow[0:1, 0:160],
                  in_=ps_row[0:1, 0:160])
            pb.op("vector", "tensor_copy", out=scratch["s2row"][0:1, 0:128],
                  in_=ps_row[0:1, 160:288], sem=sV)
            pb.wait("tensor", sV)
            pb.op("tensor", "transpose", ps_tr[:, 0:1], phirow[0:1, 31:159],
                  ident[0:1, 0:1])
            pb.op("tensor", "transpose", ps_tr[:, 1:2], scratch["s2row"][0:1, 0:128],
                  ident[0:1, 0:1], sem=sT)
            pb.op("tensor", "matmul", ps_c[:, 0:160], aw["ones_row"][0:1, :],
                  phirow[0:1, 0:160], start=True, stop=True, sem=sT)
            pb.wait("vector", sT)
            pb.op("vector", "tensor_copy", out=cols[:, 8:10], in_=ps_tr[:, 0:2],
                  sem=sV)
            pb.wait("scalar", sV)
            pb.op("scalar", "activation", out=cols[:, 10:11], in_=cols[:, 9:10],
                  func=AF.Ln, bias=pb.cf(1e-30))
            pb.op("scalar", "activation", out=cols[:, 10:11], in_=cols[:, 10:11],
                  func=AF.Exp, scale=pb.cf(0.5), sem=sS)
            pb.wait("vector", sS)
            pb.op("vector", "tensor_scalar", out=cols[:, 10:11],
                  in0=cols[:, 10:11], scalar1=1e-12, scalar2=None, op0=ALU.max)
            pb.op("vector", "reciprocal", out=cols[:, 10:11], in_=cols[:, 10:11])
            pb.op("vector", "tensor_scalar", out=bandf[:], in0=ps_b[:, 0:160],
                  scalar1=-2.0, scalar2=cols[:, 8:9], op0=ALU.mult, op1=ALU.add)
            pb.op("vector", "tensor_tensor", out=bandf[:], in0=bandf[:],
                  in1=ps_c[:, 0:160], op=ALU.add)
            pb.op("vector", "tensor_scalar", out=bandf[:], in0=bandf[:],
                  scalar1=0.0, scalar2=1e-8, op0=ALU.max, op1=ALU.add, sem=sV)
            pb.wait("scalar", sV)
            pb.op("scalar", "activation", out=bandf[:], in_=bandf[:], func=AF.Ln)
            pb.op("scalar", "activation", out=bandf[:], in_=bandf[:],
                  func=AF.Exp, scale=pb.cf(0.5))
            pb.op("scalar", "activation", out=bandf[:], in_=bandf[:],
                  func=AF.Exp, scale=-1.0, sem=sS)
            pb.wait("vector", sS)
            pb.op("vector", "scalar_tensor_tensor", out=bandf[:], in0=bandf[:],
                  scalar=1.0, in1=aw["bandmask"][:], op0=ALU.bypass, op1=ALU.mult,
                  accum_out=cols[:, 11:12])
            pb.op("vector", "tensor_scalar", out=cols[:, 11:12],
                  in0=cols[:, 11:12], scalar1=1e-8, scalar2=None, op0=ALU.add)
            pb.op("vector", "reciprocal", out=cols[:, 12:13], in_=cols[:, 11:12])
            pb.op("vector", "tensor_scalar", out=bandb[:], in0=bandf[:],
                  scalar1=cols[:, 12:13], scalar2=None, op0=ALU.mult, sem=sV)
            pb.wait("tensor", sV)
            pb.op("tensor", "transpose", ps_trb[:, 0:128], bandb[:, 0:128],
                  identb[:])
            pb.op("tensor", "transpose", ps_trb[0:32, 128:256], bandb[:, 128:160],
                  identb[:])
            pb.op("tensor", "transpose", ps_trb[:, 256:384], sbf[:], identb[:],
                  sem=sT)
            pb.wait("vector", sT)
            pb.op("vector", "tensor_copy", out=bTa[:], in_=ps_trb[:, 0:128])
            pb.op("vector", "tensor_copy", out=bTb[0:32, 0:128],
                  in_=ps_trb[0:32, 128:256])
            pb.op("vector", "tensor_copy", out=qtm[:], in_=ps_trb[:, 256:384],
                  sem=sV)
            pb.wait("tensor", sV)
            pb.op("tensor", "matmul", ps_c[:, 0:128], bTa[:],
                  vu[:, 1024 * i + 128 * h:1024 * i + 128 * (h + 1)],
                  start=True, stop=False)
            pb.op("tensor", "matmul", ps_c[:, 0:128], bTb[0:32, 0:128],
                  vu[0:32, 1024 * (i + 1) + 128 * h:1024 * (i + 1) + 128 * (h + 1)],
                  start=False, stop=True)
            pb.op("tensor", "matmul", ps_c[:, 128:256], sbf[:],
                  hebbT[:, 128 * h:128 * (h + 1)], start=True, stop=True, sem=sT)
            pb.wait("vector", sT)
            pb.op("vector", "tensor_copy", out=cA[:], in_=ps_c[:, 0:128])
            pb.op("vector", "scalar_tensor_tensor", out=cA[:],
                  in0=ps_c[:, 128:256], scalar=cols[:, 10:11], in1=cA[:],
                  op0=ALU.mult, op1=ALU.add)
            pb.op("vector", "tensor_tensor", out=ub[:], in0=cA[:], in1=qtm[:],
                  op=ALU.mult, sem=sV)
            pb.wait("tensor", sV)
            pb.op("tensor", "transpose", ps_trb[:, 384:512], ub[:], identb[:],
                  sem=sT)
            pb.wait("vector", sT)
            pb.op("vector", "tensor_copy", out=uT[:, 128 * h:128 * (h + 1)],
                  in_=ps_trb[:, 384:512], sem=sV)
            pb.wait("tensor", sV)
        for dh in range(2):
            pairs = [(uT[:, 128 * h:128 * (h + 1)],
                      wsh[:, 1024 * h + 512 * dh:1024 * h + 512 * (dh + 1)])
                     for h in range(8)]
            emit_mm(pb, ps_dec[:, 512 * dh:512 * (dh + 1)], pairs,
                    sem=sT if dh == 1 else None)
        pb.wait("vector", sT)
        pb.op("vector", "tensor_tensor", out=xres[:, D * i:D * (i + 1)],
              in0=xres[:, D * i:D * (i + 1)], in1=ps_dec[:, 0:1024],
              op=ALU.add, sem=sV)
        pb.wait("tensor", sV)


def alloc_attna1(pb, tag=""):
    aw = {
        "wshare": pb.sbuf("a5wsh" + tag, [128, ND * 1024], BF16),
        "hebb_bf": pb.sbuf("a5hebb" + tag, [128, 8 * 128], BF16),
        "mb1c": pb.sbuf("a5mb1" + tag, [128, 4], F32),
        "mb2base": pb.sbuf("a5mb2" + tag, [128, 8], F32),
        "mb2c": pb.sbuf("a5mb2c" + tag, [128, 8], F32),
        "basec": pb.sbuf("a5base" + tag, [128, 8], F32),
        "wvbc": pb.sbuf("a5wv" + tag, [128, 2048], BF16),
        "bandmask": pb.sbuf("a5bm" + tag, [128, 160], F32),
        "ones_col": pb.sbuf("a5oc" + tag, [128, 1], F32),
        "ones_row": pb.sbuf("a5or" + tag, [1, 128], F32),
    }
    scratch = {
        "qTh": pb.sbuf("a5qTh" + tag, [128, 8 * U5], BF16),
        "qTl": pb.sbuf("a5qTl" + tag, [128, 8 * U5], F8E5),
        "vu": pb.sbuf("a5vu" + tag, [128, 17 * 1024], BF16),
        "x5ch": pb.sbuf("a5x5ch" + tag, [128, ND * 512], BF16),
        "x5h": pb.sbuf("a5x5h" + tag, [128, ND * 32], BF16),
        "xlnb": pb.sbuf("a5xlnb" + tag, [128, D], F32),
        "xhalo_sb": pb.sbuf("a5xhalo" + tag, [31, D], F32),
        "sigpeer_sb": pb.sbuf("a5sigp" + tag, [128, 8], F32),
        "ncols": pb.sbuf("a5cols" + tag, [128, 16], F32),
        "srw": pb.sbuf("a5srw" + tag, [128, 160], F32),
        "zf": pb.sbuf("a5zf" + tag, [128, 160], F32),
        "zh": pb.sbuf("a5zh" + tag, [128, 160], BF16),
        "zl": pb.sbuf("a5zl" + tag, [128, 160], BF16),
        "scr": pb.sbuf("a5scr" + tag, [128, 160], F32),
        "scr2": pb.sbuf("a5scr2" + tag, [128, 160], F32),
        "bandf": pb.sbuf("a5bandf" + tag, [128, 160], F32),
        "bandb": pb.sbuf("a5bandb" + tag, [128, 160], BF16),
        "bTa": pb.sbuf("a5bTa" + tag, [128, 128], BF16),
        "bTb": pb.sbuf("a5bTb" + tag, [32, 128], BF16),
        "phirow": pb.sbuf("a5phir" + tag, [1, 160], F32),
        "s2row": pb.sbuf("a5s2r" + tag, [1, 128], F32),
        "sbf": pb.sbuf("a5sbf" + tag, [128, 128], BF16),
        "qtm": pb.sbuf("a5qtm" + tag, [128, 128], BF16),
        "cA": pb.sbuf("a5cA" + tag, [128, 128], F32),
        "ub": pb.sbuf("a5ub" + tag, [128, 128], BF16),
        "uT": pb.sbuf("a5uT" + tag, [128, 8 * 128], BF16),
        "sig": pb.sbuf("a5sig" + tag, [128, 12], F32),
        "sigb": pb.sbuf("a5sigb" + tag, [128, 12], BF16),
        "mdsq": pb.sbuf("a5mdsq" + tag, [128, 8], F32),
        "vtmp": pb.sbuf("a5vtmp" + tag, [128, 1024], BF16),
        "rq": pb.sbuf("a5rq" + tag, [128, 512], F32),
        "stats": pb.sbuf("a5stats" + tag, [128, 4 * NT], F32),
        "rstd": pb.sbuf("a5rstd" + tag, [128, NT], F32),
        "nbias": pb.sbuf("a5nbias" + tag, [128, NT], F32),
    }
    return aw, scratch


def attna1_load_wshare(pb, dram_rows, aw, stage, dma_sem, sems, dsts):
    pb.wait("sync", sems["sT"], pb.cnt[sems["sT"]])
    emit_load_cast(pb, dram_rows, stage, dsts, dma_sem, sems)


def attna1_weight_prep(pb, dram, aw, stage, dma_sem, sems):
    rows = [dram["hebbc"][128 * h:128 * (h + 1), :] for h in range(8)]
    dsts = [aw["hebb_bf"][:, 128 * h:128 * (h + 1)] for h in range(8)]
    emit_load_cast(pb, rows, stage, dsts, dma_sem, sems)
    for nm, key in (("mb1c", "mb1c"), ("mb2c", "mb2c"), ("basec", "basec"),
                    ("wvbc", "wvbc"), ("bandmask", "bandmask"),
                    ("onesc", "ones_col"), ("onesr", "ones_row")):
        ap = aw[key][0:1, :] if key == "ones_row" else aw[key][:]
        pb.op("sync", "dma_start", out=ap, in_=dram[nm][:], sem=dma_sem, inc=16)


def build_attna1_program():
    nc = bass.Bass("TRN2", target_bir_lowering=False, debug=False)
    pb = PB(nc)
    dram = {
        "xres": nc.dram_tensor("xres", [TOK, D], F32, kind="ExternalInput").ap(),
        "xhalo": nc.dram_tensor("xhalo", [31, D], F32, kind="ExternalInput").ap(),
        "sigpeer": nc.dram_tensor("sigpeer", [128, 8], F32, kind="ExternalInput").ap(),
        "encq": nc.dram_tensor("encq", [D, 1024], F32, kind="ExternalInput").ap(),
        "encv": nc.dram_tensor("encv", [D, 1024], F32, kind="ExternalInput").ap(),
        "decoder": nc.dram_tensor("decoder", [1024, D], F32, kind="ExternalInput").ap(),
        "mw1": nc.dram_tensor("mw1", [1024, 512], F32, kind="ExternalInput").ap(),
        "mw2": nc.dram_tensor("mw2", [512, 1024], F32, kind="ExternalInput").ap(),
        "hebbc": nc.dram_tensor("hebbc", [1024, 128], F32, kind="ExternalInput").ap(),
        "mb1c": nc.dram_tensor("mb1c", [128, 4], F32, kind="ExternalInput").ap(),
        "mb2c": nc.dram_tensor("mb2c", [128, 8], F32, kind="ExternalInput").ap(),
        "basec": nc.dram_tensor("basec", [128, 8], F32, kind="ExternalInput").ap(),
        "wvbc": nc.dram_tensor("wvbc", [128, 2048], BF16, kind="ExternalInput").ap(),
        "bandmask": nc.dram_tensor("bandmask", [128, 160], F32, kind="ExternalInput").ap(),
        "onesc": nc.dram_tensor("onesc", [128, 1], F32, kind="ExternalInput").ap(),
        "onesr": nc.dram_tensor("onesr", [1, 128], F32, kind="ExternalInput").ap(),
        "ident": nc.dram_tensor("ident", [128, 128], F32, kind="ExternalInput").ap(),
        "identb": nc.dram_tensor("identb", [128, 128], BF16, kind="ExternalInput").ap(),
        "xout": nc.dram_tensor("xout", [TOK, D], F32, kind="ExternalOutput").ap(),
        "sig_out": nc.dram_tensor("sig_out", [128, 8], F32, kind="ExternalOutput").ap(),
    }
    aw, scratch = alloc_attna1(pb)
    scratch["ident"] = pb.sbuf("identsb", [128, 128], F32)
    scratch["identb"] = pb.sbuf("identbsb", [128, 128], BF16)
    scratch["ps_a"] = pb.psum("ps_a", [128, 512], F32)
    scratch["ps_b"] = pb.psum("ps_b", [128, 512], F32)
    scratch["ps_c"] = pb.psum("ps_c", [128, 512], F32)
    scratch["ps_dec"] = pb.psum("ps_dec", [128, 1024], F32)
    scratch["ps_tr"] = pb.psum("ps_tr", [128, 512], F32)
    scratch["ps_trb"] = pb.psum("ps_trb", [128, 512], BF16)
    scratch["ps_row"] = pb.psum("ps_row", [128, 512], F32)
    xres = pb.sbuf("xres_sb", [128, NT * D], F32)
    stage = pb.sbuf("stage", [128, 1024], F32)
    dma_sem = pb.sem("dma")
    sems = {"sV": pb.sem("sV"), "sS": pb.sem("sS"), "sT": pb.sem("sT"),
            "sG": pb.sem("sG")}
    st = pb.sem("st")

    pb.op("sync", "dma_start", out=xres.rearrange("p (i d) -> p i d", d=D),
          in_=dram["xres"].rearrange("(i p) d -> p i d", p=128),
          sem=dma_sem, inc=16)
    pb.op("sync", "dma_start", out=scratch["xhalo_sb"][:], in_=dram["xhalo"][:],
          sem=dma_sem, inc=16)
    pb.op("sync", "dma_start", out=scratch["sigpeer_sb"][:], in_=dram["sigpeer"][:],
          sem=dma_sem, inc=16)
    pb.op("sync", "dma_start", out=scratch["ident"][:], in_=dram["ident"][:],
          sem=dma_sem, inc=16)
    pb.op("sync", "dma_start", out=scratch["identb"][:], in_=dram["identb"][:],
          sem=dma_sem, inc=16)
    attna1_weight_prep(pb, dram, aw, stage, dma_sem, sems)
    pb.wait("vector", dma_sem)
    pb.wait("scalar", dma_sem)
    pb.wait("tensor", dma_sem)

    scratch["dram"] = dram
    scratch["stage"] = stage
    emit_attna1(pb, xres, aw, scratch, sems, dma_sem)

    pb.wait("sync", sems["sV"])
    pb.op("sync", "dma_start",
          out=dram["xout"].rearrange("(i p) d -> p i d", p=128),
          in_=xres.rearrange("p (i d) -> p i d", d=D), sem=st, inc=16)
    pb.op("sync", "dma_start", out=dram["sig_out"][:], in_=scratch["sig"][:, 0:8],
          sem=st, inc=16)
    pb.wait("sync", st)
    return pb.build()


def host_attna1_consts():
    wv = np.full((2048,), 1.0 / 4096, np.float32)
    wv[::8] += 0.5 / 512
    import ml_dtypes
    wvbc = np.broadcast_to(wv, (128, 2048)).astype(ml_dtypes.bfloat16)
    r = np.arange(128)[:, None]
    c = np.arange(160)[None, :]
    bandmask = (((c - r) >= 0) & ((c - r) <= 31)).astype(np.float32)
    return np.ascontiguousarray(wvbc), np.ascontiguousarray(bandmask)




CH1 = 512
CH2 = 256
TFULL = 4096


def emit_mamba_phase1(pb, dram, mw, scratch, sems, dma_sem):
    sV, sS, sT = sems["sV"], sems["sS"], sems["sT"]
    ident = scratch["ident"]
    ps_q, ps_xd, ps_tr = scratch["ps_q"], scratch["ps_xd"], scratch["ps_tr"]
    xch, xlnb, xlnT = scratch["xch"], scratch["xlnb"], scratch["xlnT"]
    xir = scratch["xi_roll"]
    xc = scratch["xc"]
    xdP = scratch["xdT_f32"]
    stats = scratch["stats"]
    rstd, nbias = scratch["rstd"], scratch["nbias"]
    W = 3 + CH1
    for j in range(ND):
        pb.op("vector", "memset", xir[:, W * j:W * j + 3], 0.0,
              sem=sV if j == ND - 1 else None)
    for ck in range(TFULL // CH1):
        for s4 in range(4):
            i = 4 * ck + s4
            if s4 % 2 == 0:
                pb.wait("sync", sT, pb.cnt[sT])
                pb.wait("sync", sV, pb.cnt[sV])
                pb.dma(xch.rearrange("p (i d) -> p i d", d=D),
                       dram["xfull"].rearrange("(i p) d -> p i d", p=128)
                       [:, i:i + 2, :], dma_sem)
                pb.wait("vector", dma_sem)
            sx = s4 % 2
            for hh in range(2):
                pb.op("vector", "reduce_sum", out=stats[:, 0:1] if hh == 0 else stats[:, 1:2],
                      in_=xch[:, D * sx + 512 * hh:D * sx + 512 * (hh + 1)],
                      axis=AX.X)
            pb.op("vector", "tensor_tensor", out=stats[:, 0:1], in0=stats[:, 0:1],
                  in1=stats[:, 1:2], op=ALU.add, sem=sV)
            pb.wait("scalar", sV)
            pb.op("scalar", "activation", out=xlnb[:, 0:D],
                  in_=xch[:, D * sx:D * (sx + 1)], func=AF.Square,
                  accum_out=stats[:, 2:3], sem=sS)
            pb.wait("vector", sS)
            pb.op("vector", "tensor_scalar", out=stats[:, 3:4], in0=stats[:, 0:1],
                  scalar1=1.0 / D, scalar2=None, op0=ALU.mult)
            pb.op("vector", "tensor_tensor", out=stats[:, 4:5], in0=stats[:, 3:4],
                  in1=stats[:, 3:4], op=ALU.mult)
            pb.op("vector", "scalar_tensor_tensor", out=stats[:, 4:5],
                  in0=stats[:, 2:3], scalar=1.0 / D, in1=stats[:, 4:5],
                  op0=ALU.mult, op1=ALU.subtract, sem=sV)
            pb.wait("scalar", sV)
            pb.op("scalar", "activation", out=rstd[:, i:i + 1], in_=stats[:, 4:5],
                  func=AF.Ln, bias=pb.cf(1e-5))
            pb.op("scalar", "activation", out=rstd[:, i:i + 1], in_=rstd[:, i:i + 1],
                  func=AF.Exp, scale=pb.cf(-0.5), sem=sS)
            pb.wait("vector", sS)
            pb.op("vector", "scalar_tensor_tensor", out=nbias[:, i:i + 1],
                  in0=stats[:, 3:4], scalar=-1.0, in1=rstd[:, i:i + 1],
                  op0=ALU.mult, op1=ALU.mult, sem=sV)
            emit_ln_apply(pb, xch[:, D * sx:D * (sx + 1)], xlnb[:, 0:D],
                          rstd[:, i:i + 1], nbias[:, i:i + 1], sem=sV)
            pb.wait("tensor", sV)
            for g in range(2):
                pb.wait("tensor", sV)
                for jj in range(4):
                    j = 4 * g + jj
                    pb.op("tensor", "transpose", ps_tr[:, 128 * jj:128 * (jj + 1)],
                          xlnb[:, 128 * j:128 * (j + 1)], ident[:],
                          sem=sT if jj == 3 else None)
                pb.wait("vector", sT)
                pb.op("vector", "tensor_copy",
                      out=xlnT.rearrange("p (j t) -> p j t", t=CH1)
                      [:, 4 * g:4 * g + 4, 128 * s4:128 * (s4 + 1)],
                      in_=ps_tr[:].rearrange("p (j t) -> p j t", t=128), sem=sV)
        pb.wait("tensor", sV)
        for j in range(ND):
            pairs = [(mw["inprojT_xi"][:, 1024 * k + 128 * j:1024 * k + 128 * (j + 1)],
                      xlnT[:, CH1 * k:CH1 * (k + 1)]) for k in range(ND)]
            emit_mm(pb, ps_q[:, 0:CH1], pairs, sem=sT)
            pb.wait("vector", sT)
            pb.op("vector", "tensor_copy", out=xir[:, W * j + 3:W * j + 3 + CH1],
                  in_=ps_q[:, 0:CH1], sem=sV)
            pb.wait("tensor", sV)
            pb.op("vector", "tensor_scalar", out=scratch["xlnb"][:, 0:CH1],
                  in0=xir[:, W * j + 0:W * j + CH1],
                  scalar1=mw["cwc"][:, 4 * j:4 * j + 1], scalar2=None, op0=ALU.mult)
            for k in range(1, 4):
                pb.op("vector", "scalar_tensor_tensor", out=scratch["xlnb"][:, 0:CH1],
                      in0=xir[:, W * j + k:W * j + k + CH1],
                      scalar=mw["cwc"][:, 4 * j + k:4 * j + k + 1],
                      in1=scratch["xlnb"][:, 0:CH1], op0=ALU.mult, op1=ALU.add,
                      sem=sV if k == 3 else None)
            pb.wait("scalar", sV)
            pb.op("scalar", "activation", out=xc[:, TFULL * j + CH1 * ck:TFULL * j + CH1 * (ck + 1)],
                  in_=scratch["xlnb"][:, 0:CH1], func=AF.Silu,
                  bias=mw["cbc"][:, j:j + 1], sem=sS)
            pb.wait("vector", sS)
            pb.op("vector", "tensor_copy", out=xir[:, W * j:W * j + 3],
                  in_=xir[:, W * j + CH1:W * j + CH1 + 3], sem=sV)
        pb.wait("tensor", sS)
        pairs = [(mw["xprojT"][:, 96 * j:96 * (j + 1)],
                  xc[:, TFULL * j + CH1 * ck:TFULL * j + CH1 * (ck + 1)])
                 for j in range(ND)]
        emit_mm(pb, ps_xd[0:96, 0:CH1], pairs, sem=sT)
        pb.wait("vector", sT)
        pb.op("vector", "tensor_copy", out=xdP[0:96, CH1 * ck:CH1 * (ck + 1)],
              in_=ps_xd[0:96, 0:CH1], sem=sV)
        pb.wait("tensor", sV)


def emit_mamba_phase2(pb, dram, mw, scratch, sems, dma_sem, peer_both=None):
    sV, sS, sT = sems["sV"], sems["sS"], sems["sT"]
    ident = scratch["ident"]
    ps_q, ps_xd, ps_tr = scratch["ps_q"], scratch["ps_xd"], scratch["ps_tr"]
    ps_bc, ps_o = scratch["ps_bc"], scratch["ps_o"]
    xch, xlnb, xlnT = scratch["xch"], scratch["xlnb"], scratch["xlnT"]
    xc = scratch["xc"]
    xdP = scratch["xdT_f32"]
    xdb = scratch["xdT_bf"]
    rstd, nbias = scratch["rstd"], scratch["nbias"]
    dtT, sz = scratch["dtT"], scratch["sz"]
    wch, ybuf, scr = scratch["wch"], scratch["ybuf"], scratch["scr2"]
    dA, hbuf = scratch["dA"], scratch["hbuf"]
    gbuf = scratch["gbuf"]
    hcarry = scratch["hcarry"]
    dbx = scratch["dbx"]
    Acols, Dpc = scratch["Acols"], mw["Dpc"]

    if peer_both is None:
        for hh in range(2):
            pb.wait("sync", sV, pb.cnt[sV])
            pb.dma(xch[0:96, 0:2048],
                   dram["xdbl_peer"][0:96, 2048 * hh:2048 * (hh + 1)], dma_sem)
            pb.wait("vector", dma_sem)
            pb.op("vector", "tensor_tensor",
                  out=xdP[0:96, 2048 * hh:2048 * (hh + 1)],
                  in0=xdP[0:96, 2048 * hh:2048 * (hh + 1)],
                  in1=xch[0:96, 0:2048], op=ALU.add, sem=sV)
    else:
        secA, secB = peer_both
        for qq in range(4):
            pb.wait("sync", sV, pb.cnt[sV])
            pb.dma(xch[0:96, 0:1024], secA[0:96, 1024 * qq:1024 * (qq + 1)],
                   dma_sem)
            pb.dma(xch[0:96, 1024:2048], secB[0:96, 1024 * qq:1024 * (qq + 1)],
                   dma_sem)
            pb.wait("vector", dma_sem)
            pb.op("vector", "tensor_tensor",
                  out=xdP[0:96, 1024 * qq:1024 * (qq + 1)],
                  in0=xch[0:96, 0:1024], in1=xch[0:96, 1024:2048],
                  op=ALU.add, sem=sV)
    pb.op("vector", "tensor_copy", out=xdb[0:96, :], in_=xdP[0:96, :], sem=sV)
    pb.wait("scalar", sV)
    pb.op("scalar", "activation", out=Acols[:], in_=mw["alogc"][:], func=AF.Exp,
          sem=sS)
    pb.wait("vector", sS)
    pb.op("vector", "tensor_scalar", out=Acols[:], in0=Acols[:], scalar1=-1.0,
          scalar2=None, op0=ALU.mult, sem=sV)
    pb.wait("scalar", sV)
    pb.wait("tensor", sV)

    NC2 = TFULL // CH2
    for ck in range(NC2):
        pb.wait("sync", sT, pb.cnt[sT])
        pb.wait("sync", sV, pb.cnt[sV])
        pb.dma(xch.rearrange("p (i d) -> p i d", d=D)[:, 0:2, :],
               dram["xfull"].rearrange("(i p) d -> p i d", p=128)
               [:, 2 * ck:2 * (ck + 1), :], dma_sem)
        pb.wait("vector", dma_sem)
        for s2 in range(2):
            i = 2 * ck + s2
            emit_ln_apply(pb, xch[:, D * s2:D * (s2 + 1)], xlnb[:, 0:D],
                          rstd[:, i:i + 1], nbias[:, i:i + 1], sem=sV)
            pb.wait("tensor", sV)
            for g in range(2):
                pb.wait("tensor", sV)
                for jj in range(4):
                    j = 4 * g + jj
                    pb.op("tensor", "transpose", ps_tr[:, 128 * jj:128 * (jj + 1)],
                          xlnb[:, 128 * j:128 * (j + 1)], ident[:],
                          sem=sT if jj == 3 else None)
                pb.wait("vector", sT)
                pb.op("vector", "tensor_copy",
                      out=xlnT.rearrange("p (j t) -> p j t", t=CH1)
                      [:, 4 * g:4 * g + 4, 128 * s2:128 * (s2 + 1)],
                      in_=ps_tr[:].rearrange("p (j t) -> p j t", t=128), sem=sV)
        pb.wait("tensor", sV)
        t0 = CH2 * ck
        for j in range(ND):
            pairs = [(mw["inprojT_z"][:, 1024 * k + 128 * j:1024 * k + 128 * (j + 1)],
                      xlnT[:, CH1 * k:CH1 * k + CH2]) for k in range(ND)]
            emit_mm(pb, ps_q[:, 0:CH2], pairs, sem=sT)
            pb.wait("scalar", sT)
            pb.op("scalar", "activation", out=sz[:, 0:CH2], in_=ps_q[:, 0:CH2],
                  func=AF.Silu, sem=sS)
            pb.wait("tensor", sS)
            pb.op("tensor", "matmul", ps_q[:, 256:256 + CH2],
                  mw["dtPT"][0:64, 128 * j:128 * (j + 1)],
                  xdb[0:64, t0:t0 + CH2], start=True, stop=True, sem=sT)
            pb.wait("scalar", sT)
            pb.op("scalar", "activation", out=dtT[:, 0:CH2],
                  in_=ps_q[:, 256:256 + CH2], func=AF.Abs,
                  bias=mw["dtbc"][:, j:j + 1])
            pb.op("scalar", "activation", out=dtT[:, 0:CH2], in_=dtT[:, 0:CH2],
                  func=AF.Exp, scale=-1.0)
            pb.op("scalar", "activation", out=dtT[:, 0:CH2], in_=dtT[:, 0:CH2],
                  func=AF.Ln, bias=1.0, sem=sS)
            pb.wait("vector", sS)
            pb.op("vector", "tensor_scalar", out=scr[:, 0:CH2],
                  in0=ps_q[:, 256:256 + CH2], scalar1=mw["dtbc"][:, j:j + 1],
                  scalar2=0.0, op0=ALU.add, op1=ALU.max)
            pb.op("vector", "tensor_tensor", out=dtT[:, 0:CH2], in0=dtT[:, 0:CH2],
                  in1=scr[:, 0:CH2], op=ALU.add)
            pb.op("vector", "tensor_tensor", out=wch[:, 0:CH2], in0=dtT[:, 0:CH2],
                  in1=xc[:, TFULL * j + t0:TFULL * j + t0 + CH2], op=ALU.mult,
                  sem=sV)
            pb.wait("scalar", sV)
            pb.wait("tensor", sV)
            for s in range(16):
                pb.wait("scalar", sV)
                pb.op("scalar", "activation", out=dA[:, 0:CH2],
                      in_=dtT[:, 0:CH2], func=AF.Exp,
                      scale=Acols[:, 16 * j + s:16 * j + s + 1], sem=sS)
                pb.wait("vector", sS)
                pb.op("tensor", "matmul", ps_bc[:, 0:CH2],
                      mw["oneh"][0:96, 128 * s:128 * (s + 1)],
                      xdb[0:96, t0:t0 + CH2], start=True, stop=True, sem=sT)
                pb.wait("vector", sT)
                pb.op("vector", "tensor_tensor", out=dbx[:, 0:CH2],
                      in0=wch[:, 0:CH2], in1=ps_bc[:, 0:CH2], op=ALU.mult,
                      sem=sV)
                ini = 0.0 if ck == 0 else hcarry[:, 16 * j + s:16 * j + s + 1]
                pb.op("vector", "tensor_tensor_scan",
                      out=hbuf[:, CH2 * s:CH2 * (s + 1)],
                      data0=dA[:, 0:CH2], data1=dbx[:, 0:CH2],
                      initial=ini, op0=ALU.mult, op1=ALU.add, sem=sV)
                pb.wait("tensor", sV, pb.cnt[sV] - 1)
                pb.op("tensor", "matmul", ps_bc[:, 256:256 + CH2],
                      mw["oneh"][0:96, 2048 + 128 * s:2048 + 128 * (s + 1)],
                      xdb[0:96, t0:t0 + CH2], start=True, stop=True, sem=sT)
                pb.wait("vector", sT)
                if s == 0:
                    pb.op("vector", "tensor_tensor", out=ybuf[:, 0:CH2],
                          in0=hbuf[:, CH2 * s:CH2 * (s + 1)],
                          in1=ps_bc[:, 256:256 + CH2], op=ALU.mult, sem=sV)
                else:
                    pb.op("vector", "tensor_tensor", out=scr[:, 0:CH2],
                          in0=hbuf[:, CH2 * s:CH2 * (s + 1)],
                          in1=ps_bc[:, 256:256 + CH2], op=ALU.mult)
                    pb.op("vector", "tensor_tensor", out=ybuf[:, 0:CH2],
                          in0=ybuf[:, 0:CH2], in1=scr[:, 0:CH2], op=ALU.add,
                          sem=sV)
                pb.wait("tensor", sV)
            pb.op("vector", "tensor_copy",
                  out=hcarry[:, 16 * j:16 * (j + 1)].rearrange("p (s o) -> p s o", o=1),
                  in_=hbuf.rearrange("p (s t) -> p s t", t=CH2)[:, :, CH2 - 1:CH2],
                  sem=sV)
            pb.op("vector", "scalar_tensor_tensor", out=scr[:, 0:CH2],
                  in0=xc[:, TFULL * j + t0:TFULL * j + t0 + CH2],
                  scalar=Dpc[:, j:j + 1], in1=ybuf[:, 0:CH2],
                  op0=ALU.mult, op1=ALU.add)
            pb.op("vector", "tensor_tensor", out=gbuf[:, CH2 * j:CH2 * (j + 1)],
                  in0=scr[:, 0:CH2], in1=sz[:, 0:CH2], op=ALU.mult, sem=sV)
            pb.wait("tensor", sV)
        for sub in range(2):
            for dh in range(2):
                pairs = [(gbuf[:, CH2 * j + 128 * sub:CH2 * j + 128 * (sub + 1)],
                          mw["outprojT"][:, 1024 * j + 512 * dh:1024 * j + 512 * (dh + 1)])
                         for j in range(ND)]
                emit_mm(pb, ps_o[:, 512 * dh:512 * (dh + 1)], pairs,
                        sem=sT if dh == 1 else None)
            pb.wait("vector", sT)
            pb.op("vector", "tensor_copy", out=scratch["xlnb"][:, 0:1024],
                  in_=ps_o[:, 0:1024], sem=sV)
            pb.wait("tensor", sV)
            pb.wait("sync", sV, pb.cnt[sV])
            pb.dma(dram["out_partial"].rearrange("(i p) d -> p i d", p=128)
                   [:, 2 * ck + sub:2 * ck + sub + 1, :],
                   scratch["xlnb"][:, 0:1024].rearrange("p (i d) -> p i d", d=D),
                   dma_sem)
            pb.wait("vector", dma_sem)
    pb.wait("sync", dma_sem, pb.cnt[dma_sem])


def alloc_mamba(pb, tag=""):
    mw = {
        "inprojT_xi": pb.sbuf("m1ipx" + tag, [128, ND * 1024], BF16),
        "inprojT_z": pb.sbuf("m1ipz" + tag, [128, ND * 1024], BF16),
        "outprojT": pb.sbuf("m1op" + tag, [128, ND * 1024], BF16),
        "xprojT": pb.sbuf("m1xp" + tag, [128, ND * 96], BF16),
        "dtPT": pb.sbuf("m1dtp" + tag, [64, ND * 128], BF16),
        "oneh": pb.sbuf("m1oneh" + tag, [96, 2 * 16 * 128], BF16),
        "cwc": pb.sbuf("m1cw" + tag, [128, 4 * ND], F32),
        "cbc": pb.sbuf("m1cb" + tag, [128, ND], F32),
        "dtbc": pb.sbuf("m1dtb" + tag, [128, ND], F32),
        "alogc": pb.sbuf("m1alog" + tag, [128, 16 * ND], F32),
        "Dpc": pb.sbuf("m1dp" + tag, [128, ND], F32),
    }
    scratch = {
        "xch": pb.sbuf("m1xch" + tag, [128, 2 * D], F32),
        "xlnb": pb.sbuf("m1xlnb" + tag, [128, D], F32),
        "xlnT": pb.sbuf("m1xlnT" + tag, [128, ND * CH1], BF16),
        "xi_roll": pb.sbuf("m1xir" + tag, [128, ND * (3 + CH1)], BF16),
        "xc": pb.sbuf("m1xc" + tag, [128, ND * TFULL], BF16),
        "xdT_f32": pb.sbuf("m1xdp" + tag, [96, TFULL], F32),
        "xdT_bf": pb.sbuf("m1xdb" + tag, [96, TFULL], BF16),
        "stats": pb.sbuf("m1stats" + tag, [128, 8], F32),
        "rstd": pb.sbuf("m1rstd" + tag, [128, 32], F32),
        "nbias": pb.sbuf("m1nbias" + tag, [128, 32], F32),
        "dtT": pb.sbuf("m1dtT" + tag, [128, CH2], F32),
        "sz": pb.sbuf("m1sz" + tag, [128, CH2], F32),
        "wch": pb.sbuf("m1wch" + tag, [128, CH2], F32),
        "ybuf": pb.sbuf("m1y" + tag, [128, CH2], F32),
        "scr2": pb.sbuf("m1scr" + tag, [128, CH2], F32),
        "dbx": pb.sbuf("m1dbx" + tag, [128, CH2], BF16),
        "dA": pb.sbuf("m1dA" + tag, [128, CH2], F32),
        "hbuf": pb.sbuf("m1h" + tag, [128, 16 * CH2], BF16),
        "gbuf": pb.sbuf("m1g" + tag, [128, ND * CH2], BF16),
        "hcarry": pb.sbuf("m1hc" + tag, [128, 16 * ND], F32),
        "Acols": pb.sbuf("m1acol" + tag, [128, 16 * ND], F32),
    }
    return mw, scratch


def mamba_weight_prep(pb, dram, mw, stage, dma_sem, sems):
    rows = [dram["inprojT_xi"][128 * k:128 * (k + 1), 512 * c:512 * (c + 1)]
            for k in range(ND) for c in range(2)]
    dsts = [mw["inprojT_xi"][:, 1024 * k + 512 * c:1024 * k + 512 * (c + 1)]
            for k in range(ND) for c in range(2)]
    emit_load_cast(pb, rows, stage, dsts, dma_sem, sems)
    rows = [dram["inprojT_z"][128 * k:128 * (k + 1), 512 * c:512 * (c + 1)]
            for k in range(ND) for c in range(2)]
    dsts = [mw["inprojT_z"][:, 1024 * k + 512 * c:1024 * k + 512 * (c + 1)]
            for k in range(ND) for c in range(2)]
    emit_load_cast(pb, rows, stage, dsts, dma_sem, sems)
    rows = [dram["outprojT"][128 * j:128 * (j + 1), 512 * c:512 * (c + 1)]
            for j in range(ND) for c in range(2)]
    dsts = [mw["outprojT"][:, 1024 * j + 512 * c:1024 * j + 512 * (c + 1)]
            for j in range(ND) for c in range(2)]
    emit_load_cast(pb, rows, stage, dsts, dma_sem, sems)
    rows = [dram["xprojT"][128 * j:128 * (j + 1), :] for j in range(ND)]
    dsts = [mw["xprojT"][:, 96 * j:96 * (j + 1)] for j in range(ND)]
    emit_load_cast(pb, rows, stage, dsts, dma_sem, sems)
    gsem = pb.sem("gdma")
    pb.op("gpsimd", "dma_start", out=mw["dtPT"][0:64, :], in_=dram["dtPT"][0:64, :],
          sem=gsem, inc=16)
    pb.wait("tensor", gsem)
    pb.op("sync", "dma_start", out=mw["oneh"][0:96, :], in_=dram["oneh"][:],
          sem=dma_sem, inc=16)
    for nm in ("cwc", "cbc", "dtbc", "alogc", "Dpc"):
        pb.op("sync", "dma_start", out=mw[nm][:], in_=dram[nm + "_in"][:],
              sem=dma_sem, inc=16)


def build_mamba_program():
    nc = bass.Bass("TRN2", target_bir_lowering=False, debug=False)
    pb = PB(nc)
    dram = {
        "xfull": nc.dram_tensor("xfull", [TFULL, D], F32, kind="ExternalInput").ap(),
        "xdbl_peer": nc.dram_tensor("xdbl_peer", [96, TFULL], F32, kind="ExternalInput").ap(),
        "inprojT_xi": nc.dram_tensor("inprojT_xi", [D, 1024], F32, kind="ExternalInput").ap(),
        "inprojT_z": nc.dram_tensor("inprojT_z", [D, 1024], F32, kind="ExternalInput").ap(),
        "outprojT": nc.dram_tensor("outprojT", [1024, 1024], F32, kind="ExternalInput").ap(),
        "xprojT": nc.dram_tensor("xprojT", [1024, 96], F32, kind="ExternalInput").ap(),
        "dtPT": nc.dram_tensor("dtPT", [64, 1024], F32, kind="ExternalInput").ap(),
        "oneh": nc.dram_tensor("oneh", [96, 4096], BF16, kind="ExternalInput").ap(),
        "cwc_in": nc.dram_tensor("cwc_in", [128, 32], F32, kind="ExternalInput").ap(),
        "cbc_in": nc.dram_tensor("cbc_in", [128, 8], F32, kind="ExternalInput").ap(),
        "dtbc_in": nc.dram_tensor("dtbc_in", [128, 8], F32, kind="ExternalInput").ap(),
        "alogc_in": nc.dram_tensor("alogc_in", [128, 128], F32, kind="ExternalInput").ap(),
        "Dpc_in": nc.dram_tensor("Dpc_in", [128, 8], F32, kind="ExternalInput").ap(),
        "ident": nc.dram_tensor("ident", [128, 128], F32, kind="ExternalInput").ap(),
        "xdbl_out": nc.dram_tensor("xdbl_out", [96, TFULL], F32, kind="ExternalOutput").ap(),
        "xc_out": nc.dram_tensor("xc_out", [1024, TFULL], BF16, kind="ExternalOutput").ap(),
        "out_partial": nc.dram_tensor("out_partial", [TFULL, D], F32, kind="ExternalOutput").ap(),
    }
    mw, scratch = alloc_mamba(pb)
    scratch["ident"] = pb.sbuf("identsb", [128, 128], F32)
    scratch["ps_q"] = pb.psum("ps_q", [128, 512], F32)
    scratch["ps_xd"] = pb.psum("ps_xd", [128, 512], F32)
    scratch["ps_tr"] = pb.psum("ps_tr", [128, 512], F32)
    scratch["ps_bc"] = pb.psum("ps_bc", [128, 512], F32)
    scratch["ps_o"] = pb.psum("ps_o", [128, 1024], F32)
    stage = pb.sbuf("stage", [128, 512], F32)
    dma_sem = pb.sem("dma")
    sems = {"sV": pb.sem("sV"), "sS": pb.sem("sS"), "sT": pb.sem("sT"),
            "sG": pb.sem("sG")}
    st = pb.sem("st")

    pb.op("sync", "dma_start", out=scratch["ident"][:], in_=dram["ident"][:],
          sem=dma_sem, inc=16)
    mamba_weight_prep(pb, dram, mw, stage, dma_sem, sems)
    pb.wait("vector", dma_sem)
    pb.wait("scalar", dma_sem)
    pb.wait("tensor", dma_sem)

    emit_mamba_phase1(pb, dram, mw, scratch, sems, dma_sem)
    pb.wait("sync", sems["sV"])
    pb.op("sync", "dma_start", out=dram["xdbl_out"][0:96, :],
          in_=scratch["xdT_f32"][0:96, :], sem=st, inc=16)
    pb.op("sync", "dma_start",
          out=dram["xc_out"].rearrange("(j p) t -> p j t", p=128),
          in_=scratch["xc"].rearrange("p (j t) -> p j t", t=TFULL),
          sem=st, inc=16)
    pb.wait("sync", st)
    emit_mamba_phase2(pb, dram, mw, scratch, sems, dma_sem)
    return pb.build()


def host_mamba_inputs(in_proj_w, conv_w, conv_b, x_proj_w, dt_proj_w, dt_proj_b,
                      A_log, Dp, out_proj_w, eh):
    e0 = 1024 * eh
    ipT = np.ascontiguousarray(in_proj_w.T)
    inprojT_xi = np.ascontiguousarray(ipT[:, e0:e0 + 1024])
    inprojT_z = np.ascontiguousarray(ipT[:, 2048 + e0:2048 + e0 + 1024])
    outprojT = np.ascontiguousarray(out_proj_w.T[e0:e0 + 1024, :])
    xprojT = np.ascontiguousarray(x_proj_w.T[e0:e0 + 1024, :])
    dtPT = np.ascontiguousarray(dt_proj_w.T[:, e0:e0 + 1024])
    oneh = np.zeros((96, 4096), np.float32)
    for s in range(16):
        oneh[64 + s, 128 * s:128 * (s + 1)] = 1.0
        oneh[80 + s, 2048 + 128 * s:2048 + 128 * (s + 1)] = 1.0
    import ml_dtypes
    oneh = oneh.astype(ml_dtypes.bfloat16)
    cwc = np.ascontiguousarray(
        conv_w[e0:e0 + 1024].reshape(8, 128, 4).transpose(1, 0, 2).reshape(128, 32))
    cbc = np.ascontiguousarray(conv_b[e0:e0 + 1024].reshape(8, 128).T)
    dtbc = np.ascontiguousarray(dt_proj_b[e0:e0 + 1024].reshape(8, 128).T)
    alogc = np.ascontiguousarray(
        A_log[e0:e0 + 1024].reshape(8, 128, 16).transpose(1, 0, 2).reshape(128, 128))
    Dpc = np.ascontiguousarray(Dp[e0:e0 + 1024].reshape(8, 128).T)
    return dict(inprojT_xi=inprojT_xi, inprojT_z=inprojT_z, outprojT=outprojT,
                xprojT=xprojT, dtPT=dtPT, oneh=np.ascontiguousarray(oneh),
                cwc_in=cwc, cbc_in=cbc, dtbc_in=dtbc, alogc_in=alogc, Dpc_in=Dpc)



def build_ffn0_program(add_partials=False):
    nc = bass.Bass("TRN2", target_bir_lowering=False, debug=False)
    pb = PB(nc)
    dram = {
        "xres": nc.dram_tensor("xres", [TOK, D], F32, kind="ExternalInput").ap(),
        "w1T": nc.dram_tensor("w1T", [D, 4096], F32, kind="ExternalInput").ap(),
        "w2T": nc.dram_tensor("w2T", [4096, D], F32, kind="ExternalInput").ap(),
        "b1c": nc.dram_tensor("b1c", [128, 32], F32, kind="ExternalInput").ap(),
        "b2r": nc.dram_tensor("b2r", [1, D], F32, kind="ExternalInput").ap(),
        "ones1": nc.dram_tensor("ones1", [1, 128], BF16, kind="ExternalInput").ap(),
        "ident": nc.dram_tensor("ident", [128, 128], F32, kind="ExternalInput").ap(),
        "maskbf": nc.dram_tensor("maskd", [TOK, D], F8E5, kind="ExternalInput").ap(),
        "xout": nc.dram_tensor("xout", [TOK, D], F32, kind="ExternalOutput").ap(),
    }
    weights = alloc_ffn0_weights(pb)
    weights["maskbf"] = pb.sbuf("maskbf_sb", [128, NT * D], F8E5)
    scratch = alloc_ffn0_scratch(pb)
    scratch["ident"] = pb.sbuf("identsb", [128, 128], F32)
    scratch["ps_tr"] = pb.psum("ps_tr", [128, 512], F32)
    scratch["ps_h"] = pb.psum("ps_h", [128, 512], F32)
    scratch["ps_y"] = pb.psum("ps_y", [128, 1024], F32)
    ps_b2 = pb.psum("ps_b2", [128, 1024], F32)
    xres = pb.sbuf("xres_sb", [128, NT * D], F32)
    stage = pb.sbuf("stage", [128, 2048], F32)
    dma_sem = pb.sem("dma")
    sems = {"sV": pb.sem("sV"), "sS": pb.sem("sS"), "sT": pb.sem("sT"),
            "sG": pb.sem("sG")}
    st = pb.sem("st")

    xr3 = dram["xres"].rearrange("(i p) d -> p i d", p=128)
    pb.op("sync", "dma_start", out=xres.rearrange("p (i d) -> p i d", d=D),
          in_=xr3, sem=dma_sem, inc=16)
    if add_partials:
        dram["pa"] = nc.dram_tensor("pa", [TOK, D], F32, kind="ExternalInput").ap()
        dram["pb2"] = nc.dram_tensor("pb2", [TOK, D], F32, kind="ExternalInput").ap()
        pstage = scratch["xlnb"]
        for i in range(NT):
            pb.wait("sync", sems["sV"], pb.cnt[sems["sV"]])
            pb.dma(pstage[:, 0:D],
                   dram["pa"].rearrange("(i p) d -> p i d", p=128)[:, i:i + 1, :],
                   dma_sem)
            pb.dma(pstage[:, D:2 * D],
                   dram["pb2"].rearrange("(i p) d -> p i d", p=128)[:, i:i + 1, :],
                   dma_sem)
            pb.wait("vector", dma_sem)
            pb.op("vector", "tensor_tensor", out=xres[:, D * i:D * (i + 1)],
                  in0=xres[:, D * i:D * (i + 1)], in1=pstage[:, 0:D], op=ALU.add)
            pb.op("vector", "tensor_tensor", out=xres[:, D * i:D * (i + 1)],
                  in0=xres[:, D * i:D * (i + 1)], in1=pstage[:, D:2 * D],
                  op=ALU.add, sem=sems["sV"])
        pb.wait("scalar", sems["sV"])
        pb.wait("tensor", sems["sV"])
    mk3 = dram["maskbf"].rearrange("(i p) d -> p i d", p=128)
    pb.op("sync", "dma_start",
          out=weights["maskbf"].rearrange("p (i d) -> p i d", d=D),
          in_=mk3, sem=dma_sem, inc=16)
    pb.op("sync", "dma_start", out=scratch["ident"][:], in_=dram["ident"][:],
          sem=dma_sem, inc=16)
    ffn0_weight_prep(pb, dram, weights, stage, ps_b2, dma_sem, sems)
    pb.wait("vector", dma_sem)
    pb.wait("scalar", dma_sem)
    pb.wait("tensor", dma_sem)

    emit_ffn0(pb, xres, weights, scratch, sems)

    pb.wait("sync", sems["sV"])
    pb.op("sync", "dma_start",
          out=dram["xout"].rearrange("(i p) d -> p i d", p=128),
          in_=xres.rearrange("p (i d) -> p i d", d=D), sem=st, inc=16)
    pb.wait("sync", st)
    return pb.build()



def host_ffn0_inputs(W1, b1, W2, b2):
    w1T = np.ascontiguousarray(W1.T)
    w2T = np.ascontiguousarray(W2.T)
    b1c = np.ascontiguousarray(b1.reshape(32, 128).T)
    b2r = np.ascontiguousarray(b2.reshape(1, D))
    return w1T, w2T, b1c, b2r


def host_dropout_mask():
    import jax
    import ml_dtypes
    with jax.default_device(jax.devices("cpu")[0]):
        key = jax.random.key(42, impl="threefry2x32")
        keep = np.asarray(jax.random.bernoulli(key, 0.8, (4, 4096, D)))
    return (keep.astype(np.float32) * 1.25).astype(ml_dtypes.float8_e5m2)


def _shards(x):
    out = []
    for c in range(NCORES):
        b, h = c // 2, c % 2
        out.append(np.ascontiguousarray(x[b, h * TOK:(h + 1) * TOK]))
    return out


GROUPS = [[0, 1], [2, 3], [4, 5], [6, 7]]


def build_fused_program():
    nc = bass.Bass("TRN2", target_bir_lowering=False, debug=False)
    pb = PB(nc)
    ein = lambda nm, shp, dt=F32: nc.dram_tensor(nm, shp, dt, kind="ExternalInput").ap()
    dram = {
        "xres": ein("xres", [TOK, D]),
        "xfull": ein("xfull", [TFULL, D]),
        "ident": ein("ident", [128, 128]),
        "identb": ein("identb", [128, 128], BF16),
        "onesc": ein("onesc", [128, 1]),
        "onesr": ein("onesr", [1, 128]),
        "ones1": ein("ones1", [1, 128], BF16),
        "hflags": ein("hflags", [128, 3]),
        "inprojT_xi": ein("inprojT_xi", [D, 1024]),
        "inprojT_z": ein("inprojT_z", [D, 1024]),
        "outprojT": ein("outprojT", [1024, 1024]),
        "xprojT": ein("xprojT", [1024, 96]),
        "dtPT": ein("dtPT", [64, 1024]),
        "oneh": ein("oneh", [96, 4096], BF16),
        "cwc_in": ein("cwc_in", [128, 32]),
        "cbc_in": ein("cbc_in", [128, 8]),
        "dtbc_in": ein("dtbc_in", [128, 8]),
        "alogc_in": ein("alogc_in", [128, 128]),
        "Dpc_in": ein("Dpc_in", [128, 8]),
        "w1T": ein("w1T", [D, 4096]),
        "w2T": ein("w2T", [4096, D]),
        "b1c": ein("b1c", [128, 32]),
        "b2r": ein("b2r", [1, D]),
        "maskbf": ein("maskd", [TOK, D], F8E5),
        "wqkv": ein("wqkv", [D, 3072]),
        "wproj": ein("wproj", [D, D]),
        "lnw": ein("lnw", [1, D]),
        "maskneg": ein("maskneg", [128, 512]),
        "encq": ein("encq", [D, 1024]),
        "encv": ein("encv", [D, 1024]),
        "decoder": ein("decoder", [1024, D]),
        "mw1": ein("mw1", [1024, 512]),
        "mw2": ein("mw2", [512, 1024]),
        "hebbc": ein("hebbc", [1024, 128]),
        "mb1c": ein("mb1c", [128, 4]),
        "mb2c": ein("mb2c", [128, 8]),
        "basec": ein("basec", [128, 8]),
        "wvbc": ein("wvbc", [128, 2048], BF16),
        "bandmask": ein("bandmask", [128, 160]),
        "f1w1T": ein("f1w1T", [D, 4096]),
        "f1w2T": ein("f1w2T", [4096, D]),
        "cw": ein("cw", [128, 96]),
        "cb": ein("cb", [128, 32]),
        "WwT": ein("WwT", [D, D]),
        "WrT": ein("WrT", [D, D]),
        "decay": ein("decay", [1, 1]),
        "log_alpha": ein("log_alpha", [1, 1]),
        "gflag": ein("gflag", [1, 1]),
        "iotaDI": ein("iotaDI", [128, 1]),
        "iotaGW": ein("iotaGW", [128, 1]),
        "iota64": ein("iota64", [128, 1]),
        "iotat": ein("iotat", [128, 16]),
        "dmat": ein("dmat", [128, 64]),
        "causalT": ein("causalT", [128, 64]),
        "xout": nc.dram_tensor("xout", [TOK, D], F32, kind="ExternalOutput").ap(),
    }
    dram["ones32"] = dram["onesr"]
    idr = lambda nm, shp, dt=F32: nc.dram_tensor(nm, shp, dt, kind="Internal").ap()
    ag1i, ag1o = idr("ag1i", [96, TFULL]), idr("ag1o", [192, TFULL])
    ag2i, ag2o = idr("ag2i", [TFULL, D]), idr("ag2o", [2 * TFULL, D])
    ag5i, ag5o = idr("ag5i", [31, D]), idr("ag5o", [62, D])
    ag4i, ag4o = idr("ag4i", [128, 8]), idr("ag4o", [256, 8])
    ag6i, ag6o = idr("ag6i", [2, D]), idr("ag6o", [4, D])
    agwi, agwo = idr("agwi", [1, D]), idr("agwo", [2, D])
    ag7i, ag7o = idr("ag7i", [D, D], BF16), idr("ag7o", [2 * D, D], BF16)
    ag8i, ag8o = idr("ag8i", [2, D]), idr("ag8o", [4, D])
    w1q_dram, w2q_dram = idr("w1q_dram", [D, 4096], BF16), idr("w2q_dram", [4096, D], BF16)
    ff1cols = idr("ff1cols", [128, 4])

    identsb = pb.sbuf("ident_p", [128, 128], F32)
    identbsb = pb.sbuf("identb_p", [128, 128], BF16)
    hflags = pb.sbuf("hflags_p", [128, 3], F32)
    pb.arena_init(211984)
    xres = pb.sbuf("xres_a", [128, NT * D], F32)
    pb.arena_persist()

    dma_sem = pb.sem("dma")
    dmaC = pb.sem("dmaC")
    csem = pb.sem("csem")
    sbar = pb.sem("sbar")
    sems = {"sV": pb.sem("sV"), "sS": pb.sem("sS"), "sT": pb.sem("sT"),
            "sG": pb.sem("sG")}
    dmaW = (pb.sem("dmaW0"), pb.sem("dmaW1"))
    st = pb.sem("st")
    sV = sems["sV"]

    pb.op("sync", "dma_start", out=identsb[:], in_=dram["ident"][:], sem=dma_sem, inc=16)
    pb.op("sync", "dma_start", out=identbsb[:], in_=dram["identb"][:], sem=dma_sem, inc=16)
    pb.op("sync", "dma_start", out=hflags[:], in_=dram["hflags"][:], sem=dma_sem, inc=16)

    def barrier():
        for eng in ("vector", "scalar", "tensor", "gpsimd"):
            pb.draininc(eng, sbar)
        for eng in ("sync", "vector", "scalar", "tensor", "gpsimd"):
            pb.wait(eng, sbar)
            for ds in (dma_sem, dmaC, dmaW[0], dmaW[1]):
                pb.wait(eng, ds)

    def allgather(agin_ap, agout_ap):
        pb.op("gpsimd", "collective_compute", "AllGather", ALU.bypass,
              replica_groups=GROUPS, ins=[agin_ap], outs=[agout_ap],
              sem=csem, inc=1)

    pb.arena_reset(base=0)
    mw, mscr = alloc_mamba(pb)
    mscr["ident"] = identsb
    mscr["ps_q"] = pb.psum("ps_q", [128, 512], F32)
    mscr["ps_xd"] = pb.psum("ps_xd", [128, 512], F32)
    mscr["ps_tr"] = pb.psum("ps_tr", [128, 512], F32)
    mscr["ps_bc"] = pb.psum("ps_bc", [128, 512], F32)
    mscr["ps_o"] = pb.psum("ps_o", [128, 1024], F32)
    dram_m = dict(dram)
    dram_m["out_partial"] = ag2i
    stg1 = pb.sbuf("stg1", [128, 512], F32)
    mamba_weight_prep(pb, dram_m, mw, stg1, dma_sem, sems)
    pb.wait("vector", dma_sem)
    pb.wait("scalar", dma_sem)
    pb.wait("tensor", dma_sem)
    emit_mamba_phase1(pb, dram_m, mw, mscr, sems, dma_sem)
    pb.wait("sync", sV)
    pb.dma(ag1i[0:96, :], mscr["xdT_f32"][0:96, :], dmaC)
    pb.wait("gpsimd", dmaC)
    allgather(ag1i[:, :], ag1o[:, :])
    pb.wait("sync", csem)
    emit_mamba_phase2(pb, dram_m, mw, mscr, sems, dma_sem,
                      peer_both=(ag1o[0:96, :], ag1o[96:192, :]))
    barrier()
    allgather(ag2i[:, :], ag2o[:, :])

    pb.arena_reset()
    f0w = alloc_ffn0_weights(pb)
    f0w["maskbf"] = pb.sbuf("maskbf_sb", [128, NT * D], F8E5)
    f0s = alloc_ffn0_scratch(pb)
    f0s["ident"] = identsb
    f0s["ps_tr"] = pb.psum("ps_tr", [128, 512], F32)
    f0s["ps_h"] = pb.psum("ps_h", [128, 512], F32)
    f0s["ps_y"] = pb.psum("ps_y", [128, 1024], F32)
    ps_b2 = pb.psum("ps_b2", [128, 1024], F32)
    stage = pb.sbuf("stg2", [128, 2048], F32)
    pst = f0s["xlnb"]
    pb.op("sync", "dma_start", out=xres.rearrange("p (i d) -> p i d", d=D),
          in_=dram["xres"].rearrange("(i p) d -> p i d", p=128), sem=dma_sem, inc=16)
    pb.wait("sync", csem)
    g2 = ag2o.rearrange("(i p) d -> p i d", p=128)
    for i in range(NT):
        pb.wait("sync", sV, pb.cnt[sV])
        pb.dma(pst[:, 0:D], g2[:, i:i + 1, :], dmaC)
        pb.dma(pst[:, D:2 * D], g2[:, 32 + i:32 + i + 1, :], dmaC)
        pb.wait("vector", dmaC)
        pb.wait("vector", dma_sem)
        pb.op("vector", "tensor_tensor", out=pst[:, 0:D],
              in0=pst[:, 0:D], in1=pst[:, D:2 * D], op=ALU.add, sem=sV)
        pb.op("vector", "scalar_tensor_tensor", out=xres[:, D * i:D * (i + 1)],
              in0=pst[:, 0:D], scalar=hflags[:, 2:3],
              in1=xres[:, D * i:D * (i + 1)], op0=ALU.mult, op1=ALU.add, sem=sV)
        pb.wait("sync", sV, pb.cnt[sV])
        pb.dma(pst[:, 0:D], g2[:, 16 + i:16 + i + 1, :], dmaC)
        pb.dma(pst[:, D:2 * D], g2[:, 48 + i:48 + i + 1, :], dmaC)
        pb.wait("vector", dmaC)
        pb.op("vector", "tensor_tensor", out=pst[:, 0:D],
              in0=pst[:, 0:D], in1=pst[:, D:2 * D], op=ALU.add)
        pb.op("vector", "scalar_tensor_tensor", out=xres[:, D * i:D * (i + 1)],
              in0=pst[:, 0:D], scalar=hflags[:, 0:1],
              in1=xres[:, D * i:D * (i + 1)], op0=ALU.mult, op1=ALU.add, sem=sV)
    pb.wait("scalar", sV)
    pb.wait("tensor", sV)
    pb.op("sync", "dma_start",
          out=f0w["maskbf"].rearrange("p (i d) -> p i d", d=D),
          in_=dram["maskbf"].rearrange("(i p) d -> p i d", p=128), sem=dma_sem, inc=16)
    ffn0_weight_prep(pb, dram, f0w, stage, ps_b2, dma_sem, sems)
    pb.wait("vector", dma_sem)
    pb.wait("scalar", dma_sem)
    pb.wait("tensor", dma_sem)
    emit_ffn0(pb, xres, f0w, f0s, sems)
    barrier()

    pb.arena_reset()
    abw, abs_ = alloc_attnb0(pb)
    abs_["ident"] = identsb
    abs_["identb"] = identbsb
    abs_["ps_tr"] = pb.psum("ps_tr", [128, 512], F32)
    abs_["ps_trb"] = pb.psum("ps_trb", [128, 512], BF16)
    abs_["ps_q"] = pb.psum("ps_q", [128, 512], F32)
    abs_["ps_att"] = pb.psum("ps_att", [128, 256], F32)
    abs_["ps_yg"] = pb.psum("ps_yg", [128, 512], F32)
    abs_["ps_p"] = pb.psum("ps_p", [128, 512], F32)
    ps_b2 = pb.psum("ps_b2", [128, 1024], F32)
    stage = pb.sbuf("stg3", [128, 2048], F32)
    attnb0_weight_prep(pb, dram, abw, stage, ps_b2, dma_sem, sems)
    pb.wait("vector", dma_sem)
    pb.wait("scalar", dma_sem)
    pb.wait("tensor", dma_sem)
    emit_attnb0(pb, xres, abw, abs_, sems)
    barrier()

    pb.arena_reset()
    f0w = alloc_ffn0_weights(pb)
    f0w["maskbf"] = pb.sbuf("maskbf_sb4", [128, NT * D], F8E5)
    f0s = alloc_ffn0_scratch(pb)
    f0s["ident"] = identsb
    f0s["ps_tr"] = pb.psum("ps_tr", [128, 512], F32)
    f0s["ps_h"] = pb.psum("ps_h", [128, 512], F32)
    f0s["ps_y"] = pb.psum("ps_y", [128, 1024], F32)
    ps_b2 = pb.psum("ps_b2", [128, 1024], F32)
    stage = pb.sbuf("stg4", [128, 2048], F32)
    pb.op("sync", "dma_start",
          out=f0w["maskbf"].rearrange("p (i d) -> p i d", d=D),
          in_=dram["maskbf"].rearrange("(i p) d -> p i d", p=128), sem=dma_sem, inc=16)
    ffn0_weight_prep(pb, dram, f0w, stage, ps_b2, dma_sem, sems)
    pb.wait("vector", dma_sem)
    pb.wait("scalar", dma_sem)
    pb.wait("tensor", dma_sem)
    emit_ffn0(pb, xres, f0w, f0s, sems)
    barrier()

    pb.dma(ag5i[0:31, :], xres[97:128, 15 * D:16 * D], dmaC)
    pb.wait("gpsimd", dmaC)
    allgather(ag5i[:, :], ag5o[:, :])
    pb.arena_reset()
    aw5, s5 = alloc_attna1(pb)
    s5["ident"] = identsb
    s5["identb"] = identbsb
    s5["ps_a"] = pb.psum("ps_a", [128, 512], F32)
    s5["ps_b"] = pb.psum("ps_b", [128, 512], F32)
    s5["ps_c"] = pb.psum("ps_c", [128, 512], F32)
    s5["ps_dec"] = pb.psum("ps_dec", [128, 1024], F32)
    s5["ps_tr"] = pb.psum("ps_tr", [128, 512], F32)
    s5["ps_trb"] = pb.psum("ps_trb", [128, 512], BF16)
    s5["ps_row"] = pb.psum("ps_row", [128, 512], F32)
    sigst = pb.sbuf("sigst", [128, 16], F32)
    pb.wait("sync", csem)
    pb.dma(s5["xhalo_sb"][0:31, :], ag5o[0:31, :], dma_sem)
    pb.wait("vector", dma_sem)
    pb.op("vector", "tensor_scalar", out=s5["xhalo_sb"][0:31, :],
          in0=s5["xhalo_sb"][0:31, :], scalar1=hflags[0:31, 0:1], scalar2=None,
          op0=ALU.mult, sem=sV)
    pb.wait("scalar", sV)
    pb.wait("tensor", sV)

    def sig_xchg(pb2, sig):
        pb2.draininc("vector", sV)
        pb2.wait("sync", sV)
        pb2.dma(ag4i[:, :], sig[:, 0:8], dmaC)
        pb2.wait("gpsimd", dmaC)
        allgather(ag4i[:, :], ag4o[:, :])
        pb2.wait("sync", csem)
        pb2.dma(sigst[:, 0:8], ag4o[0:128, :], dmaC)
        pb2.dma(sigst[:, 8:16], ag4o[128:256, :], dmaC)
        pb2.wait("vector", dmaC)
        pb2.op("vector", "tensor_tensor", out=sig[:, 0:8],
               in0=sigst[:, 0:8], in1=sigst[:, 8:16], op=ALU.add, sem=sV)
    s5["sig_xchg"] = sig_xchg
    stage = pb.sbuf("stg5", [128, 1024], F32)
    attna1_weight_prep(pb, dram, aw5, stage, dma_sem, sems)
    pb.wait("vector", dma_sem)
    pb.wait("scalar", dma_sem)
    pb.wait("tensor", dma_sem)
    s5["dram"] = dram
    s5["stage"] = stage
    emit_attna1(pb, xres, aw5, s5, sems, dma_sem)
    barrier()

    pb.dma(ag6i[0:1, :], xres[127:128, 15 * D:16 * D], dmaC)
    pb.dma(ag6i[1:2, :], xres[0:1, 0:D], dmaC)
    pb.wait("gpsimd", dmaC)
    allgather(ag6i[:, :], ag6o[:, :])
    pb.arena_reset()
    fw, s6 = alloc_ffn1(pb)
    s6["ident"] = identsb
    s6["ps_tr"] = pb.psum("ps_tr", [128, 512], F32)
    s6["ps_h"] = pb.psum("ps_h", [128, 512], F32)
    s6["ps_hs"] = pb.psum("ps_hs", [128, 2], F32)
    s6["ps_y0"] = pb.psum("ps_y0", [128, 1024], F32)
    s6["ps_y1"] = pb.psum("ps_y1", [128, 1024], F32)
    s6["ps_one"] = pb.psum("ps_one", [128, 8], F32)
    s6["dmaW"] = dmaW
    dram_f1 = dict(dram)
    dram_f1["w1q_dram"] = w1q_dram
    dram_f1["w2q_dram"] = w2q_dram
    pb.wait("sync", csem)
    pb.dma(s6["xhalo_sb"][0:1, :], ag6o[0:1, :], dma_sem)
    pb.dma(s6["xhalo_sb"][1:2, :], ag6o[3:4, :], dma_sem)
    pb.wait("vector", dma_sem)
    pb.op("vector", "tensor_scalar", out=s6["xhalo_sb"][0:2, :],
          in0=s6["xhalo_sb"][0:2, :], scalar1=hflags[0:2, 1:2], scalar2=None,
          op0=ALU.mult, sem=sV)
    pb.wait("scalar", sV)
    pb.wait("tensor", sV)
    for (key, dst) in (("onesc", fw["ones_col"][:]), ("cw", fw["cwc"][:]),
                       ("cb", fw["cbc"][:])):
        pb.op("sync", "dma_start", out=dst, in_=dram[key][:], sem=dma_sem, inc=16)
    pb.op("sync", "dma_start", out=fw["ones_row"][0:1, :], in_=dram["onesr"][:],
          sem=dma_sem, inc=16)
    pb.wait("vector", dma_sem)
    pb.wait("scalar", dma_sem)
    pb.wait("tensor", dma_sem)
    emit_ffn1_weightprep(pb, dram_f1, fw, s6, sems, dma_sem)
    pb.wait("sync", sV)
    pb.dma(ff1cols[:, 0:2], fw["dq_col"][:, 0:2], dmaC)
    pb.dma(ff1cols[:, 2:4], fw["sw_col"][:, 0:2], dmaC)
    emit_ffn1(pb, xres, dram_f1, fw, s6, sems, dma_sem)
    barrier()

    pb.dma(agwi[0:1, :], xres[127:128, 15 * D:16 * D], dmaC)
    pb.wait("gpsimd", dmaC)
    allgather(agwi[:, :], agwo[:, :])
    pb.arena_reset()
    aw7, s7 = alloc_attnb1(pb)
    s7["ident"] = identsb
    s7["identb"] = identbsb
    s7["ps_a"] = pb.psum("ps_a", [128, 1024], F32)
    s7["ps_c"] = pb.psum("ps_c", [128, 1024], F32)
    s7["ps_u"] = pb.psum("ps_u", [128, 512], F32)
    s7["ps_trb"] = pb.psum("ps_trb", [128, 512], BF16)
    s7["ps_tr"] = pb.psum("ps_tr", [128, 512], F32)
    s7["ps_sst"] = pb.psum("ps_sst", [128, 64], F32)
    pb.wait("sync", csem)
    pb.dma(s7["whalo_sb"][0:1, :], agwo[0:1, :], dma_sem)
    pb.wait("vector", dma_sem)
    pb.op("vector", "tensor_scalar", out=s7["whalo_sb"][0:1, :],
          in0=s7["whalo_sb"][0:1, :], scalar1=hflags[0:1, 0:1], scalar2=None,
          op0=ALU.mult, sem=sV)
    pb.wait("scalar", sV)
    pb.wait("tensor", sV)
    attnb1_weight_prep(pb, dram, aw7, s7["rc"], dma_sem, sems)
    pb.wait("vector", dma_sem)
    pb.wait("scalar", dma_sem)
    pb.wait("tensor", dma_sem)
    s7["WpT_dram"] = ag7o[0:D, :]

    def W_xchg(pb2):
        pb2.draininc("vector", sV)
        pb2.wait("sync", sV)
        pb2.dma(ag7i.rearrange("(j p) d -> p j d", p=128),
                s7["Wst"].rearrange("p (j d) -> p j d", d=D), dmaC)
        pb2.wait("gpsimd", dmaC)
        allgather(ag7i[:, :].bitcast(F32), ag7o[:, :].bitcast(F32))
        pb2.wait("sync", csem)
    s7["W_xchg"] = W_xchg
    emit_attnb1(pb, xres, aw7, s7, sems, dma_sem)
    barrier()

    pb.dma(ag8i[0:1, :], xres[127:128, 15 * D:16 * D], dmaC)
    pb.dma(ag8i[1:2, :], xres[0:1, 0:D], dmaC)
    pb.wait("gpsimd", dmaC)
    allgather(ag8i[:, :], ag8o[:, :])
    pb.arena_reset()
    fw, s8 = alloc_ffn1(pb)
    s8["ident"] = identsb
    s8["ps_tr"] = pb.psum("ps_tr", [128, 512], F32)
    s8["ps_h"] = pb.psum("ps_h", [128, 512], F32)
    s8["ps_hs"] = pb.psum("ps_hs", [128, 2], F32)
    s8["ps_y0"] = pb.psum("ps_y0", [128, 1024], F32)
    s8["ps_y1"] = pb.psum("ps_y1", [128, 1024], F32)
    s8["ps_one"] = pb.psum("ps_one", [128, 8], F32)
    s8["dmaW"] = dmaW
    pb.wait("sync", csem)
    pb.dma(s8["xhalo_sb"][0:1, :], ag8o[0:1, :], dma_sem)
    pb.dma(s8["xhalo_sb"][1:2, :], ag8o[3:4, :], dma_sem)
    pb.dma(fw["dq_col"][:, 0:2], ff1cols[:, 0:2], dma_sem)
    pb.dma(fw["sw_col"][:, 0:2], ff1cols[:, 2:4], dma_sem)
    pb.wait("vector", dma_sem)
    pb.op("vector", "tensor_scalar", out=s8["xhalo_sb"][0:2, :],
          in0=s8["xhalo_sb"][0:2, :], scalar1=hflags[0:2, 1:2], scalar2=None,
          op0=ALU.mult, sem=sV)
    pb.wait("scalar", sV)
    pb.wait("tensor", sV)
    for (key, dst) in (("onesc", fw["ones_col"][:]), ("cw", fw["cwc"][:]),
                       ("cb", fw["cbc"][:])):
        pb.op("sync", "dma_start", out=dst, in_=dram[key][:], sem=dma_sem, inc=16)
    pb.op("sync", "dma_start", out=fw["ones_row"][0:1, :], in_=dram["onesr"][:],
          sem=dma_sem, inc=16)
    pb.wait("vector", dma_sem)
    pb.wait("scalar", dma_sem)
    pb.wait("tensor", dma_sem)
    emit_ffn1(pb, xres, dram_f1, fw, s8, sems, dma_sem)

    pb.wait("sync", sV)
    pb.op("sync", "dma_start",
          out=dram["xout"].rearrange("(i p) d -> p i d", p=128),
          in_=xres.rearrange("p (i d) -> p i d", d=D), sem=st, inc=16)
    pb.wait("sync", st)
    return pb.build()


def kernel_unfused(**inputs):
    import ml_dtypes
    x0 = np.asarray(inputs["L0_attn_a_x"], np.float32)
    ident = np.eye(128, dtype=np.float32)
    identb = np.eye(128, dtype=ml_dtypes.bfloat16)
    onesc = np.ones((128, 1), np.float32)
    onesr = np.ones((1, 128), np.float32)
    ones1b = np.ones((1, 128), dtype=ml_dtypes.bfloat16)
    cores = list(range(NCORES))

    def run(nc, maps):
        return run_bass_kernel_spmd(nc, maps, core_ids=cores)

    nc_m = build_mamba_program()
    mmaps = []
    for c in cores:
        b, eh = c // 2, c % 2
        hi = host_mamba_inputs(
            np.asarray(inputs["L0_attn_a_in_proj_w"], np.float32),
            np.asarray(inputs["L0_attn_a_conv_w"], np.float32),
            np.asarray(inputs["L0_attn_a_conv_b"], np.float32),
            np.asarray(inputs["L0_attn_a_x_proj_w"], np.float32),
            np.asarray(inputs["L0_attn_a_dt_proj_w"], np.float32),
            np.asarray(inputs["L0_attn_a_dt_proj_b"], np.float32),
            np.asarray(inputs["L0_attn_a_A_log"], np.float32),
            np.asarray(inputs["L0_attn_a_Dp"], np.float32),
            np.asarray(inputs["L0_attn_a_out_proj_w"], np.float32), eh)
        hi["xfull"] = np.ascontiguousarray(x0[b])
        hi["xdbl_peer"] = np.zeros((96, TFULL), np.float32)
        hi["ident"] = ident
        mmaps.append(hi)
    r1 = run(nc_m, mmaps)
    for c in cores:
        mmaps[c]["xdbl_peer"] = np.ascontiguousarray(r1.results[c ^ 1]["xdbl_out"])
    r2 = run(nc_m, mmaps)
    pouts = [r2.results[c]["out_partial"] for c in cores]

    W1 = np.asarray(inputs["L0_ffn_W1"], np.float32)
    b1 = np.asarray(inputs["L0_ffn_b1"], np.float32)
    W2 = np.asarray(inputs["L0_ffn_W2"], np.float32)
    b2 = np.asarray(inputs["L0_ffn_b2"], np.float32)
    w1T, w2T, b1c, b2r = host_ffn0_inputs(W1, b1, W2, b2)
    mask = host_dropout_mask()
    xsh = _shards(x0)
    nc_f0p = build_ffn0_program(add_partials=True)
    maps = []
    for c in cores:
        b, h = c // 2, c % 2
        sl = np.s_[h * TOK:(h + 1) * TOK]
        maps.append({
            "xres": xsh[c],
            "pa": np.ascontiguousarray(pouts[2 * b][sl]),
            "pb2": np.ascontiguousarray(pouts[2 * b + 1][sl]),
            "w1T": w1T, "w2T": w2T, "b1c": b1c, "b2r": b2r, "ones1": ones1b,
            "ident": ident,
            "maskd": np.ascontiguousarray(mask[b, h * TOK:(h + 1) * TOK]),
        })
    r = run(nc_f0p, maps)
    xsh = [np.ascontiguousarray(r.results[c]["xout"]) for c in cores]

    nc_ab = build_attnb0_program()
    maskneg = host_attnb0_masks()
    maps = []
    for c in cores:
        maps.append({
            "xres": xsh[c],
            "wqkv": np.asarray(inputs["L0_attn_b_w_qkv"], np.float32),
            "wproj": np.asarray(inputs["L0_attn_b_w_proj"], np.float32),
            "lnw": np.asarray(inputs["L0_attn_b_ln_w"], np.float32).reshape(1, -1),
            "ones32": onesr, "maskneg": maskneg, "ident": ident, "identb": identb,
        })
    r = run(nc_ab, maps)
    xsh = [np.ascontiguousarray(r.results[c]["xout"]) for c in cores]

    nc_f0 = build_ffn0_program(add_partials=False)
    maps = []
    for c in cores:
        b, h = c // 2, c % 2
        maps.append({
            "xres": xsh[c],
            "w1T": w1T, "w2T": w2T, "b1c": b1c, "b2r": b2r, "ones1": ones1b,
            "ident": ident,
            "maskd": np.ascontiguousarray(mask[b, h * TOK:(h + 1) * TOK]),
        })
    r = run(nc_f0, maps)
    xsh = [np.ascontiguousarray(r.results[c]["xout"]) for c in cores]

    nc_a5 = build_attna1_program()
    enc_q = np.asarray(inputs["L1_attn_a_enc_q"], np.float32)
    enc_v = np.asarray(inputs["L1_attn_a_enc_v"], np.float32)
    encq = np.ascontiguousarray(enc_q.transpose(1, 0, 2).reshape(D, 1024))
    encv = np.ascontiguousarray(enc_v.transpose(1, 0, 2).reshape(D, 1024))
    hebbc = np.ascontiguousarray(
        np.asarray(inputs["L1_attn_a_hebbian"], np.float32).reshape(1024, 128))
    mb1c = np.ascontiguousarray(
        np.asarray(inputs["L1_attn_a_m_b1"], np.float32).reshape(4, 128).T)
    mb2c = np.ascontiguousarray(
        np.asarray(inputs["L1_attn_a_m_b2"], np.float32).reshape(8, 128).T)
    basec = np.ascontiguousarray(
        np.asarray(inputs["L1_attn_a_base_metric"], np.float32).T)
    wvbc, bandmask = host_attna1_consts()
    maps = []
    for c in cores:
        b, h = c // 2, c % 2
        halo = np.zeros((31, D), np.float32)
        if h == 1:
            halo[:] = xsh[2 * b][TOK - 31:TOK]
        maps.append({
            "xres": xsh[c], "xhalo": halo,
            "sigpeer": np.zeros((128, 8), np.float32),
            "encq": encq, "encv": encv,
            "decoder": np.asarray(inputs["L1_attn_a_decoder"], np.float32),
            "mw1": np.asarray(inputs["L1_attn_a_m_w1"], np.float32),
            "mw2": np.asarray(inputs["L1_attn_a_m_w2"], np.float32),
            "hebbc": hebbc, "mb1c": mb1c, "mb2c": mb2c, "basec": basec,
            "wvbc": wvbc, "bandmask": bandmask, "onesc": onesc, "onesr": onesr,
            "ident": ident, "identb": identb,
        })
    ra = run(nc_a5, maps)
    for c in cores:
        maps[c]["sigpeer"] = np.ascontiguousarray(ra.results[c ^ 1]["sig_out"])
    r = run(nc_a5, maps)
    xsh = [np.ascontiguousarray(r.results[c]["xout"]) for c in cores]

    nc_f1 = build_ffn1_program()
    f1w1T, f1w2T, cw, cb = host_ffn1_inputs(
        np.asarray(inputs["L1_ffn_w1"], np.float32),
        np.asarray(inputs["L1_ffn_conv_w"], np.float32),
        np.asarray(inputs["L1_ffn_conv_b"], np.float32),
        np.asarray(inputs["L1_ffn_w2"], np.float32))

    def ffn1_maps():
        maps = []
        for c in cores:
            b, h = c // 2, c % 2
            halo = np.zeros((2, D), np.float32)
            if h == 1:
                halo[0] = xsh[2 * b][TOK - 1]
            if h == 0:
                halo[1] = xsh[2 * b + 1][0]
            maps.append({
                "xres": xsh[c], "xhalo": halo,
                "f1w1T": f1w1T, "f1w2T": f1w2T, "cw": cw, "cb": cb,
                "ident": ident, "onesc": onesc, "onesr": onesr,
            })
        return maps
    r = run(nc_f1, ffn1_maps())
    xsh = [np.ascontiguousarray(r.results[c]["xout"]) for c in cores]

    nc_a7 = build_attnb1_program()
    WwT = np.ascontiguousarray(np.asarray(inputs["L1_attn_b_W_write"], np.float32).T)
    WrT = np.ascontiguousarray(np.asarray(inputs["L1_attn_b_W_read"], np.float32).T)
    iotaDI, iotaGW, iota64, iotat, dmat, causalT = host_attnb1_consts()
    decay = np.asarray(inputs["L1_attn_b_decay"], np.float32).reshape(1, 1)
    la = np.asarray(inputs["L1_attn_b_log_alpha"], np.float32).reshape(1, 1)
    maps = []
    for c in cores:
        b, h = c // 2, c % 2
        whalo = np.zeros((1, D), np.float32)
        if h == 1:
            whalo[0] = xsh[2 * b][TOK - 1]
        maps.append({
            "xres": xsh[c], "whalo": whalo,
            "WwT": WwT, "WrT": WrT,
            "WpT": np.zeros((D, D), ml_dtypes.bfloat16),
            "decay": decay, "log_alpha": la,
            "gflag": np.full((1, 1), float(h), np.float32),
            "onesr": onesr, "iotaDI": iotaDI, "iotaGW": iotaGW,
            "iota64": iota64, "iotat": iotat, "dmat": dmat,
            "causalT": causalT, "ident": ident, "identb": identb,
        })
    ra = run(nc_a7, maps)
    for c in cores:
        peer = c - 1 if c % 2 == 1 else c
        maps[c]["WpT"] = np.ascontiguousarray(ra.results[peer]["WT_out"])
    rb = run(nc_a7, maps)
    xsh = [np.ascontiguousarray((rb if c % 2 == 1 else ra).results[c]["xout"])
           for c in cores]

    r = run(nc_f1, ffn1_maps())
    xsh = [np.ascontiguousarray(r.results[c]["xout"]) for c in cores]

    out = np.empty((4, TFULL, D), np.float32)
    for c in cores:
        b, h = c // 2, c % 2
        out[b, h * TOK:(h + 1) * TOK] = xsh[c]
    return out


def kernel(**inputs):
    import ml_dtypes
    x0 = np.asarray(inputs["L0_attn_a_x"], np.float32)
    ident = np.eye(128, dtype=np.float32)
    identb = np.eye(128, dtype=ml_dtypes.bfloat16)
    onesc = np.ones((128, 1), np.float32)
    onesr = np.ones((1, 128), np.float32)
    ones1b = np.ones((1, 128), dtype=ml_dtypes.bfloat16)

    w1T, w2T, b1c, b2r = host_ffn0_inputs(
        np.asarray(inputs["L0_ffn_W1"], np.float32),
        np.asarray(inputs["L0_ffn_b1"], np.float32),
        np.asarray(inputs["L0_ffn_W2"], np.float32),
        np.asarray(inputs["L0_ffn_b2"], np.float32))
    mask = host_dropout_mask()
    maskneg = host_attnb0_masks()
    enc_q = np.asarray(inputs["L1_attn_a_enc_q"], np.float32)
    enc_v = np.asarray(inputs["L1_attn_a_enc_v"], np.float32)
    encq = np.ascontiguousarray(enc_q.transpose(1, 0, 2).reshape(D, 1024))
    encv = np.ascontiguousarray(enc_v.transpose(1, 0, 2).reshape(D, 1024))
    hebbc = np.ascontiguousarray(
        np.asarray(inputs["L1_attn_a_hebbian"], np.float32).reshape(1024, 128))
    mb1c = np.ascontiguousarray(
        np.asarray(inputs["L1_attn_a_m_b1"], np.float32).reshape(4, 128).T)
    mb2c = np.ascontiguousarray(
        np.asarray(inputs["L1_attn_a_m_b2"], np.float32).reshape(8, 128).T)
    basec = np.ascontiguousarray(
        np.asarray(inputs["L1_attn_a_base_metric"], np.float32).T)
    wvbc, bandmask = host_attna1_consts()
    f1w1T, f1w2T, cw, cb = host_ffn1_inputs(
        np.asarray(inputs["L1_ffn_w1"], np.float32),
        np.asarray(inputs["L1_ffn_conv_w"], np.float32),
        np.asarray(inputs["L1_ffn_conv_b"], np.float32),
        np.asarray(inputs["L1_ffn_w2"], np.float32))
    WwT = np.ascontiguousarray(np.asarray(inputs["L1_attn_b_W_write"], np.float32).T)
    WrT = np.ascontiguousarray(np.asarray(inputs["L1_attn_b_W_read"], np.float32).T)
    iotaDI, iotaGW, iota64, iotat, dmat, causalT = host_attnb1_consts()
    decay = np.asarray(inputs["L1_attn_b_decay"], np.float32).reshape(1, 1)
    la = np.asarray(inputs["L1_attn_b_log_alpha"], np.float32).reshape(1, 1)

    nc = build_fused_program()
    maps = []
    for c in range(NCORES):
        b, h = c // 2, c % 2
        sl = np.s_[h * TOK:(h + 1) * TOK]
        hi = host_mamba_inputs(
            np.asarray(inputs["L0_attn_a_in_proj_w"], np.float32),
            np.asarray(inputs["L0_attn_a_conv_w"], np.float32),
            np.asarray(inputs["L0_attn_a_conv_b"], np.float32),
            np.asarray(inputs["L0_attn_a_x_proj_w"], np.float32),
            np.asarray(inputs["L0_attn_a_dt_proj_w"], np.float32),
            np.asarray(inputs["L0_attn_a_dt_proj_b"], np.float32),
            np.asarray(inputs["L0_attn_a_A_log"], np.float32),
            np.asarray(inputs["L0_attn_a_Dp"], np.float32),
            np.asarray(inputs["L0_attn_a_out_proj_w"], np.float32), h)
        hf = np.zeros((128, 3), np.float32)
        hf[:, 0] = h
        hf[0, 1] = h
        hf[1, 1] = 1.0 - h
        hf[:, 2] = 1.0 - h
        hi.update({
            "xres": np.ascontiguousarray(x0[b, sl]),
            "xfull": np.ascontiguousarray(x0[b]),
            "ident": ident, "identb": identb, "onesc": onesc, "onesr": onesr,
            "ones1": ones1b, "hflags": hf,
            "w1T": w1T, "w2T": w2T, "b1c": b1c, "b2r": b2r,
            "maskd": np.ascontiguousarray(mask[b, sl]),
            "wqkv": np.asarray(inputs["L0_attn_b_w_qkv"], np.float32),
            "wproj": np.asarray(inputs["L0_attn_b_w_proj"], np.float32),
            "lnw": np.asarray(inputs["L0_attn_b_ln_w"], np.float32).reshape(1, -1),
            "maskneg": maskneg,
            "encq": encq, "encv": encv,
            "decoder": np.asarray(inputs["L1_attn_a_decoder"], np.float32),
            "mw1": np.asarray(inputs["L1_attn_a_m_w1"], np.float32),
            "mw2": np.asarray(inputs["L1_attn_a_m_w2"], np.float32),
            "hebbc": hebbc, "mb1c": mb1c, "mb2c": mb2c, "basec": basec,
            "wvbc": wvbc, "bandmask": bandmask,
            "f1w1T": f1w1T, "f1w2T": f1w2T, "cw": cw, "cb": cb,
            "WwT": WwT, "WrT": WrT, "decay": decay, "log_alpha": la,
            "gflag": np.full((1, 1), float(h), np.float32),
            "iotaDI": iotaDI, "iotaGW": iotaGW, "iota64": iota64,
            "iotat": iotat, "dmat": dmat, "causalT": causalT,
        })
        maps.append(hi)
    r = run_bass_kernel_spmd(nc, maps, core_ids=list(range(NCORES)))
    out = np.empty((4, TFULL, D), np.float32)
    for c in range(NCORES):
        b, h = c // 2, c % 2
        out[b, h * TOK:(h + 1) * TOK] = r.results[c]["xout"]
    return out
